# version 07
# baseline (speedup 1.0000x reference)
import contextlib
import numpy as np
import ml_dtypes
import concourse.bass as bass
import concourse.mybir as mybir
from concourse.bass_utils import run_bass_kernel_spmd

F32 = mybir.dt.float32
BF16 = mybir.dt.bfloat16
I32 = mybir.dt.int32
AF = mybir.ActivationFunctionType
ALU = mybir.AluOpType
AX = mybir.AxisListType
NP_BF16 = ml_dtypes.bfloat16

N_CORES = 8
RG = [list(range(N_CORES))]
D = 4096
T = 8192
NST = 16
DTR = 128
DSL = D // N_CORES
HEADS = 8
HDIM = 256
PART = 2048
HID = 11008
OWN_T = T // N_CORES
LN_EPS = 1e-5

NF4 = [-1.0, -0.6961928009986877, -0.5250730514526367, -0.39491748809814453,
       -0.28444138169288635, -0.18477343022823334, -0.09105003625154495, 0.0,
       0.07958029955625534, 0.16093020141124725, 0.24611230194568634,
       0.33791524171829224, 0.44070982933044434, 0.5626170039176941,
       0.6989939212799072, 1.0]


class Sched:
    ENGS = ("sync", "gpsimd", "vector", "scalar", "tensor")

    def __init__(self, nc, stack):
        self.nc = nc
        self.stack = stack
        self.ops = {e: [] for e in self.ENGS}
        self.sem = {e: stack.enter_context(nc.semaphore(f"sem_{e}"))
                    for e in self.ENGS}
        self.cnt = {e: 0 for e in self.ENGS}
        self.dsem = {}
        self.dcnt = {}
        self.ccsem = stack.enter_context(nc.semaphore("sem_cc"))
        self.cccnt = 0

    extra = ()

    def op(self, eng, fn, deps=(), inc=None, tag=""):
        if inc is None:
            inc = 16 if eng in ("sync", "gpsimd") else 1
        self.cnt[eng] += inc
        tok = (self.sem[eng], self.cnt[eng])
        self.ops[eng].append((fn, tuple(deps) + tuple(self.extra), tok, inc, tag))
        return tok

    def dma(self, eng, out, in_, deps=(), tag="", ch=0):
        key = (eng, ch)
        if key not in self.dsem:
            self.dsem[key] = self.stack.enter_context(
                self.nc.semaphore(f"dma_{eng}{ch}"))
            self.dcnt[key] = 0
        self.dcnt[key] += 16
        tok = (self.dsem[key], self.dcnt[key])
        self.ops[eng].append(
            (lambda e: e.dma_start(out=out, in_=in_),
             tuple(deps) + tuple(self.extra), tok, 16, tag))
        return tok

    def idma(self, out, in_view, idx_ap, deps=(), tag="", ch=0, element_offset=0):
        import bass_rust as _br
        key = ("gpsimd", ch)
        if key not in self.dsem:
            self.dsem[key] = self.stack.enter_context(
                self.nc.semaphore(f"dma_gpsimd{ch}"))
            self.dcnt[key] = 0
        self.dcnt[key] += 16
        tok = (self.dsem[key], self.dcnt[key])

        def fn(e, out=out, in_view=in_view, idx_ap=idx_ap, eo=element_offset):
            return e.indirect_dma_start(
                out=out, out_offset=None, in_=in_view,
                in_offset=_br.IndirectOffsetOnAxis(idx_ap, 0),
                element_offset=eo)
        self.ops["gpsimd"].append((fn, tuple(deps) + tuple(self.extra), tok,
                                   16, tag))
        return tok

    def barrier(self):
        toks = []
        for e in self.ENGS:
            if self.cnt[e]:
                toks.append((self.sem[e], self.cnt[e]))
        for k, sem in self.dsem.items():
            toks.append((sem, self.dcnt[k]))
        if self.cccnt:
            toks.append((self.ccsem, self.cccnt))
        nops = []
        for e in ("gpsimd", "vector", "scalar", "tensor"):
            nops.append(self.op(e, lambda eng: eng.nop(), deps=toks, inc=1))
        self.extra = tuple(nops)
        return nops

    def cc(self, kind, alu, ins, outs, deps=(), tag=""):
        self.cccnt += 1
        tok = (self.ccsem, self.cccnt)

        def fn(e):
            return e.collective_compute(kind, alu, replica_groups=RG,
                                        ins=ins, outs=outs)
        self.ops["gpsimd"].append((fn, tuple(deps) + tuple(self.extra), tok,
                                   None, tag))
        return tok

    def emit(self, block):
        ccsem = self.ccsem
        for eng_name in self.ENGS:
            ops = self.ops[eng_name]
            if not ops:
                continue

            self_serial = eng_name in ("vector", "scalar", "gpsimd")
            esem = self.sem[eng_name]

            def body(eng, ops=ops, self_serial=self_serial, esem=esem):
                waited = {}
                last_compute = None
                for fn, deps, tok, inc, tag in ops:
                    is_dma = tok[0] in self.dsem.values()
                    deps = list(deps)
                    if self_serial and not is_dma and last_compute is not None:
                        deps.append(last_compute)
                    best = {}
                    for (sem, val) in deps:
                        k = id(sem)
                        if best.get(k, (None, -1))[1] < val:
                            best[k] = (sem, val)
                    for (sem, val) in best.values():
                        key = id(sem)
                        if waited.get(key, -1) < val:
                            eng.wait_ge(sem, val)
                            waited[key] = val
                    inst = fn(eng)
                    if isinstance(inst, (list, tuple)):
                        inst = inst[-1]
                    if tok[0] is ccsem:
                        inst.then_inc(tok[0])
                    else:
                        inst.then_inc(tok[0], inc)
                    if not is_dma and tok[0] is esem:
                        last_compute = tok
            getattr(block, eng_name)(body)


def inner_bcast(ap, width):
    a = ap.ap
    assert a[-1][1] == 1, a
    return bass.AP(tensor=ap.tensor, offset=ap.offset,
                   ap=list(a[:-1]) + [[0, width]])


def mid_bcast(ap2d, n_mid):
    return bass.AP(tensor=ap2d.tensor, offset=ap2d.offset,
                   ap=[ap2d.ap[0], [0, n_mid], ap2d.ap[1]])


def emit_identity(S, idn):
    t0 = S.op("gpsimd", lambda e: e.memset(idn[:], 0.0), inc=1)
    return S.op("gpsimd", lambda e: e.affine_select(
        out=idn[:], in_=idn[:], compare_op=ALU.not_equal, fill=1.0,
        base=0, pattern=[[-1, idn.shape[1]]], channel_multiplier=1),
        deps=[t0], inc=1)


TS_M = 1024
NSUB_M = T // TS_M
NDT = DSL // 128


def phase_mamba(nc, S, stk, IO):
    TS = TS_M
    TB = T // 128
    NSUB = T // TS
    x0d = IO['x0d']
    x0dT = IO['x0dT']
    alog = IO['alog']
    dvec = IO['dvec']
    xpwS = IO['xpwS']
    dtwT = IO['dtwT']
    dtb = IO['dtb']
    x1d = IO['x1d']
    st_in = IO['st_in']
    st_out = IO['st_out']
    st_scr = IO['st_scr']
    xp_ins = IO['xp_ins']
    xp_outs = IO['xp_outs']
    _pfx = IO.get("pfx", "")
    sb = lambda name, shape, dt: stk.enter_context(
        nc.sbuf_tensor(_pfx + name, shape, dt))
    psu = lambda name, shape: stk.enter_context(
        nc.psum_tensor(_pfx + name, shape, F32))

    xps_f = sb("xps_f", [128, 4 * 160], F32)
    xps_b = sb("xps_b", [128, 4 * 160], BF16)
    dtw_b = sb("dtw_b", [128, NDT * 128], BF16)
    a4 = sb("a4", [128, NDT * NST], F32)
    d4 = sb("d4", [128, NDT], F32)
    dtb4 = sb("dtb4", [128, NDT], F32)
    onesc = sb("onesc", [128, 1], F32)
    ones1r = sb("ones1r", [1, 128], F32)
    state = sb("state", [128, NDT * NST], F32)
    idn = sb("idn", [128, 128], F32)
    stat_a = sb("stat_a", [128, TB], F32)
    stat_b = sb("stat_b", [128, TB], F32)
    stat_t = sb("stat_t", [TB, 128], F32)
    sa_t = sb("sa_t", [TB, 128], F32)
    sb_t = sb("sb_t", [TB, 128], F32)
    m64 = sb("m64", [TB, 128], F32)
    r64 = sb("r64", [TB, 128], F32)
    t64 = sb("t64", [TB, 128], F32)
    wsrow = sb("wsrow", [1, 160], F32)
    wcolA = sb("wcolA", [128, 1], F32)
    wcolB = sb("wcolB", [16, 1], F32)
    wcolC = sb("wcolC", [16, 1], F32)
    ceps = sb("ceps", [128, 1], F32)
    cone = sb("cone", [128, 1], F32)
    cneg1 = sb("cneg1", [128, 1], F32)
    mrow_s = sb("mrow_s", [1, TS], F32)
    rrow_s = sb("rrow_s", [1, TS], F32)
    bcrow = sb("bcrow", [1, TS], F32)

    x0t = sb("x0t", [128, NDT * TS], F32)
    x0b = sb("x0b", [128, NDT * 512], BF16)
    xln = sb("xln", [128, NDT * TS], F32)
    dt_ = sb("dt_", [128, NDT * TS], F32)
    u_ = sb("u_", [128, NDT * TS], F32)
    da = sb("da", [128, NDT * (TS + 1)], F32)
    dbx = sb("dbx", [128, NDT * (TS + 1)], F32)
    h_ = sb("h_", [128, NDT * (TS + 1)], F32)
    ch = sb("ch", [128, NDT * TS], F32)
    y_ = sb("y_", [128, NDT * TS], F32)
    brep = sb("brep", [128, TS], F32)
    crep = sb("crep", [128, TS], F32)
    mrep = sb("mrep", [128, TS], F32)
    rrep = sb("rrep", [128, TS], F32)
    xpr = sb("xpr", [128, TS], F32)
    bcr_b = sb("bcr_b", [16, TS], F32)
    bcr_c = sb("bcr_c", [16, TS], F32)
    dtin_b = sb("dtin_b", [128, TS], BF16)
    se1 = sb("se1", [128, 512], F32)
    se2 = sb("se2", [128, 512], F32)

    ps_m = psu("ps_m", [128, 512])
    ps_b = psu("ps_b", [16, 512])
    ps_c = psu("ps_c", [16, 512])
    ps_r = psu("ps_r", [1, 160])
    ps_t = psu("ps_t", [TB, 128])
    ps_e = psu("ps_e", [128, 512])


    def r3(buf, inner):
        return buf[:].rearrange("p (a t) -> p a t", a=NDT)

    x0t3 = lambda: r3(x0t, TS)
    xln3 = lambda: r3(xln, TS)
    dt3 = lambda: r3(dt_, TS)
    u3 = lambda: r3(u_, TS)
    ch3 = lambda: r3(ch, TS)
    da3 = lambda: da[:].rearrange("p (a t) -> p a t", a=NDT)
    dbx3 = lambda: dbx[:].rearrange("p (a t) -> p a t", a=NDT)
    h3 = lambda: h_[:].rearrange("p (a t) -> p a t", a=NDT)
    st3 = lambda: state[:].rearrange("p (a n) -> p a n", a=NDT)
    a43 = lambda: a4[:].rearrange("p (a n) -> p a n", a=NDT)
    d43 = lambda: d4[:].rearrange("p (a o) -> p a o", a=NDT)

    t_alog = S.dma("sync", a4[:], alog, ch=0)
    t_dvec = S.dma("sync", d4[:], dvec, ch=0)
    t_dtb = S.dma("sync", dtb4[:], dtb, ch=0)
    t_dtw = S.dma("sync", x0t[:, 0:NDT * 128], dtwT, ch=0)
    t_xps = S.dma("sync", xps_f[:], xpwS, ch=0)
    setup_g = [t_alog, t_dvec, t_dtb, t_dtw, t_xps]
    t_alog = t_dvec = t_dtb = t_dtw = t_xps = tuple(setup_g[-1])
    setup_g = [tuple(setup_g[-1])]
    t_alog, = setup_g; t_dvec, = setup_g; t_dtb, = setup_g
    t_dtw, = setup_g; t_xps, = setup_g

    t_a1 = S.op("scalar", lambda e: e.activation(out=a4[:], in_=a4[:], func=AF.Exp),
                deps=[t_alog])
    t_a2 = S.op("vector", lambda e: e.tensor_scalar(
        out=a4[:], in0=a4[:], scalar1=-1.0, scalar2=None, op0=ALU.mult),
        deps=[t_a1])
    t_dtwb = S.op("vector", lambda e: e.tensor_copy(
        out=dtw_b[:], in_=x0t[:, 0:NDT * 128]), deps=[t_dtw])
    t_xpsb = S.op("vector", lambda e: e.tensor_copy(out=xps_b[:], in_=xps_f[:]),
                  deps=[t_xps])
    t_ones = S.op("vector", lambda e: e.memset(onesc[:], 1.0))
    t_ceps = S.op("vector", lambda e: e.memset(ceps[:], LN_EPS))
    t_cone = S.op("vector", lambda e: e.memset(cone[:], 1.0))
    t_cneg1 = S.op("vector", lambda e: e.memset(cneg1[:], -1.0))
    t_ones1r = S.op("vector", lambda e: e.memset(ones1r[:], 1.0))
    t_idn = emit_identity(S, idn)
    t_st0 = S.op("vector", lambda e: e.memset(state[:], 0.0))
    t_da0 = S.op("vector", lambda e: e.memset(da[:], 0.0))
    t_dbx0 = S.op("vector", lambda e: e.memset(dbx[:], 0.0))
    t_y0 = S.op("vector", lambda e: e.memset(y_[:], 0.0))

    last_use = {0: None, 1: None}
    all_stats = []
    x0sA = brep[:, 0:512]
    x0sB = crep[:, 0:512]
    for j in range(TB):
        buf = x0sA if j % 2 == 0 else x0sB
        sebuf = se1 if j % 2 == 0 else se2
        pu = last_use[j % 2]
        t_l = S.dma("sync", buf, x0d[j * 128:(j + 1) * 128, :],
                    deps=[pu] if pu else [], ch=1 + (j % 2))
        t_r = S.op("vector", lambda e, b=buf, j=j: e.reduce_sum(
            out=stat_a[:, j:j + 1], in_=b, axis=AX.X), deps=[t_l])
        t_s = S.op("scalar", lambda e, b=buf, o=sebuf: e.activation(
            out=o[:], in_=b, func=AF.Square), deps=[t_l])
        t_q = S.op("vector", lambda e, o=sebuf, j=j: e.reduce_sum(
            out=stat_b[:, j:j + 1], in_=o[:], axis=AX.X), deps=[t_s])
        last_use[j % 2] = t_q
        all_stats += [t_r, t_q]
    TBP = TB

    t_z2 = S.op("vector", lambda e: e.memset(stat_t[:], 0.0))
    t_d2z = S.dma("gpsimd", st_in.rearrange("r (p f) -> (r p) f", p=TB)[2 * TB:3 * TB, :],
                  stat_t[:], deps=[t_z2])
    t_ws = None
    for a in range(4):
        t_ws = S.op("tensor", lambda e, a=a: e.matmul(
            ps_r[:], onesc[:], xps_f[:, a * 160:(a + 1) * 160],
            start=(a == 0), stop=(a == 3)), deps=[t_ones, t_xps])
    t_wse = S.op("vector", lambda e: e.tensor_copy(out=wsrow[:], in_=ps_r[:]),
                 deps=[t_ws])
    t_dws = S.dma("gpsimd", st_in[2:3, 0:160], wsrow[:], deps=[t_wse, t_d2z])

    t_tp1 = S.op("tensor", lambda e: e.transpose(ps_t[:], stat_a[:], idn[:]),
                 deps=all_stats + [t_idn, t_ws])
    t_ev1 = S.op("vector", lambda e: e.tensor_copy(out=stat_t[:], in_=ps_t[:]),
                 deps=[t_tp1, t_dws])
    t_d1 = S.dma("gpsimd", st_in.rearrange("r (p f) -> (r p) f", p=TB)[0:TB, :],
                 stat_t[:], deps=[t_ev1])
    t_tp2 = S.op("tensor", lambda e: e.transpose(ps_t[:], stat_b[:], idn[:]),
                 deps=[t_ev1])
    t_ev2 = S.op("vector", lambda e: e.tensor_copy(out=sa_t[:], in_=ps_t[:]),
                 deps=[t_tp2])
    t_d2 = S.dma("gpsimd", st_in.rearrange("r (p f) -> (r p) f", p=TB)[TB:2 * TB, :],
                 sa_t[:], deps=[t_ev2])

    t_ccst = S.cc("AllReduce", ALU.add, [st_in.opt()], [st_out.opt()],
                  deps=[t_d1, t_d2, t_dws])

    t_la = S.dma("sync", sa_t[:], st_out.rearrange("r (p f) -> (r p) f", p=TB)[0:TB, :],
                 deps=[t_ccst, t_d2])
    t_lb = S.dma("sync", sb_t[:], st_out.rearrange("r (p f) -> (r p) f", p=TB)[TB:2 * TB, :],
                 deps=[t_ccst])
    t_wa = S.dma("sync", wcolA[:], st_out[2:3, 0:128].rearrange("o r -> r o"),
                 deps=[t_ccst])
    t_wb = S.dma("sync", wcolB[:], st_out[2:3, 128:144].rearrange("o r -> r o"),
                 deps=[t_ccst])
    t_wc = S.dma("sync", wcolC[:], st_out[2:3, 144:160].rearrange("o r -> r o"),
                 deps=[t_ccst])
    t_la = t_lb = t_wa = t_wb = t_wc = t_wc
    t_m1 = S.op("vector", lambda e: e.tensor_scalar(
        out=m64[:], in0=sa_t[:], scalar1=1.0 / D, scalar2=None, op0=ALU.mult),
        deps=[t_la])
    t_m2 = S.op("vector", lambda e: e.tensor_scalar(
        out=r64[:], in0=sb_t[:], scalar1=1.0 / D, scalar2=None, op0=ALU.mult),
        deps=[t_lb])
    t_m3 = S.op("vector", lambda e: e.tensor_tensor(
        out=t64[:], in0=m64[:], in1=m64[:], op=ALU.mult), deps=[t_m1])
    t_m4 = S.op("vector", lambda e: e.tensor_tensor(
        out=r64[:], in0=r64[:], in1=t64[:], op=ALU.subtract), deps=[t_m2, t_m3])
    t_m5 = S.op("scalar", lambda e: e.activation(
        out=t64[:], in_=r64[:], func=AF.Sqrt, bias=ceps[0:TB, 0:1]),
        deps=[t_m4, t_ceps])
    t_m6 = S.op("vector", lambda e: e.reciprocal(out=r64[:], in_=t64[:]),
                deps=[t_m5])
    t_dm = S.dma("gpsimd", st_scr.rearrange("r (p f) -> (r p) f", p=TB)[0:TB, :],
                 m64[:], deps=[t_m3, t_m6])
    t_dr = S.dma("gpsimd", st_scr.rearrange("r (p f) -> (r p) f", p=TB)[TB:2 * TB, :],
                 r64[:], deps=[t_m6])

    prev = {}
    tok_scan = None
    tok_se_prev = None

    for s in range(NSUB):
        t0c = s * TS
        t_x0 = S.dma("sync", x0t3(),
                     x0dT.rearrange("(a p) t -> p a t", p=128)[:, :, t0c:t0c + TS],
                     deps=[d for d in [prev.get("x0t")] if d], ch=3)
        t_ms = S.dma("sync", mrow_s[:], st_scr[0:1, t0c:t0c + TS],
                     deps=[t_dr] + ([prev["mrow"]] if "mrow" in prev else []),
                     ch=4)
        t_rs = S.dma("sync", rrow_s[:], st_scr[1:2, t0c:t0c + TS],
                     deps=[t_dr] + ([prev["rrow"]] if "rrow" in prev else []),
                     ch=4)
        t_ms = t_rs

        ev_mr = []
        ev_prev0 = prev.get("ps_e")
        for half in range(2):
            sl = slice(half * 512, (half + 1) * 512)
            t_bm = S.op("tensor", lambda e, sl=sl: e.matmul(
                ps_e[:], ones1r[:], mrow_s[0:1, sl], start=True, stop=True),
                deps=[t_ms, t_ones1r]
                + ([prev["mrep"]] if "mrep" in prev else [])
                + ([ev_prev0] if ev_prev0 else []))
            t_em = S.op("vector", lambda e, sl=sl: e.tensor_copy(
                out=mrep[:, sl], in_=ps_e[:]), deps=[t_bm])
            t_br = S.op("tensor", lambda e, sl=sl: e.matmul(
                ps_e[:], ones1r[:], rrow_s[0:1, sl], start=True, stop=True),
                deps=[t_rs, t_em])
            t_er = S.op("vector", lambda e, sl=sl: e.tensor_copy(
                out=rrep[:, sl], in_=ps_e[:]), deps=[t_br])
            ev_mr += [t_em, t_er]
            ev_prev0 = t_er
            prev["mrep"] = t_er
        prev["ps_e"] = t_er

        dma_xp = []
        for half in range(2):
            sl = slice(half * 512, (half + 1) * 512)
            t_cb = S.op("vector", lambda e, sl=sl: e.tensor_copy(
                out=x0b[:], in_=r3(x0t, TS)[:, :, sl]),
                deps=[t_x0] + ([prev["x0b"]] if "x0b" in prev else []))
            tm = td = tb2 = tc2 = None
            for a in range(4):
                tm = S.op("tensor", lambda e, a=a: e.matmul(
                    ps_m[:], xps_b[:, a * 160:a * 160 + 128],
                    x0b[:, a * 512:(a + 1) * 512],
                    start=(a == 0), stop=(a == 3)),
                    deps=[t_cb, t_xpsb] + ([prev["ps_m"]] if "ps_m" in prev else []))
            for a in range(4):
                tb2 = S.op("tensor", lambda e, a=a: e.matmul(
                    ps_b[:], xps_b[:, a * 160 + 128:a * 160 + 144],
                    x0b[:, a * 512:(a + 1) * 512],
                    start=(a == 0), stop=(a == 3)),
                    deps=[prev["ps_b"]] if "ps_b" in prev else [])
            for a in range(4):
                tc2 = S.op("tensor", lambda e, a=a: e.matmul(
                    ps_c[:], xps_b[:, a * 160 + 144:a * 160 + 160],
                    x0b[:, a * 512:(a + 1) * 512],
                    start=(a == 0), stop=(a == 3)),
                    deps=[prev["ps_c"]] if "ps_c" in prev else [])
            t_e1 = S.op("vector", lambda e, sl=sl: e.tensor_copy(
                out=xpr[:, sl], in_=ps_m[:]), deps=[tm])
            bcr_war = [prev["bcr"]] if "bcr" in prev else []
            t_e2 = S.op("vector", lambda e, sl=sl: e.tensor_copy(
                out=bcr_b[:, sl], in_=ps_b[:]), deps=[tb2] + bcr_war)
            t_e3 = S.op("vector", lambda e, sl=sl: e.tensor_copy(
                out=bcr_c[:, sl], in_=ps_c[:]), deps=[tc2] + bcr_war)
            prev["ps_m"], prev["ps_b"], prev["ps_c"] = t_e1, t_e2, t_e3
            prev["x0b"] = tm if tc2 is None else tc2
            dma_xp += [t_e1, t_e2, t_e3]
        t_dx1 = S.dma("gpsimd", xp_ins[s][0:128, :], xpr[:], deps=dma_xp, ch=3)
        t_dx2 = S.dma("gpsimd", xp_ins[s][128:144, :], bcr_b[:], deps=dma_xp, ch=3)
        t_dx3 = S.dma("gpsimd", xp_ins[s][144:160, :], bcr_c[:], deps=dma_xp, ch=3)
        t_ccxp = S.cc("AllReduce", ALU.add, [xp_ins[s].opt()], [xp_outs[s].opt()],
                      deps=[t_dx1, t_dx2, t_dx3])
        t_lx1 = S.dma("sync", xpr[:], xp_outs[s][0:128, :], deps=[t_ccxp], ch=5)
        bw = [prev["bcr"]] if "bcr" in prev else []
        t_lx2 = S.dma("sync", bcr_b[:], xp_outs[s][128:144, :],
                      deps=[t_ccxp] + bw, ch=5)
        t_lx3 = S.dma("sync", bcr_c[:], xp_outs[s][144:160, :],
                      deps=[t_ccxp] + bw, ch=5)
        t_lx1 = t_lx2 = t_lx3

        t_c1 = S.op("vector", lambda e: e.tensor_scalar(
            out=se1[:], in0=mrep[:, 0:512], scalar1=wcolA[:], scalar2=None,
            op0=ALU.mult), deps=[t_lx1, t_wa, prev["mrep"]])
        t_c1b = S.op("vector", lambda e: e.tensor_scalar(
            out=se2[:], in0=mrep[:, 512:1024], scalar1=wcolA[:], scalar2=None,
            op0=ALU.mult), deps=[t_c1])
        t_c2 = S.op("vector", lambda e: e.tensor_tensor(
            out=xpr[:, 0:512], in0=xpr[:, 0:512], in1=se1[:], op=ALU.subtract))
        t_c2b = S.op("vector", lambda e: e.tensor_tensor(
            out=xpr[:, 512:1024], in0=xpr[:, 512:1024], in1=se2[:], op=ALU.subtract))
        t_c3 = S.op("vector", lambda e: e.tensor_tensor(
            out=xpr[:], in0=xpr[:], in1=rrep[:], op=ALU.mult))
        t_c4 = S.op("vector", lambda e: e.tensor_copy(out=dtin_b[:], in_=xpr[:]),
                    deps=[t_c3])
        t_c5 = S.op("vector", lambda e: e.tensor_scalar(
            out=se1[0:16, :], in0=mrep[0:16, 0:512], scalar1=wcolB[:],
            scalar2=None, op0=ALU.mult), deps=[t_lx2, t_wb])
        t_c5b = S.op("vector", lambda e: e.tensor_scalar(
            out=se2[0:16, :], in0=mrep[0:16, 512:1024], scalar1=wcolB[:],
            scalar2=None, op0=ALU.mult))
        t_c6 = S.op("vector", lambda e: e.tensor_tensor(
            out=bcr_b[0:16, 0:512], in0=bcr_b[0:16, 0:512], in1=se1[0:16, :],
            op=ALU.subtract))
        t_c6b = S.op("vector", lambda e: e.tensor_tensor(
            out=bcr_b[0:16, 512:1024], in0=bcr_b[0:16, 512:1024], in1=se2[0:16, :],
            op=ALU.subtract))
        t_c7 = S.op("vector", lambda e: e.tensor_tensor(
            out=bcr_b[:], in0=bcr_b[:], in1=rrep[0:16, :], op=ALU.mult))
        t_c8 = S.op("vector", lambda e: e.tensor_scalar(
            out=se1[0:16, :], in0=mrep[0:16, 0:512], scalar1=wcolC[:],
            scalar2=None, op0=ALU.mult), deps=[t_lx3, t_wc])
        t_c8b = S.op("vector", lambda e: e.tensor_scalar(
            out=se2[0:16, :], in0=mrep[0:16, 512:1024], scalar1=wcolC[:],
            scalar2=None, op0=ALU.mult))
        t_c9 = S.op("vector", lambda e: e.tensor_tensor(
            out=bcr_c[0:16, 0:512], in0=bcr_c[0:16, 0:512], in1=se1[0:16, :],
            op=ALU.subtract))
        t_c9b = S.op("vector", lambda e: e.tensor_tensor(
            out=bcr_c[0:16, 512:1024], in0=bcr_c[0:16, 512:1024], in1=se2[0:16, :],
            op=ALU.subtract))
        t_c10 = S.op("vector", lambda e: e.tensor_tensor(
            out=bcr_c[:], in0=bcr_c[:], in1=rrep[0:16, :], op=ALU.mult))
        tok_bc = t_c10

        t_sp = []
        last_ps_reader = None
        prev_dt = tok_bc
        for a in range(4):
            for half in range(2):
                sl = slice(half * 512, (half + 1) * 512)
                deps_mm = [t_c4, t_dtwb]
                if last_ps_reader:
                    deps_mm.append(last_ps_reader)
                if "dtin" in prev:
                    deps_mm.append(prev.pop("dtin"))
                t_mm = S.op("tensor", lambda e, a=a, sl=sl: e.matmul(
                    ps_m[:], dtw_b[:, a * 128:(a + 1) * 128], dtin_b[:, sl],
                    start=True, stop=True), deps=deps_mm)
                t_s1 = S.op("scalar", lambda e, a=a: e.activation(
                    out=se1[:], in_=ps_m[:], func=AF.Abs, bias=dtb4[:, a:a + 1]),
                    deps=[t_mm, t_dtb, prev_dt])
                t_s2 = S.op("scalar", lambda e: e.activation(
                    out=se2[:], in_=se1[:], func=AF.Exp,
                    scale=cneg1[:]), deps=[t_cneg1])
                t_s3 = S.op("scalar", lambda e: e.activation(
                    out=se1[:], in_=se2[:], func=AF.Ln,
                    bias=cone[:]), deps=[t_cone])
                t_s4 = S.op("scalar", lambda e, a=a: e.activation(
                    out=se2[:], in_=ps_m[:], func=AF.Relu, bias=dtb4[:, a:a + 1]))
                last_ps_reader = t_s4
                t_dt = S.op("vector", lambda e, a=a, sl=sl: e.tensor_tensor(
                    out=dt_[:, a * TS + sl.start: a * TS + sl.stop],
                    in0=se1[:], in1=se2[:], op=ALU.add), deps=[t_s3, t_s4])
                t_sp.append(t_dt)
                prev_dt = t_dt
        last_dtmm = t_mm
        prev["ps_m"] = t_s4
        tok_dt = t_sp[-1]

        t_xl1 = S.op("vector", lambda e: e.tensor_tensor(
            out=ch3(), in0=x0t3(), in1=mid_bcast(mrep[:], NDT), op=ALU.subtract),
            deps=[t_x0, prev["mrep"]] + t_sp[-1:]
            + ([prev["chout"]] if "chout" in prev else []))
        t_xl2 = S.op("vector", lambda e: e.tensor_tensor(
            out=xln3(), in0=ch3(), in1=mid_bcast(rrep[:], NDT), op=ALU.mult))
        t_u = S.op("vector", lambda e: e.tensor_tensor(
            out=u3(), in0=dt3(), in1=xln3(), op=ALU.mult), deps=[tok_dt])
        tok_u = t_u

        for n in range(NST):
            dep_b = [tok_bc]
            if tok_scan:
                dep_b.append(tok_scan)
            if "bcrow" in prev:
                dep_b.append(prev["bcrow"])
            t_rb = S.dma("gpsimd", bcrow[:], bcr_b[n:n + 1, :], deps=dep_b,
                         ch=1)
            tb = tc = None
            ev_prev = prev.get("ps_e")
            for half in range(2):
                sl = slice(half * 512, (half + 1) * 512)
                tb_m = S.op("tensor", lambda e, sl=sl: e.matmul(
                    ps_e[:], ones1r[:], bcrow[0:1, sl], start=True,
                    stop=True), deps=[t_rb]
                    + ([tok_scan] if tok_scan else [])
                    + ([ev_prev] if ev_prev else []))
                tb = S.op("vector", lambda e, sl=sl: e.tensor_copy(
                    out=brep[:, sl], in_=ps_e[:]), deps=[tb_m])
                ev_prev = tb
            t_rc = S.dma("gpsimd", bcrow[:], bcr_c[n:n + 1, :],
                         deps=[tb_m, tb], ch=1)
            for half in range(2):
                sl = slice(half * 512, (half + 1) * 512)
                tc_m = S.op("tensor", lambda e, sl=sl: e.matmul(
                    ps_e[:], ones1r[:], bcrow[0:1, sl], start=True,
                    stop=True), deps=[t_rc, ev_prev])
                tc_m_last = tc_m
                prev["bcrow"] = tc_m
                tc = S.op("vector", lambda e, sl=sl: e.tensor_copy(
                    out=crep[:, sl], in_=ps_e[:]), deps=[tc_m])
                ev_prev = tc
            prev["ps_e"] = tc
            a_col = inner_bcast(a43()[:, :, n:n + 1], TS)
            t_dta = S.op("vector", lambda e, ac=a_col: e.tensor_tensor(
                out=da3()[:, :, 1:TS + 1], in0=dt3(), in1=ac, op=ALU.mult),
                deps=[tok_u, t_a2] + ([tok_se_prev] if tok_se_prev else [t_da0]))
            t_exp = S.op("scalar", lambda e: e.activation(
                out=da3()[:, :, 1:TS + 1], in_=da3()[:, :, 1:TS + 1],
                func=AF.Exp), deps=[t_dta] + ([tok_scan] if tok_scan else []))
            tok_se_prev = t_exp
            t_dbx = S.op("vector", lambda e, n=n: e.tensor_tensor(
                out=dbx3()[:, :, 1:TS + 1], in0=u3(), in1=mid_bcast(brep[:], NDT),
                op=ALU.mult), deps=[tc, t_dbx0])
            t_inj = S.op("vector", lambda e, n=n: e.tensor_copy(
                out=dbx3()[:, :, 0:1], in_=st3()[:, :, n:n + 1]),
                deps=[t_st0])
            t_scan = S.op("vector", lambda e: e.tensor_tensor_scan(
                out=h_[:], data0=da[:], data1=dbx[:], initial=0.0,
                op0=ALU.mult, op1=ALU.add), deps=[t_exp, t_dbx, t_inj])
            t_sv = S.op("vector", lambda e, n=n: e.tensor_copy(
                out=st3()[:, :, n:n + 1], in_=h3()[:, :, TS:TS + 1]),
                deps=[t_scan])
            t_ch = S.op("vector", lambda e: e.tensor_tensor(
                out=ch3(), in0=h3()[:, :, 1:TS + 1], in1=mid_bcast(crep[:], NDT),
                op=ALU.mult), deps=[t_scan, t_xl2])
            t_y = S.op("vector", lambda e: e.tensor_tensor(
                out=y_[:], in0=y_[:], in1=ch[:], op=ALU.add), deps=[t_y0])
            tok_scan = t_y

        t_f1 = S.op("vector", lambda e: e.tensor_tensor(
            out=ch3(), in0=xln3(), in1=inner_bcast(d43()[:, :, 0:1], TS),
            op=ALU.mult), deps=[tok_scan, t_dvec])
        t_f2 = S.op("vector", lambda e: e.tensor_tensor(
            out=y_[:], in0=y_[:], in1=ch[:], op=ALU.add))
        t_f3 = S.op("vector", lambda e: e.tensor_tensor(
            out=ch[:], in0=y_[:], in1=x0t[:], op=ALU.add))
        t_out = S.dma("gpsimd",
                      x1d.rearrange("(a p) t -> p a t", p=128)[:, :, t0c:t0c + TS],
                      ch3(), deps=[t_f3], ch=2)
        t_y0 = S.op("vector", lambda e: e.memset(y_[:], 0.0))
        prev["x0t"] = t_out
        prev["chout"] = t_out
        prev["mrow"] = t_br
        prev["rrow"] = t_br
        prev["bcr"] = tc_m_last
        prev["dtin"] = last_dtmm
        prev["mrep"] = t_f1

    return {'x1d_stores': [prev['chout']]}


def build_mamba():
    nc = bass.Bass("TRN2", target_bir_lowering=False)
    TS = TS_M
    TB = T // 128
    NSUB = T // TS

    x0d = nc.dram_tensor("x0d", [T, DSL], F32, kind="ExternalInput").ap()
    x0dT = nc.dram_tensor("x0dT", [DSL, T], F32, kind="ExternalInput").ap()
    alog = nc.dram_tensor("alog", [128, NDT * NST], F32, kind="ExternalInput").ap()
    dvec = nc.dram_tensor("dvec", [128, NDT], F32, kind="ExternalInput").ap()
    xpwS = nc.dram_tensor("xpwS", [128, NDT * 160], F32, kind="ExternalInput").ap()
    dtwT = nc.dram_tensor("dtwT", [DTR, DSL], F32, kind="ExternalInput").ap()
    dtb = nc.dram_tensor("dtb", [128, NDT], F32, kind="ExternalInput").ap()

    x1d = nc.dram_tensor("x1d", [DSL, T], F32, kind="ExternalOutput").ap()

    st_in = nc.dram_tensor("st_in", [3, T], F32, kind="Internal").ap()
    st_out = nc.dram_tensor("st_out", [3, T], F32, kind="Internal").ap()
    st_scr = nc.dram_tensor("st_scr", [2, T], F32, kind="Internal").ap()
    xp_ins = [nc.dram_tensor(f"xp_in{s}", [160, TS], F32, kind="Internal").ap()
              for s in range(T // TS_M)]
    xp_outs = [nc.dram_tensor(f"xp_out{s}", [160, TS], F32, kind="Internal").ap()
               for s in range(T // TS_M)]

    with contextlib.ExitStack() as stk:
        S = Sched(nc, stk)
        IO = dict(x0d=x0d, x0dT=x0dT, alog=alog, dvec=dvec, xpwS=xpwS,
                  dtwT=dtwT, dtb=dtb, x1d=x1d, st_in=st_in, st_out=st_out,
                  st_scr=st_scr, xp_ins=xp_ins, xp_outs=xp_outs)
        with contextlib.ExitStack() as pstk:
            phase_mamba(nc, S, pstk, IO)
        S.emit(block=stk.enter_context(nc.Block()))
    return nc


def phase_ffn0(nc, S, stk, IO):
    To = OWN_T
    NK = D // 128
    NH = To // 512
    x1T = IO.get('x1T')
    wqT = IO.get('wqT')
    scaleT = IO.get('scaleT')
    zeroT = IO.get('zeroT')
    bias0 = IO.get('bias0')
    wkvT = IO.get('wkvT')
    bkv_c = IO.get('bkv_c')
    x2T = IO.get('x2T')
    qd = IO.get('qd')
    kd = IO.get('kd')
    vt = IO.get('vt')
    wdeq = IO.get('wdeq')
    wkvb = IO.get('wkvb')
    b0scr = IO.get('b0scr')
    x1idx = IO.get("x1idx")
    x1v = IO.get("x1v")
    tok_pre = IO.get("tok_pre", [])

    def x1_load(buf, rowtile, deps, ch, colsl=None):
        if x1idx is None:
            src = x1T[rowtile * 128:(rowtile + 1) * 128,
                      colsl if colsl is not None else slice(0, To)]
            return S.dma("sync", buf, src, deps=deps, ch=ch)
        eo = colsl.start if colsl is not None else 0
        return S.idma(buf, x1v, x1idx_sb[:, rowtile:rowtile + 1],
                      deps=list(deps) + list(tok_pre), ch=6 + ch % 2,
                      element_offset=eo)
    _pfx = IO.get("pfx", "")
    sb = lambda name, shape, dt: stk.enter_context(
        nc.sbuf_tensor(_pfx + name, shape, dt))
    psu = lambda name, shape, dt=F32: stk.enter_context(
        nc.psum_tensor(_pfx + name, shape, dt))

    xn = sb("xn", [128, NK * To], BF16)
    x1idx_sb = sb("x1idx_sb", [128, NK], I32) if x1idx is not None else None
    if x1idx is not None:
        IO["tok_idx"] = S.dma("sync", x1idx_sb[:], x1idx, ch=0)
        tok_pre = list(tok_pre) + [IO["tok_idx"]]
    e_all = sb("e_all", [128, NK * 32], BF16)
    zsc = sb("zsc", [32, D], F32)
    zsn_b = sb("zsn_b", [32, D], BF16)
    wq_i = sb("wq_i", [128, 512], I32)
    wq_f = sb("wq_f", [128, 512], F32)
    s_rep = sb("s_rep", [128, 512], F32)
    wq_b = sb("wq_b", [128, D], BF16)
    xt_a = sb("xt_a", [128, To], F32)
    xt_b = sb("xt_b", [128, To], F32)
    sqt = sb("sqt", [128, To], F32)
    mrep = sb("mrep", [128, To], F32)
    rrep = sb("rrep", [128, To], F32)
    srow = sb("srow", [1, To], F32)
    qrow = sb("qrow", [1, To], F32)
    trow = sb("trow", [1, To], F32)
    s_b = sb("s_b", [32, To], BF16)
    wst_a = sb("wst_a", [128, 32 * 128], BF16)
    wst_b = sb("wst_b", [128, 32 * 128], BF16)
    wkv_f = sb("wkv_f", [128, 3072], F32)
    wkv_c = sb("wkv_c", [128, 3072], BF16)
    bq = sb("bq", [128, 48], F32)
    b0_b = sb("b0_b", [1, D], BF16)
    ones1r = sb("ones1r", [1, 128], F32)
    ones1b = sb("ones1b", [1, 512], BF16)
    scrow = sb("scrow", [1, 512], F32)
    onesc = sb("onesc", [128, 1], F32)
    ceps = sb("ceps", [128, 1], F32)
    out_b = sb("out_b", [128, To], BF16)
    idn_b = sb("idn_b", [128, 128], BF16)
    vt_s = sb("vt_s", [128, PART], BF16)

    ps_o = psu("ps_o", [128, 512])
    ps_e = psu("ps_e", [128, 512])
    ps_s = psu("ps_s", [32, 512])
    ps_tb = psu("ps_tb", [128, 128], BF16)
    ps_r = [psu(f"ps_r{i}", [1, 512]) for i in range(2 * NH)]


    t_sc = S.dma("sync", zsc[:], scaleT, ch=0)

    t_b0 = S.dma("sync", xt_a[0:8, 0:512],
                 bias0.rearrange("o (a f) -> (o a) f", a=8), ch=0)
    t_bq = S.dma("sync", bq[:], bkv_c, ch=0)
    t_set = t_sc = t_b0 = t_bq
    t_ones = S.op("vector", lambda e: e.memset(ones1r[:], 1.0))
    t_onesb = S.op("vector", lambda e: e.memset(ones1b[:], 1.0))
    t_onesc = S.op("vector", lambda e: e.memset(onesc[:], 1.0))
    t_ceps = S.op("vector", lambda e: e.memset(ceps[:], LN_EPS))
    t_zs3 = None
    pzs = None
    for q8 in range(D // 512):
        sl = slice(q8 * 512, (q8 + 1) * 512)
        t_z1 = S.dma("sync", wq_f[0:32, :], zeroT[:, sl],
                     deps=[pzs] if pzs else [], ch=3)
        t_z2 = S.op("vector", lambda e, sl=sl: e.tensor_tensor(
            out=wq_f[0:32, :], in0=wq_f[0:32, :], in1=zsc[0:32, sl],
            op=ALU.mult), deps=[t_z1, t_set])
        t_z3 = S.op("vector", lambda e: e.tensor_scalar(
            out=wq_f[0:32, :], in0=wq_f[0:32, :], scalar1=-1.0,
            scalar2=None, op0=ALU.mult))
        t_zs3 = S.op("vector", lambda e, sl=sl: e.tensor_copy(
            out=zsn_b[:, sl], in_=wq_f[0:32, :]))
        pzs = t_zs3
    t_b0c = S.op("vector", lambda e: e.tensor_copy(
        out=out_b[0:8, 0:512], in_=xt_a[0:8, 0:512]), deps=[t_set])
    t_b0d = S.dma("gpsimd", b0scr.rearrange("o (a f) -> (o a) f", a=8),
                  out_b[0:8, 0:512], deps=[t_b0c], ch=3)
    t_b0b = S.dma("sync", b0_b[:], b0scr, deps=[t_b0d], ch=0)
    t_id0 = S.op("gpsimd", lambda e: e.memset(idn_b[:], 0.0), inc=1)
    t_idn = S.op("gpsimd", lambda e: e.affine_select(
        out=idn_b[:], in_=idn_b[:], compare_op=ALU.not_equal, fill=1.0,
        base=0, pattern=[[-1, 128]], channel_multiplier=1),
        deps=[t_id0], inc=1)
    t_e0 = S.op("gpsimd", lambda e: e.memset(e_all[:], 0.0), inc=1)
    t_eall = t_e0
    for g in range(NK):
        t_eall = S.op("gpsimd", lambda e, g=g: e.affine_select(
            out=e_all[:, g * 32:(g + 1) * 32],
            in_=e_all[:, g * 32:(g + 1) * 32],
            compare_op=ALU.not_equal, fill=1.0,
            base=-g, pattern=[[1, 32]], channel_multiplier=0),
            deps=[t_e0], inc=1)

    ln_state = {"war": [t_b0b]}

    def emit_ln(src_dram, norm_into_xn=True, use_x1_loader=False):
        war = ln_state["war"]
        tma = tmb = None
        prevload = {0: None, 1: None}
        for g in range(NK):
            buf = xt_a if g % 2 == 0 else xt_b
            pu = prevload[g % 2]
            if use_x1_loader:
                t_l = x1_load(buf[:], g, deps=([pu] if pu else []) + war,
                              ch=g % 2)
            else:
                t_l = S.dma("sync", buf[:], src_dram[g * 128:(g + 1) * 128, :],
                            deps=([pu] if pu else []) + war, ch=1 + g % 2)
            t_sq = S.op("scalar", lambda e, b=buf: e.activation(
                out=sqt[:], in_=b[:], func=AF.Square),
                deps=[t_l] + ([tmb] if tmb else []))
            for h in range(NH):
                sl = slice(h * 512, (h + 1) * 512)
                tma = S.op("tensor", lambda e, b=buf, sl=sl, g=g, h=h:
                           e.matmul(ps_r[h][:], onesc[:], b[:, sl],
                                    start=(g == 0), stop=(g == NK - 1)),
                           deps=[t_l, t_onesc] + war)
                tmb = S.op("tensor", lambda e, sl=sl, g=g, h=h: e.matmul(
                    ps_r[NH + h][:], onesc[:], sqt[:, sl],
                    start=(g == 0), stop=(g == NK - 1)),
                    deps=[t_sq])
            prevload[g % 2] = tmb
        ev = []
        for h in range(NH):
            sl = slice(h * 512, (h + 1) * 512)
            ev.append(S.op("vector", lambda e, sl=sl, h=h: e.tensor_scalar(
                out=srow[:, sl], in0=ps_r[h][:], scalar1=1.0 / D,
                scalar2=None, op0=ALU.mult), deps=[tma, tmb]))
            ev.append(S.op("vector", lambda e, sl=sl, h=h: e.tensor_scalar(
                out=qrow[:, sl], in0=ps_r[NH + h][:], scalar1=1.0 / D,
                scalar2=None, op0=ALU.mult), deps=[tmb]))
        t_v1 = S.op("vector", lambda e: e.tensor_tensor(
            out=trow[:], in0=srow[:], in1=srow[:], op=ALU.mult), deps=ev)
        t_v2 = S.op("vector", lambda e: e.tensor_tensor(
            out=qrow[:], in0=qrow[:], in1=trow[:], op=ALU.subtract))
        t_v3 = S.op("scalar", lambda e: e.activation(
            out=trow[:], in_=qrow[:], func=AF.Sqrt, bias=ceps[0:1, 0:1]),
            deps=[t_v2, t_ceps])
        t_v4 = S.op("vector", lambda e: e.reciprocal(
            out=qrow[:], in_=trow[:]), deps=[t_v3])
        evb = None
        for h in range(NH):
            sl = slice(h * 512, (h + 1) * 512)
            t_m1 = S.op("tensor", lambda e, sl=sl: e.matmul(
                ps_e[:], ones1r[:], srow[0:1, sl], start=True, stop=True),
                deps=[t_v4, t_ones] + ([evb] if evb else []) + war
                + ([ln_state["ps_e"]] if ln_state.get("ps_e") else []))
            t_m2 = S.op("vector", lambda e, sl=sl: e.tensor_copy(
                out=mrep[:, sl], in_=ps_e[:]), deps=[t_m1])
            t_m3 = S.op("tensor", lambda e, sl=sl: e.matmul(
                ps_e[:], ones1r[:], qrow[0:1, sl], start=True, stop=True),
                deps=[t_m2])
            evb = S.op("vector", lambda e, sl=sl: e.tensor_copy(
                out=rrep[:, sl], in_=ps_e[:]), deps=[t_m3])
        tok_norm = evb
        ln_state["ps_e"] = evb
        tok_xn = None
        if norm_into_xn:
            prevload = {0: None, 1: None}
            for g in range(NK):
                buf = xt_a if g % 2 == 0 else xt_b
                pu = prevload[g % 2]
                if use_x1_loader:
                    t_l = x1_load(buf[:], g, deps=([pu] if pu else [tma, tmb]),
                                  ch=g % 2)
                else:
                    t_l = S.dma("sync", buf[:], src_dram[g * 128:(g + 1) * 128, :],
                                deps=([pu] if pu else [tma, tmb]), ch=1 + g % 2)
                t_n1 = S.op("vector", lambda e, b=buf: e.tensor_tensor(
                    out=sqt[:], in0=b[:], in1=mrep[:], op=ALU.subtract),
                    deps=[t_l, tok_norm, tmb])
                tok_xn = S.op("vector", lambda e, g=g: e.tensor_tensor(
                    out=xn[:, g * To:(g + 1) * To], in0=sqt[:], in1=rrep[:],
                    op=ALU.mult))
                prevload[g % 2] = tok_xn
        return tok_xn, tok_norm

    tok_xn1, _ = emit_ln(x1T, use_x1_loader=(x1idx is not None))

    tok_deq = []
    pv = {"wq": None, "srep": None, "store": None}
    for g in range(NK):
        for q8 in range(D // 512):
            sl = slice(q8 * 512, (q8 + 1) * 512)
            t_lw = S.dma("sync", wq_i[:], wqT[g * 128:(g + 1) * 128, sl],
                         deps=[pv["wq"]] if pv["wq"] else [], ch=3)
            t_sr = S.dma("gpsimd", scrow[:], zsc[g:g + 1, sl],
                         deps=[pv["bm"]] if pv.get("bm") else [], ch=2)
            t_bm = S.op("tensor", lambda e: e.matmul(
                ps_e[:], ones1r[:], scrow[:], start=True, stop=True),
                deps=[t_set, t_ones, t_sr]
                + ([pv["srep"]] if pv["srep"] else [])
                + ([ln_state["ps_e"]] if ln_state.get("ps_e") else []))
            pv["bm"] = t_bm
            ln_state["ps_e"] = None
            t_eb = S.op("vector", lambda e: e.tensor_copy(
                out=s_rep[:], in_=ps_e[:]), deps=[t_bm])
            pv["srep"] = t_eb
            ln_state["ps_e"] = t_eb
            t_cf = S.op("vector", lambda e: e.tensor_copy(
                out=wq_f[:], in_=wq_i[:]), deps=[t_lw])
            pv["wq"] = t_cf
            t_mul = S.op("vector", lambda e, sl=sl: e.tensor_tensor(
                out=wq_b[:, sl], in0=wq_f[:], in1=s_rep[:], op=ALU.mult),
                deps=[t_cf, t_eb]
                + ([pv["store"]] if pv["store"] else []))
        t_st = None
        for nt in range(NK):
            t_st = S.dma("gpsimd",
                         wdeq.rearrange("p (nt g nn) -> p nt g nn",
                                        nt=NK, g=NK)[:, nt, g, :],
                         wq_b[:, nt * 128:(nt + 1) * 128],
                         deps=[t_mul], ch=0)
        pv["store"] = t_st
        tok_deq.append(t_st)

    tok_kv = None
    pkv = {}
    for g in range(NK):
        for hlf in range(2):
            t_lk = S.dma("sync", wkv_f[:],
                         wkvT[g * 128:(g + 1) * 128,
                              hlf * 3072:(hlf + 1) * 3072],
                         deps=[pkv["c"]] if pkv.get("c") else [],
                         ch=6 + hlf)
            t_ck = S.op("vector", lambda e: e.tensor_copy(
                out=wkv_c[:], in_=wkv_f[:]),
                deps=[t_lk] + ([pkv["st"]] if pkv.get("st") else []))
            pkv["c"] = t_ck
            for rc in range(24):
                tok_kv = S.dma("gpsimd",
                               wkvb.rearrange("p (rc g nn) -> p rc g nn",
                                              rc=48, g=NK)[
                                   :, hlf * 24 + rc, g, :],
                               wkv_c[:, rc * 128:(rc + 1) * 128],
                               deps=[t_ck], ch=3)
            pkv["st"] = tok_kv

    tok_x2 = []
    tok_s2mm = []
    pv2 = {"wtile": None, "x1res": None, "psol": None}
    for h in range(NH):
        sl = slice(h * 512, (h + 1) * 512)
        tsm = None
        for g in range(NK):
            tsm = S.op("tensor", lambda e, g=g, sl=sl: e.matmul(
                ps_s[:], e_all[:, g * 32:(g + 1) * 32],
                xn[:, g * To + sl.start:g * To + sl.stop],
                start=(g == 0), stop=(g == NK - 1)),
                deps=[tok_xn1, t_eall]
                + ([pv2["sevac"]] if "sevac" in pv2 else []))
        t_sev = S.op("vector", lambda e, sl=sl: e.tensor_copy(
            out=s_b[:, sl], in_=ps_s[:]), deps=[tsm])
        pv2["sevac"] = t_sev
        for nt in range(NK):
            deps_first = [tok_deq[-1], t_sev]
            if pv2.get("psol"):
                deps_first.append(pv2["psol"])
            wbuf = wst_a if nt % 2 == 0 else wst_b
            t_lw = S.dma("sync", wbuf[:],
                         wdeq[:, nt * NK * 128:(nt + 1) * NK * 128],
                         deps=[pv2[f"wt{nt % 2}"]]
                         if pv2.get(f"wt{nt % 2}") else [tok_deq[-1]],
                         ch=4 + nt % 2, tag="wdq")
            tmm = None
            for g in range(NK):
                tmm = S.op("tensor", lambda e, g=g, nt=nt, sl=sl, w=wbuf:
                           e.matmul(ps_o[:], w[:, g * 128:(g + 1) * 128],
                                    xn[:, g * To + sl.start:g * To + sl.stop],
                                    start=(g == 0), stop=False),
                           deps=[t_lw] + deps_first)
            pv2[f"wt{nt % 2}"] = tmm
            t_zc2 = S.op("tensor", lambda e, nt=nt, sl=sl: e.matmul(
                ps_o[:], zsn_b[0:32, nt * 128:(nt + 1) * 128], s_b[0:32, sl],
                start=False, stop=False), deps=[t_zs3, t_sev])
            t_bc2 = S.op("tensor", lambda e, nt=nt: e.matmul(
                ps_o[:], b0_b[0:1, nt * 128:(nt + 1) * 128], ones1b[:],
                start=False, stop=True), deps=[t_b0b, t_onesb])
            if x1idx is None:
                t_rx = S.dma("sync", xt_a[:, sl] if h == 0 else xt_b[:, sl],
                             x1T[nt * 128:(nt + 1) * 128, sl],
                             deps=[pv2["x1res"]] if pv2["x1res"] else [], ch=1)
            else:
                t_rx = x1_load(xt_a[:, sl] if h == 0 else xt_b[:, sl], nt,
                               deps=[pv2["x1res"]] if pv2["x1res"] else [],
                               ch=h, colsl=sl)
            rbuf = xt_a if h == 0 else xt_b
            t_x2 = S.op("vector", lambda e, sl=sl, b=rbuf: e.scalar_tensor_tensor(
                out=sqt[:, sl], in0=ps_o[:], scalar=1.0, in1=b[:, sl],
                op0=ALU.mult, op1=ALU.add), deps=[t_bc2, t_rx])
            pv2["x1res"] = t_x2
            pv2["psol"] = t_x2
            t_so = S.dma("gpsimd", x2T[nt * 128:(nt + 1) * 128, sl],
                         sqt[:, sl], deps=[t_x2], ch=1)
            tok_x2.append(t_so)
            pv2["x1res"] = t_so

    ln_state["war"] = [tok_x2[-1], pv2["psol"]]
    tok_xn2, _ = emit_ln(x2T)

    t_qkv_done = []
    pvq = {"w": None, "w2": None, "ob": None, "vs": None}
    for part in range(3):
        for rt in range(PART // 128):
            rcol = part * 16 + rt
            wb = wst_a if rcol % 2 == 0 else wst_b
            t_lw = S.dma("sync", wb[:],
                         wkvb[:, rcol * NK * 128:(rcol + 1) * NK * 128],
                         deps=[pvq[f"wt{rcol % 2}"]]
                         if pvq.get(f"wt{rcol % 2}") else [tok_kv],
                         ch=4 + rcol % 2)
            tmm = None
            for g in range(NK):
                for h in range(NH):
                    sl = slice(h * 512, (h + 1) * 512)
                    tmm = S.op("tensor", lambda e, wb=wb, g=g, sl=sl, h=h:
                               e.matmul(ps_o[:] if h == 0 else ps_e[:],
                                        wb[:, g * 128:(g + 1) * 128],
                                        xn[:, g * To + sl.start:
                                           g * To + sl.stop],
                                        start=(g == 0), stop=(g == NK - 1)),
                               deps=[t_lw, tok_xn2]
                               + ([pvq["ev"]] if pvq.get("ev") else [])
                               + ([ln_state["ps_e"]]
                                  if ln_state.get("ps_e") else []))
                pvq[f"wt{rcol % 2}"] = tmm
                ln_state["ps_e"] = None
            evs = []
            for h in range(NH):
                pso = ps_o if h == 0 else ps_e
                sl = slice(h * 512, (h + 1) * 512)
                evs.append(S.op("scalar", lambda e, p=pso, sl=sl, rc=rcol:
                                e.activation(out=out_b[:, sl], in_=p[:],
                                             func=AF.Identity,
                                             bias=bq[:, rc:rc + 1]),
                                deps=[tmm, t_set] +
                                ([pvq["ob"]] if pvq["ob"] else [])))
            pvq["ob"] = evs[-1]
            pvq["ev"] = evs[-1]
            if part < 2:
                dst = qd if part == 0 else kd
                t_so = S.dma("gpsimd", dst[rt * 128:(rt + 1) * 128, :],
                             out_b[:], deps=evs, ch=2)
                pvq["ob"] = t_so
                t_qkv_done.append(t_so)
            else:
                tts = []
                for tt in range(To // 128):
                    t_tp = S.op("tensor", lambda e, tt=tt: e.transpose(
                        ps_tb[:], out_b[:, tt * 128:(tt + 1) * 128],
                        idn_b[:]), deps=evs + [t_idn]
                        + ([pvq["vs"]] if pvq["vs"] else []))
                    t_ev = S.op("vector", lambda e, tt=tt, rt=rt: e.tensor_copy(
                        out=vt_s[tt * 128 if False else 0:128,
                                 rt * 128:(rt + 1) * 128]
                        if False else
                        vt_s[0:128, rt * 128:(rt + 1) * 128],
                        in_=ps_tb[:]), deps=[t_tp])
                    t_sv = S.dma("gpsimd",
                                 vt[tt * 128:(tt + 1) * 128,
                                    rt * 128:(rt + 1) * 128],
                                 vt_s[0:128, rt * 128:(rt + 1) * 128],
                                 deps=[t_ev], ch=2)
                    pvq["vs"] = t_sv
                    tts.append(t_sv)
                t_qkv_done.append(tts[-1])
                pvq["ob"] = tts[-1]

    S.op("gpsimd", lambda e: e.engine_nop(),
         deps=[tok_x2[-1], t_qkv_done[-1]], inc=1)

    return {'x2': tok_x2[-1], 'kv': t_qkv_done, 'kd': pvq.get('obk'),
            'last': t_qkv_done[-1]}


def build_ffn0_qkv():
    nc = bass.Bass("TRN2", target_bir_lowering=False)
    To = OWN_T
    NK = D // 128
    NH = To // 512

    x1T = nc.dram_tensor("x1T", [D, To], F32, kind="ExternalInput").ap()
    wqT = nc.dram_tensor("wqT", [D, D], I32, kind="ExternalInput").ap()
    scaleT = nc.dram_tensor("scaleT", [32, D], F32, kind="ExternalInput").ap()
    zeroT = nc.dram_tensor("zeroT", [32, D], F32, kind="ExternalInput").ap()
    bias0 = nc.dram_tensor("bias0", [1, D], F32, kind="ExternalInput").ap()
    wkvT = nc.dram_tensor("wkvT", [D, 3 * PART], F32, kind="ExternalInput").ap()
    bkv_c = nc.dram_tensor("bkv_c", [128, 48], F32, kind="ExternalInput").ap()

    x2T = nc.dram_tensor("x2T", [D, To], F32, kind="ExternalOutput").ap()
    qd = nc.dram_tensor("qd", [PART, To], BF16, kind="ExternalOutput").ap()
    kd = nc.dram_tensor("kd", [PART, To], BF16, kind="ExternalOutput").ap()
    vt = nc.dram_tensor("vt", [To, PART], BF16, kind="ExternalOutput").ap()

    wdeq = nc.dram_tensor("wdeq", [128, 32 * 32 * 128], BF16, kind="Internal").ap()
    wkvb = nc.dram_tensor("wkvb", [128, 48 * 32 * 128], BF16, kind="Internal").ap()
    b0scr = nc.dram_tensor("b0scr", [1, D], BF16, kind="Internal").ap()

    with contextlib.ExitStack() as stk:
        S = Sched(nc, stk)
        IO = dict(x1T=x1T, wqT=wqT, scaleT=scaleT, zeroT=zeroT, bias0=bias0,
                  wkvT=wkvT, bkv_c=bkv_c, x2T=x2T, qd=qd, kd=kd, vt=vt,
                  wdeq=wdeq, wkvb=wkvb, b0scr=b0scr)
        with contextlib.ExitStack() as pstk:
            phase_ffn0(nc, S, pstk, IO)
        S.emit(block=stk.enter_context(nc.Block()))
    return nc


def phase_attn(nc, S, stk, IO):
    To = OWN_T
    NK = D // 128
    NQT = To // 128
    NKB = T // 512
    NCT = PART // 128
    qd = IO.get('qd')
    kg = IO.get('kg')
    vg = IO.get('vg')
    woT = IO.get('woT')
    bo_c = IO.get('bo_c')
    x2T = IO.get('x2T')
    qpos = IO.get('qpos')
    x3T = IO.get('x3T')
    xn3T = IO.get('xn3T')
    x3scr = IO.get('x3scr')
    kga = IO.get('kga')
    vga = IO.get('vga')
    tok_pre = IO.get('tok_pre', [])
    _pfx = IO.get("pfx", "")
    sb = lambda name, shape, dt: stk.enter_context(
        nc.sbuf_tensor(_pfx + name, shape, dt))
    psu = lambda name, shape, dt=F32: stk.enter_context(
        nc.psum_tensor(_pfx + name, shape, dt))

    kh = sb("kh", [128, 2 * T], BF16)
    vh = sb("vh", [128, (T // 128) * 256], BF16)
    qh = sb("qh", [128, 2 * To], BF16)
    vals = sb("vals", [128, NQT * PART], BF16)
    valsT = sb("valsT", [128, NCT * To], BF16)
    NKB2 = T // 1024
    iota_r = sb("iota_r", [128, 1024], F32)
    qkc = sb("qkc", [128, NQT * NKB2], F32)
    selm = sb("selm", [128, 1024], F32)
    sc_f = sb("sc_f", [128, 1024], F32)
    p_b = sb("p_b", [128, 1024], BF16)
    pt_b = sb("pt_b", [128, 1024], BF16)
    vacc = sb("vacc", [128, 256], F32)
    m_run = sb("m_run", [128, 1], F32)
    mn_c = sb("mn_c", [128, 1], F32)
    am_c = sb("am_c", [128, 1], F32)
    ac_c = sb("ac_c", [128, 1], F32)
    nm_c = sb("nm_c", [128, 1], F32)
    lb_c = sb("lb_c", [128, 1], F32)
    l_run = sb("l_run", [128, 1], F32)
    rl_c = sb("rl_c", [128, 1], F32)
    idn_b = sb("idn_b", [128, 128], BF16)
    bo_sb = sb("bo_sb", [128, 32], F32)
    qp_c = sb("qp_c", [128, NQT], F32)
    onesc = sb("onesc", [128, 1], F32)
    ones1r = sb("ones1r", [1, 128], F32)
    ceps = sb("ceps", [128, 1], F32)

    ps_s = psu("ps_s", [128, 1024])
    ps_v = psu("ps_v", [128, 256])
    ps_tb = psu("ps_tb", [128, 1024], BF16)
    ps_o = psu("ps_o", [128, 512])
    ps_e = psu("ps_e", [128, 512])
    ps_r = [psu(f"ps_r{i}", [1, 512]) for i in range(2)]

    NH2 = To // 512
    ps_q = [ps_o, ps_e]

    t_bo = S.dma("sync", bo_sb[:], bo_c, ch=0)
    t_qp = S.dma("sync", qp_c[:], qpos, ch=0)
    t_set = t_bo = t_qp
    t_onesc = S.op("vector", lambda e: e.memset(onesc[:], 1.0))
    t_ones1r = S.op("vector", lambda e: e.memset(ones1r[:], 1.0))
    t_ceps = S.op("vector", lambda e: e.memset(ceps[:], LN_EPS))
    t_id0 = S.op("gpsimd", lambda e: e.memset(idn_b[:], 0.0), inc=1)
    t_idn = S.op("gpsimd", lambda e: e.affine_select(
        out=idn_b[:], in_=idn_b[:], compare_op=ALU.not_equal, fill=1.0,
        base=0, pattern=[[-1, 128]], channel_multiplier=1),
        deps=[t_id0], inc=1)
    dqstk = contextlib.ExitStack()
    if IO.get("dq_io") is not None:
        _dq_emit, _dq_n, _dq_done = make_dequant_emitter(
            nc, S, dqstk, IO["dq_io"], pfx="dq_")
        IO["dq_emit"] = _dq_emit
        IO["dq_done"] = _dq_done
        IO["dq_per_head"] = (_dq_n + HEADS - 1) // HEADS
    t_iot = S.op("gpsimd", lambda e: e.iota(
        out=iota_r[:], pattern=[[1, 1024]], base=0, channel_multiplier=0,
        allow_small_or_imprecise_dtypes=True), inc=1)
    t_qkc = None
    for qt in range(NQT):
        for kb in range(NKB2):
            t_qkc = S.op("vector", lambda e, qt=qt, kb=kb: e.tensor_scalar(
                out=qkc[:, qt * NKB2 + kb:qt * NKB2 + kb + 1],
                in0=qp_c[:, qt:qt + 1], scalar1=float(-kb * 1024),
                scalar2=None, op0=ALU.add), deps=[t_set])

    pv = {}
    for h in range(HEADS):
        if kga is None:
            t_k = S.dma("sync", kh[:].rearrange("p (c t) -> p c t", c=2),
                        kg.rearrange("(c p) t -> p c t", p=128)[
                            :, 2 * h:2 * h + 2, :],
                        deps=[pv["kh"]] if "kh" in pv else [], ch=1)
            t_v = S.dma("sync", vh[:].rearrange("p (a c) -> p a c", a=T // 128),
                        vg.rearrange("(a p) c -> p a c", p=128)[
                            :, :, 256 * h:256 * (h + 1)],
                        deps=[pv["vh"]] if "vh" in pv else [], ch=2)
        else:
            kd_ = [pv["kh"]] if "kh" in pv else []
            kd_ = kd_ + list(tok_pre)
            for cs in range(2):
                for r in range(8):
                    t_k = S.dma("sync",
                                kh[:].rearrange("p (c t) -> p c t", c=2)
                                [:, cs, r * To:(r + 1) * To],
                                kga[r, (2 * h + cs) * 128:
                                    (2 * h + cs + 1) * 128, :],
                                deps=kd_, ch=1)
            vd_ = [pv["vh"]] if "vh" in pv else []
            vd_ = vd_ + list(tok_pre)
            tpr = To // 128
            for aa in range(T // 128):
                t_v = S.dma("sync",
                            vh[:].rearrange("p (a c) -> p a c", a=T // 128)
                            [:, aa, :],
                            vga[aa // tpr, (aa % tpr) * 128:
                                (aa % tpr + 1) * 128,
                                256 * h:256 * (h + 1)],
                            deps=vd_, ch=2)
        t_q = S.dma("sync", qh[:].rearrange("p (c t) -> p c t", c=2),
                    qd.rearrange("(c p) t -> p c t", p=128)[
                        :, 2 * h:2 * h + 2, :],
                    deps=([pv["qh"]] if "qh" in pv else []) + list(tok_pre),
                    ch=3)
        if IO.get("dq_emit"):
            IO["dq_emit"](IO.get("dq_per_head", 10))
        for qt in range(NQT):
            t_ms = S.op("vector", lambda e: e.memset(m_run[:], 1e30),
                        deps=[pv["fin"]] if "fin" in pv else [])
            t_ls = S.op("vector", lambda e: e.memset(l_run[:], 0.0))
            t_vs = S.op("vector", lambda e: e.memset(vacc[:], 0.0))
            for kb in range(NKB2):
                j = qt * NKB2 + kb
                tmm = None
                for half in range(2):
                    psl = slice(half * 512, (half + 1) * 512)
                    for cs in range(2):
                        tmm = S.op("tensor", lambda e, cs=cs, qt=qt, kb=kb,
                                   psl=psl, half=half: e.matmul(
                            ps_s[:, psl],
                            qh[:, cs * To + qt * 128:cs * To + (qt + 1) * 128],
                            kh[:, cs * T + kb * 1024 + half * 512:
                               cs * T + kb * 1024 + half * 512 + 512],
                            start=(cs == 0), stop=(cs == 1)),
                            deps=[t_k, t_q] + ([pv["sc"]] if "sc" in pv else []))
                t_sel = S.op("vector", lambda e, j=j: e.tensor_scalar(
                    out=selm[:], in0=iota_r[:], scalar1=qkc[:, j:j + 1],
                    scalar2=-10000.0, op0=ALU.is_gt, op1=ALU.mult),
                    deps=[t_qkc, t_iot])
                t_sc = S.op("vector", lambda e: e.tensor_tensor(
                    out=sc_f[:], in0=ps_s[:], in1=selm[:], op=ALU.add),
                    deps=[tmm])
                pv["sc"] = t_sc
                nm_old = m_run if kb % 2 == 0 else mn_c
                nm_new = mn_c if kb % 2 == 0 else m_run
                t_rm = S.op("vector", lambda e: e.reduce_max(
                    out=am_c[:], in_=sc_f[:], axis=AX.X, negate=True))
                t_mn = S.op("vector", lambda e, a=nm_old, b=nm_new:
                            e.tensor_tensor(out=b[:], in0=am_c[:], in1=a[:],
                                            op=ALU.min), deps=[t_ms])
                t_am = S.op("vector", lambda e, a=nm_old, b=nm_new:
                            e.tensor_tensor(out=am_c[:], in0=b[:], in1=a[:],
                                            op=ALU.subtract))
                t_al = S.op("scalar", lambda e: e.activation(
                    out=ac_c[:], in_=am_c[:], func=AF.Exp),
                    deps=[t_am] + ([pv["pexp"]] if "pexp" in pv else []))
                t_pe = S.op("scalar", lambda e, b=nm_new: e.activation(
                    out=p_b[:], in_=sc_f[:], func=AF.Exp, bias=b[:],
                    accum_out=lb_c[:]), deps=[t_mn, t_al])
                pv["pexp"] = t_pe
                t_lr = S.op("vector", lambda e: e.scalar_tensor_tensor(
                    out=l_run[:], in0=l_run[:], scalar=ac_c[:], in1=lb_c[:],
                    op0=ALU.mult, op1=ALU.add), deps=[t_pe, t_ls])
                tps = []
                for st4 in range(8):
                    t_tp = S.op("tensor", lambda e, st4=st4: e.transpose(
                        ps_tb[:, st4 * 128:(st4 + 1) * 128],
                        p_b[:, st4 * 128:(st4 + 1) * 128], idn_b[:]),
                        deps=[t_pe, t_idn]
                        + ([pv["ptev"]] if "ptev" in pv else []))
                    tps.append(t_tp)
                t_ev = S.op("vector", lambda e: e.tensor_copy(
                    out=pt_b[:], in_=ps_tb[:]), deps=[tps[-1]])
                pv["ptev"] = t_ev
                tav = None
                for st4 in range(8):
                    tav = S.op("tensor", lambda e, st4=st4, kb=kb: e.matmul(
                        ps_v[:], pt_b[:, st4 * 128:(st4 + 1) * 128],
                        vh[:, (kb * 8 + st4) * 256:(kb * 8 + st4 + 1) * 256],
                        start=(st4 == 0), stop=(st4 == 7)),
                        deps=[t_ev, t_v]
                        + ([pv["vaccr"]] if "vaccr" in pv else []))
                t_va = S.op("vector", lambda e: e.scalar_tensor_tensor(
                    out=vacc[:], in0=vacc[:], scalar=ac_c[:], in1=ps_v[:],
                    op0=ALU.mult, op1=ALU.add), deps=[tav, t_al, t_vs])
                pv["vaccr"] = t_va
            t_rl = S.op("vector", lambda e: e.reciprocal(
                out=rl_c[:], in_=l_run[:]), deps=[t_lr])
            t_ob = S.op("vector", lambda e, h=h, qt=qt: e.tensor_scalar(
                out=vals[:, qt * PART + h * 256:qt * PART + (h + 1) * 256],
                in0=vacc[:], scalar1=rl_c[:], scalar2=None, op0=ALU.mult),
                deps=[t_va, t_rl])
            pv["fin"] = t_ob
        pv["kh"] = tmm
        pv["vh"] = tav
        pv["qh"] = tmm
    if IO.get("dq_emit"):
        IO["dq_emit"](1000)
        pv3_seed = list(IO["dq_done"]())
    else:
        pv3_seed = []
    dqstk.close()
    wo_f = sb("wo_f", [128, 128], F32)
    wo_b = sb("wo_b", [128, 128], BF16)
    wo_f2 = sb("wo_f2", [128, 128], F32)
    wo_b2 = sb("wo_b2", [128, 128], BF16)
    xt_a = sb("xt_a", [128, To], F32)
    xt_b = sb("xt_b", [128, To], F32)
    sqt = sb("sqt", [128, To], F32)
    mrep = sb("mrep", [128, To], F32)
    rrep = sb("rrep", [128, To], F32)
    srow = sb("srow", [1, To], F32)
    qrow = sb("qrow", [1, To], F32)
    trow = sb("trow", [1, To], F32)
    xnb = sb("xnb", [128, To], BF16)

    pv2 = {}
    t_vT = None
    for qt in range(NQT):
        for ct in range(NCT):
            t_tp = S.op("tensor", lambda e, qt=qt, ct=ct: e.transpose(
                ps_tb[:, 0:128],
                vals[:, qt * PART + ct * 128:qt * PART + (ct + 1) * 128],
                idn_b[:]), deps=[pv["fin"], t_idn]
                + ([t_vT] if t_vT else [pv.get("ptev")]))
            t_vT = S.op("vector", lambda e, qt=qt, ct=ct: e.tensor_copy(
                out=valsT[:, ct * To + qt * 128:ct * To + qt * 128 + 128],
                in_=ps_tb[:, 0:128]), deps=[t_tp])

    tok_x3 = []
    pv3 = {}
    for nt in range(NK):
        tmm = None
        for ct in range(NCT):
            wf = wo_f if ct % 2 == 0 else wo_f2
            wb = wo_b if ct % 2 == 0 else wo_b2
            t_lw = S.dma("sync", wf[:],
                         woT[ct * 128:(ct + 1) * 128, nt * 128:(nt + 1) * 128],
                         deps=[pv3["w"]] if "w" in pv3 else pv3_seed,
                         ch=4 + ct % 2)
            t_cw = S.op("vector", lambda e, wf=wf, wb=wb: e.tensor_copy(
                out=wb[:], in_=wf[:]),
                deps=[t_lw] + ([pv3["w2"]] if "w2" in pv3 else []))
            pv3["w"] = t_cw
            for hh in range(NH2):
                sl = slice(hh * 512, (hh + 1) * 512)
                tmm = S.op("tensor", lambda e, wb=wb, ct=ct, sl=sl, hh=hh:
                           e.matmul(ps_o[:] if hh == 0 else ps_e[:], wb[:],
                                    valsT[:, ct * To + sl.start:ct * To + sl.stop],
                                    start=(ct == 0), stop=(ct == NCT - 1)),
                           deps=[t_cw, t_vT] + pv3_seed
                           + ([pv3["ev"]] if "ev" in pv3 else []))
            pv3["w2"] = tmm
        t_rx = S.dma("sync", xt_a[:], x2T[nt * 128:(nt + 1) * 128, :],
                     deps=[pv3["rx"]] if "rx" in pv3 else pv3_seed, ch=1)
        evs = []
        for hh in range(NH2):
            pso = ps_o if hh == 0 else ps_e
            sl = slice(hh * 512, (hh + 1) * 512)
            t_bi = S.op("scalar", lambda e, p=pso, sl=sl, nt=nt: e.activation(
                out=sqt[:, sl], in_=p[:], func=AF.Identity,
                bias=bo_sb[:, nt:nt + 1]),
                deps=[tmm, t_set] + ([pv3["ev"]] if "ev" in pv3 else [])
                + pv3.get("sq", []))
            evs.append(t_bi)
        pv3["ev"] = evs[-1]
        t_x3 = S.op("vector", lambda e: e.tensor_tensor(
            out=sqt[:], in0=sqt[:], in1=xt_a[:], op=ALU.add),
            deps=evs + [t_rx])
        pv3["rx"] = t_x3
        t_s2 = S.dma("gpsimd", x3scr[nt * 128:(nt + 1) * 128, :], sqt[:],
                     deps=[t_x3], ch=1)
        if x3T is not None:
            t_s1 = S.dma("gpsimd", x3T[nt * 128:(nt + 1) * 128, :], sqt[:],
                         deps=[t_x3], ch=0)
        else:
            t_s1 = t_s2
        pv3["ev"] = t_x3
        pv3["sq"] = [t_s1, t_s2]
        tok_x3 += [t_s1, t_s2]

    tma = tmb = None
    prevload = {0: None, 1: None}
    war = [tok_x3[-2], tok_x3[-1], pv3["rx"]]
    for g in range(NK):
        buf = xt_a if g % 2 == 0 else xt_b
        pu = prevload[g % 2]
        t_l = S.dma("sync", buf[:], x3scr[g * 128:(g + 1) * 128, :],
                    deps=([pu] if pu else []) + war, ch=1 + g % 2)
        t_sq = S.op("scalar", lambda e, b=buf: e.activation(
            out=sqt[:], in_=b[:], func=AF.Square),
            deps=[t_l] + ([tmb] if tmb else []) + [pv3["rx"]]
            + pv3.get("sq", []))
        for hh in range(NH2):
            sl = slice(hh * 512, (hh + 1) * 512)
            tma = S.op("tensor", lambda e, b=buf, sl=sl, g=g, hh=hh:
                       e.matmul(ps_r[hh][:], onesc[:], b[:, sl],
                                start=(g == 0), stop=(g == NK - 1)),
                       deps=[t_l, t_onesc] + war)
            tmb = S.op("tensor", lambda e, sl=sl, g=g, hh=hh: e.matmul(
                ps_q[hh][0:1, :], onesc[:], sqt[:, sl],
                start=(g == 0), stop=(g == NK - 1)),
                deps=[t_sq] + ([pv3["ev"]] if g == 0 else []))
        prevload[g % 2] = tmb
    ev = []
    for hh in range(NH2):
        sl = slice(hh * 512, (hh + 1) * 512)
        ev.append(S.op("vector", lambda e, sl=sl, hh=hh: e.tensor_scalar(
            out=srow[:, sl], in0=ps_r[hh][:], scalar1=1.0 / D,
            scalar2=None, op0=ALU.mult), deps=[tma, tmb]))
        ev.append(S.op("vector", lambda e, sl=sl, hh=hh: e.tensor_scalar(
            out=qrow[:, sl], in0=ps_q[hh][0:1, :], scalar1=1.0 / D,
            scalar2=None, op0=ALU.mult), deps=[tmb]))
    t_v1 = S.op("vector", lambda e: e.tensor_tensor(
        out=trow[:], in0=srow[:], in1=srow[:], op=ALU.mult), deps=ev)
    t_v2 = S.op("vector", lambda e: e.tensor_tensor(
        out=qrow[:], in0=qrow[:], in1=trow[:], op=ALU.subtract))
    t_v3 = S.op("scalar", lambda e: e.activation(
        out=trow[:], in_=qrow[:], func=AF.Sqrt, bias=ceps[0:1, 0:1]),
        deps=[t_v2, t_ceps])
    t_v4 = S.op("vector", lambda e: e.reciprocal(
        out=qrow[:], in_=trow[:]), deps=[t_v3])
    evb = None
    for hh in range(NH2):
        sl = slice(hh * 512, (hh + 1) * 512)
        t_m1 = S.op("tensor", lambda e, sl=sl: e.matmul(
            ps_e[:], ones1r[:], srow[0:1, sl], start=True, stop=True),
            deps=[t_v4, t_ones1r] + ([evb] if evb else [pv3["ev"]]))
        t_m2 = S.op("vector", lambda e, sl=sl: e.tensor_copy(
            out=mrep[:, sl], in_=ps_e[:]), deps=[t_m1])
        t_m3 = S.op("tensor", lambda e, sl=sl: e.matmul(
            ps_e[:], ones1r[:], qrow[0:1, sl], start=True, stop=True),
            deps=[t_m2])
        evb = S.op("vector", lambda e, sl=sl: e.tensor_copy(
            out=rrep[:, sl], in_=ps_e[:]), deps=[t_m3])
    tok_norm = evb
    prevload = {0: None, 1: None}
    t_last = None
    for g in range(NK):
        buf = xt_a if g % 2 == 0 else xt_b
        pu = prevload[g % 2]
        t_l = S.dma("sync", buf[:], x3scr[g * 128:(g + 1) * 128, :],
                    deps=([pu] if pu else [tma, tmb]), ch=1 + g % 2)
        t_n1 = S.op("vector", lambda e, b=buf: e.tensor_tensor(
            out=sqt[:], in0=b[:], in1=mrep[:], op=ALU.subtract),
            deps=[t_l, tok_norm, tmb])
        t_n2 = S.op("vector", lambda e: e.tensor_tensor(
            out=xnb[:], in0=sqt[:], in1=rrep[:], op=ALU.mult),
            deps=([t_last] if t_last else []))
        t_st = S.dma("gpsimd", xn3T[g * 128:(g + 1) * 128, :], xnb[:],
                     deps=[t_n2], ch=2)
        prevload[g % 2] = t_st
        t_last = t_st

    S.op("gpsimd", lambda e: e.engine_nop(),
         deps=[t_last, tok_x3[-2]], inc=1)

    return {'x3': tok_x3, 'xn3': t_last}


def build_attn():
    nc = bass.Bass("TRN2", target_bir_lowering=False)
    To = OWN_T
    NK = D // 128
    NQT = To // 128
    NKB = T // 512
    NCT = PART // 128

    qd = nc.dram_tensor("qd", [PART, To], BF16, kind="ExternalInput").ap()
    kg = nc.dram_tensor("kg", [PART, T], BF16, kind="ExternalInput").ap()
    vg = nc.dram_tensor("vg", [T, PART], BF16, kind="ExternalInput").ap()
    woT = nc.dram_tensor("woT", [PART, D], F32, kind="ExternalInput").ap()
    bo_c = nc.dram_tensor("bo_c", [128, 32], F32, kind="ExternalInput").ap()
    x2T = nc.dram_tensor("x2T", [D, To], F32, kind="ExternalInput").ap()
    qpos = nc.dram_tensor("qpos", [128, NQT], F32, kind="ExternalInput").ap()

    x3T = nc.dram_tensor("x3T", [D, To], F32, kind="ExternalOutput").ap()
    xn3T = nc.dram_tensor("xn3T", [D, To], BF16, kind="ExternalOutput").ap()
    x3scr = nc.dram_tensor("x3scr", [D, To], F32, kind="Internal").ap()

    with contextlib.ExitStack() as stk:
        S = Sched(nc, stk)
        IO = dict(qd=qd, kg=kg, vg=vg, woT=woT, bo_c=bo_c, x2T=x2T, qpos=qpos,
                  x3T=x3T, xn3T=xn3T, x3scr=x3scr)
        with contextlib.ExitStack() as pstk:
            phase_attn(nc, S, pstk, IO)
        S.emit(block=stk.enter_context(nc.Block()))
    return nc


HSL = 1408


def make_dequant_emitter(nc, S, stk, IO, pfx="dq_"):
    NK = D // 128
    NHT = HSL // 128
    w1c, a1X = IO["w1c"], IO["a1X"]
    w3c, a3X = IO["w3c"], IO["a3X"]
    w2c, a2X = IO["w2c"], IO["a2X"]
    w1d, w3d, w2d = IO["w1d"], IO["w3d"], IO["w2d"]
    sb = lambda name, shape, dt: stk.enter_context(
        nc.sbuf_tensor(pfx + name, shape, dt))
    WQC = 2048
    code_i = sb("code_i", [128, WQC], I32)
    code_h = sb("code_h", [128, WQC], mybir.dt.float16)
    lut_v = sb("lut_v", [128, WQC], mybir.dt.float16)
    lut_t = sb("lut_t", [128, WQC], mybir.dt.float16)
    a_repf = sb("a_repf", [128, WQC], F32)
    wd_b = sb("wd_b", [128, WQC], BF16)
    pv = {}
    last = {}

    def w1d_3(t):
        return t.rearrange("p (ht a hh) -> p ht a hh", ht=NHT, a=NK)

    def emit_tile(codes, aX, out_dram, a, width, tiled):
        for wh in range(0, width, WQC):
            wsl = slice(wh, min(wh + WQC, width))
            w = wsl.stop - wsl.start
            t_lc = S.dma("gpsimd", code_i[:, 0:w],
                         codes[a * 128:(a + 1) * 128, wsl],
                         deps=[pv["code"]] if pv.get("code") else [], ch=4)
            t_la = S.dma("gpsimd", a_repf[:, 0:w],
                         aX[a * 128:(a + 1) * 128, wsl],
                         deps=[pv["arep"]] if pv.get("arep") else [], ch=6)
            last["lc"] = t_lc
            last["la"] = t_la
            t_cf = S.op("gpsimd", lambda e, w=w: e.tensor_copy(
                out=code_h[:, 0:w], in_=code_i[:, 0:w]), deps=[t_lc], inc=1)
            pv["code"] = t_cf
            t_acc = S.op("gpsimd", lambda e, w=w: e.tensor_scalar(
                out=lut_v[:, 0:w], in0=code_h[:, 0:w], scalar1=0.0,
                scalar2=float(NF4[0]), op0=ALU.is_equal, op1=ALU.mult),
                deps=[t_cf], inc=1)
            for i in range(1, 16):
                S.op("gpsimd", lambda e, i=i, w=w: e.tensor_scalar(
                    out=lut_t[:, 0:w], in0=code_h[:, 0:w],
                    scalar1=float(i), scalar2=float(NF4[i]),
                    op0=ALU.is_equal, op1=ALU.mult), inc=1)
                t_acc = S.op("gpsimd", lambda e, w=w: e.tensor_tensor(
                    out=lut_v[:, 0:w], in0=lut_v[:, 0:w],
                    in1=lut_t[:, 0:w], op=ALU.add), inc=1)
            t_mul = S.op("gpsimd", lambda e, w=w: e.tensor_tensor(
                out=wd_b[:, 0:w], in0=lut_v[:, 0:w], in1=a_repf[:, 0:w],
                op=ALU.mult),
                deps=[t_la, t_acc] + ([last["st"]] if last.get("st") else []),
                inc=1)
            pv["arep"] = t_mul
            last["vec"] = t_mul
            if tiled:
                for hti in range(w // 128):
                    ht = (wh + hti * 128) // 128
                    t_st = S.dma("gpsimd",
                                 w1d_3(out_dram)[:, ht, a, :],
                                 wd_b[:, hti * 128:(hti + 1) * 128],
                                 deps=[t_mul], ch=5)
            else:
                t_st = S.dma("gpsimd",
                             out_dram[a * 128:(a + 1) * 128, wsl],
                             wd_b[:, 0:w], deps=[t_mul], ch=5)
            last["st"] = t_st
        return last["st"]

    units = ([(w1c, a1X, w1d, a, HSL, True) for a in range(NK)]
             + [(w3c, a3X, w3d, a, HSL, True) for a in range(NK)]
             + [(w2c, a2X, w2d, a, D, False) for a in range(NHT)])
    state = {"i": 0}

    def emit_units(k):
        last_t = None
        while k > 0 and state["i"] < len(units):
            last_t = emit_tile(*units[state["i"]])
            state["i"] += 1
            k -= 1
        return last_t

    def done_tokens():
        return [t for t in last.values() if t is not None]

    return emit_units, len(units), done_tokens


def phase_ffn1(nc, S, stk, IO):
    To = OWN_T
    NK = D // 128
    NHT = HSL // 128
    NTC = T // 512
    xg = IO.get('xg')
    w1c = IO.get('w1c')
    a1S = IO.get('a1S')
    w3c = IO.get('w3c')
    a3S = IO.get('a3S')
    w2c = IO.get('w2c')
    a2S = IO.get('a2S')
    x3T = IO.get('x3T')
    x4 = IO.get('x4')
    w1d = IO.get('w1d')
    w3d = IO.get('w3d')
    w2d = IO.get('w2d')
    pb = IO.get('pb')
    rs_out = IO.get('rs_out')
    xga = IO.get('xga')
    tok_pre = IO.get('tok_pre', [])
    def w1d_3(t):
        return t.rearrange("p (ht a hh) -> p ht a hh", ht=NHT, a=NK)

    _pfx = IO.get("pfx", "")
    sb = lambda name, shape, dt: stk.enter_context(
        nc.sbuf_tensor(_pfx + name, shape, dt))
    psu = lambda name, shape, dt=F32: stk.enter_context(
        nc.psum_tensor(_pfx + name, shape, dt))

    WQC = 2048
    code_i = sb("code_i", [128, WQC], I32)
    code_h = sb("code_h", [128, WQC], mybir.dt.float16)
    lut_v = sb("lut_v", [128, WQC], mybir.dt.float16)
    lut_t = sb("lut_t", [128, WQC], mybir.dt.float16)
    a_rep = sb("a_rep", [128, WQC], mybir.dt.float16)
    a_rf = sb("a_rf", [128, 512], F32)
    wd_b = sb("wd_b", [128, WQC], BF16)
    arow = sb("arow", [1, WQC], F32)
    xtc_a = sb("xtc_a", [128, NK * 512], BF16)
    xtc_b = sb("xtc_b", [128, NK * 512], BF16)
    hbuf_a = sb("hbuf_a", [128, NHT * 512], BF16)
    hbuf_b = sb("hbuf_b", [128, NHT * 512], BF16)
    w1s_a = sb("w1s_a", [128, NK * 128], BF16)
    w1s_b = sb("w1s_b", [128, NK * 128], BF16)
    w3s_a = sb("w3s_a", [128, NK * 128], BF16)
    w3s_b = sb("w3s_b", [128, NK * 128], BF16)
    w2t_a = sb("w2t_a", [128, 512], BF16)
    w2t_b = sb("w2t_b", [128, 512], BF16)
    sil = sb("sil", [128, 512], F32)
    hpt_a = sb("hpt_a", [128, 512], F32)
    hpt_b = sb("hpt_b", [128, 512], F32)
    ones1r = sb("ones1r", [1, 128], F32)
    idn_f = sb("idn_f", [128, 128], F32)
    otile = sb("otile", [128, D], F32)
    rst = sb("rst", [128, D], F32)
    x3blk = sb("x3blk", [128, 128], F32)

    ps_1 = psu("ps_1", [128, 512])
    ps_3 = psu("ps_3", [128, 512])
    ps_t = [psu(f"ps_t{i}", [128, 512]) for i in range(4)]
    ps_e = psu("ps_e", [128, 512])
    ps_f = psu("ps_f", [128, 128])


    t_ones = S.op("vector", lambda e: e.memset(ones1r[:], 1.0))
    t_id0 = S.op("gpsimd", lambda e: e.memset(idn_f[:], 0.0), inc=1)
    t_idn = S.op("gpsimd", lambda e: e.affine_select(
        out=idn_f[:], in_=idn_f[:], compare_op=ALU.not_equal, fill=1.0,
        base=0, pattern=[[-1, 128]], channel_multiplier=1),
        deps=[t_id0], inc=1)

    pv = {}

    def emit_nf4_lut(width, dep_load):
        t_cf = S.op("vector", lambda e: e.tensor_copy(
            out=code_h[:, 0:width], in_=code_i[:, 0:width]), deps=[dep_load])
        t_acc = S.op("vector", lambda e: e.tensor_scalar(
            out=lut_v[:, 0:width], in0=code_h[:, 0:width], scalar1=0.0,
            scalar2=float(NF4[0]), op0=ALU.is_equal, op1=ALU.mult),
            deps=[t_cf])
        for i in range(1, 16):
            t_i = S.op("vector", lambda e, i=i: e.tensor_scalar(
                out=lut_t[:, 0:width], in0=code_h[:, 0:width],
                scalar1=float(i), scalar2=float(NF4[i]), op0=ALU.is_equal,
                op1=ALU.mult))
            t_acc = S.op("vector", lambda e: e.tensor_tensor(
                out=lut_v[:, 0:width], in0=lut_v[:, 0:width],
                in1=lut_t[:, 0:width], op=ALU.add))
        return t_acc

    def emit_arep(a_dram, row0, wofs, width, dep):
        toks = []
        for half in range(2):
            t_ar = S.dma("gpsimd", arow[:, 0:width],
                         a_dram[row0 + half:row0 + half + 1,
                                wofs:wofs + width],
                         deps=dep + (toks[-2:] if toks else []), ch=2)
            for q in range((width + 511) // 512):
                sl = slice(q * 512, min((q + 1) * 512, width))
                w = sl.stop - sl.start
                t_bm = S.op("tensor", lambda e, sl=sl, w=w: e.matmul(
                    ps_e[0:64, 0:w], ones1r[0:1, 0:64], arow[0:1, sl],
                    start=True, stop=True),
                    deps=[t_ar, t_ones] + (toks[-1:] if toks else [])
                    + ([pv["ps_e"]] if pv.get("ps_e") else []))
                t_e1 = S.op("vector", lambda e, w=w: e.tensor_copy(
                    out=a_rf[0:64, 0:w], in_=ps_e[0:64, 0:w]), deps=[t_bm])
                t_ev = S.op("vector", lambda e, sl=sl, half=half, w=w:
                            e.tensor_copy(
                                out=a_rep[half * 64:(half + 1) * 64, sl],
                                in_=a_rf[0:64, 0:w]), deps=[t_e1])
                pv["ps_e"] = t_e1
                toks.append(t_bm)
                toks.append(t_ev)
        return toks[-1]

    def emit_dequant_tile(codes, a_dram, out_dram, a, width, tiled):
        last_st = pv.get("wst")
        for wh in range(0, width, WQC):
            wsl = slice(wh, min(wh + WQC, width))
            w = wsl.stop - wsl.start
            t_lc = S.dma("sync", code_i[:, 0:w],
                         codes[a * 128:(a + 1) * 128, wsl],
                         deps=[pv["code"]] if pv.get("code") else [], ch=0)
            t_lv = emit_nf4_lut(w, t_lc)
            pv["code"] = t_lv
            t_arep = emit_arep(a_dram, 2 * a, wh, w,
                               [pv["arep"]] if pv.get("arep") else [])
            t_mul = S.op("vector", lambda e, w=w: e.tensor_tensor(
                out=lut_v[:, 0:w], in0=lut_v[:, 0:w],
                in1=a_rep[:, 0:w], op=ALU.mult), deps=[t_arep, t_lv])
            pv["arep"] = t_mul
            t_cb = S.op("vector", lambda e, w=w: e.tensor_copy(
                out=wd_b[:, 0:w], in_=lut_v[:, 0:w]),
                deps=([last_st] if last_st else []))
            if tiled:
                for hti in range(w // 128):
                    ht = (wh + hti * 128) // 128
                    t_st = S.dma("gpsimd",
                                 w1d_3(out_dram)[:, ht, a, :],
                                 wd_b[:, hti * 128:(hti + 1) * 128],
                                 deps=[t_cb], ch=0)
            else:
                t_st = S.dma("gpsimd",
                             out_dram[a * 128:(a + 1) * 128, wsl],
                             wd_b[:, 0:w], deps=[t_cb], ch=0)
            last_st = t_st
            pv["wst"] = t_st
        return last_st

    def emit_dequant(codes, a_dram, out_dram, ntile, width, tiled):
        last = None
        for a in range(ntile):
            last = emit_dequant_tile(codes, a_dram, out_dram, a, width, tiled)
        return last

    if IO.get("dq_external"):
        tok_deq = list(IO.get("tok_pre", []))
    else:
        t_d1 = emit_dequant(w1c, a1S, w1d, NK, HSL, True)
        t_d3 = emit_dequant(w3c, a3S, w3d, NK, HSL, True)
        t_d2 = emit_dequant(w2c, a2S, w2d, NHT, D, False)
        tok_deq = [t_d1, t_d3, t_d2]

    pm = {}
    t_pb_last = None
    for tc in range(NTC):
        xtc = xtc_a if tc % 2 == 0 else xtc_b
        hbuf = hbuf_a if tc % 2 == 0 else hbuf_b
        tsl = slice(tc * 512, (tc + 1) * 512)
        xw = [pm[f"xr{tc % 2}"]] if pm.get(f"xr{tc % 2}") else []
        t_x = None
        t_xs = []
        if xga is None:
            for a2 in range(2):
                t_xs.append(S.dma("sync",
                            xtc[:].rearrange("p (a t) -> p a t", a=NK)
                            [:, a2 * 16:(a2 + 1) * 16, :],
                            xg.rearrange("(a p) t -> p a t", p=128)
                            [:, a2 * 16:(a2 + 1) * 16, tsl],
                            deps=xw, ch=1 + 7 * a2))
        else:
            cpr = To // 512
            lsl = slice((tc % cpr) * 512, (tc % cpr) * 512 + 512)
            for a2 in range(2):
                t_xs.append(S.dma("sync",
                            xtc[:].rearrange("p (a t) -> p a t", a=NK)
                            [:, a2 * 16:(a2 + 1) * 16, :],
                            xga[tc // cpr].rearrange("(a p) t -> p a t", p=128)
                            [:, a2 * 16:(a2 + 1) * 16, lsl],
                            deps=xw + list(tok_pre), ch=1 + 7 * a2))
        t_x = t_xs
        for ht in range(NHT):
            w1b = w1s_a if ht % 2 == 0 else w1s_b
            w3b = w3s_a if ht % 2 == 0 else w3s_b
            war1 = [pm[f"mm1{ht % 2}"]] if pm.get(f"mm1{ht % 2}") else tok_deq
            t_w1 = S.dma("sync", w1b[:], w1d[:, ht * NK * 128:(ht + 1) * NK * 128],
                         deps=war1, ch=2 + ht % 2)
            t_w3 = S.dma("sync", w3b[:], w3d[:, ht * NK * 128:(ht + 1) * NK * 128],
                         deps=war1, ch=6 + ht % 2)
            tm1 = tm3 = None
            for a in range(NK):
                tm1 = S.op("tensor", lambda e, a=a, w=w1b, xtc=xtc: e.matmul(
                    ps_1[:], w[:, a * 128:(a + 1) * 128],
                    xtc[:, a * 512:(a + 1) * 512],
                    start=(a == 0), stop=(a == NK - 1)),
                    deps=[t_w1] + t_x + ([pm["ev1"]] if pm.get("ev1") else []))
            for a in range(NK):
                tm3 = S.op("tensor", lambda e, a=a, w=w3b, xtc=xtc: e.matmul(
                    ps_3[:], w[:, a * 128:(a + 1) * 128],
                    xtc[:, a * 512:(a + 1) * 512],
                    start=(a == 0), stop=(a == NK - 1)),
                    deps=[t_w3] + t_x + ([pm["ev3"]] if pm.get("ev3") else []))
            pm[f"mm1{ht % 2}"] = tm3
            t_si = S.op("scalar", lambda e: e.activation(
                out=sil[:], in_=ps_1[:], func=AF.Silu),
                deps=[tm1] + ([pm["sil"]] if pm.get("sil") else []))
            pm["ev1"] = t_si
            t_h = S.op("vector", lambda e, ht=ht, hbuf=hbuf: e.tensor_tensor(
                out=hbuf[:, ht * 512:(ht + 1) * 512], in0=sil[:],
                in1=ps_3[:], op=ALU.mult), deps=[t_si, tm3]
                + ([pm[f"hb{tc % 2}"]] if pm.get(f"hb{tc % 2}") else []))
            pm["ev3"] = t_h
            pm["sil"] = t_h
            pm["hlast"] = t_h
        for dc in range(D // 512):
            dsl = slice(dc * 512, (dc + 1) * 512)
            tmm = None
            for ht in range(NHT):
                wb2 = w2t_a if ht % 2 == 0 else w2t_b
                t_w2 = S.dma("sync", wb2[:],
                             w2d[ht * 128:(ht + 1) * 128, dsl],
                             deps=[pm[f"mm2{ht % 2}"]]
                             if pm.get(f"mm2{ht % 2}") else tok_deq,
                             ch=4 + ht % 2)
                for ts4 in range(4):
                    tmm = S.op("tensor", lambda e, ht=ht, ts4=ts4, w=wb2,
                               hbuf=hbuf:
                               e.matmul(ps_t[ts4][:],
                                        hbuf[:, ht * 512 + ts4 * 128:
                                             ht * 512 + (ts4 + 1) * 128],
                                        w[:], start=(ht == 0),
                                        stop=(ht == NHT - 1)),
                               deps=[t_w2, pm["hlast"]]
                               + ([pm[f"pt{ts4}"]] if pm.get(f"pt{ts4}")
                                  else []))
                pm[f"mm2{ht % 2}"] = tmm
            for ts4 in range(4):
                hb = hpt_a if ts4 % 2 == 0 else hpt_b
                t_ev = S.op("vector", lambda e, ts4=ts4, hb=hb: e.tensor_copy(
                    out=hb[:], in_=ps_t[ts4][:]), deps=[tmm]
                    + ([pm[f"hpt{ts4 % 2}"]] if pm.get(f"hpt{ts4 % 2}")
                       else []))
                t_sp = S.dma("gpsimd",
                             pb[tc * 512 + ts4 * 128:
                                tc * 512 + (ts4 + 1) * 128, dsl],
                             hb[:], deps=[t_ev], ch=1 + 4 * (ts4 % 2))
                pm[f"pt{ts4}"] = t_ev
                pm[f"hpt{ts4 % 2}"] = t_sp
                pm[f"pbl{ts4 % 2}"] = t_sp
                t_pb_last = t_sp
        pm[f"xr{tc % 2}"] = tmm
        pm[f"hb{tc % 2}"] = tmm

    t_rs = S.cc("ReduceScatter", ALU.add, [pb.opt()], [rs_out.opt()],
                deps=[pm["pbl0"], pm["pbl1"]])

    pvf = {}
    for tt in range(To // 128):
        t_ev = None
        for a in range(NK):
            t_lx = S.dma("sync", x3blk[:],
                         x3T[a * 128:(a + 1) * 128, tt * 128:(tt + 1) * 128],
                         deps=[pvf["tp"]] if pvf.get("tp") else [], ch=0)
            t_tp = S.op("tensor", lambda e: e.transpose(
                ps_f[:], x3blk[:], idn_f[:]),
                deps=[t_lx, t_idn] + ([pvf["tev"]] if pvf.get("tev") else []))
            pvf["tp"] = t_tp
            t_ev = S.op("vector", lambda e, a=a: e.tensor_copy(
                out=otile[:, a * 128:(a + 1) * 128], in_=ps_f[:]),
                deps=[t_tp] + ([pvf["ost"]] if a == 0 and pvf.get("ost")
                               else []))
            pvf["tev"] = t_ev
        t_lr = S.dma("sync", rst[:],
                     rs_out.rearrange("(b p) d -> b p d", p=128)[tt],
                     deps=[t_rs] + ([pvf["rs"]] if pvf.get("rs") else []),
                     ch=2)
        t_ad = S.op("vector", lambda e: e.tensor_tensor(
            out=otile[:], in0=otile[:], in1=rst[:], op=ALU.add),
            deps=[t_ev, t_lr])
        pvf["rs"] = t_ad
        t_st = S.dma("gpsimd",
                     x4.rearrange("(b p) d -> b p d", p=128)[tt],
                     otile[:], deps=[t_ad], ch=3)
        pvf["ost"] = t_st
    S.op("gpsimd", lambda e: e.engine_nop(), deps=[pvf["ost"]], inc=1)

    return {'x4': pvf['ost']}


def build_ffn1():
    nc = bass.Bass("TRN2", target_bir_lowering=False)
    To = OWN_T
    NK = D // 128
    NHT = HSL // 128
    NTC = T // 512

    xg = nc.dram_tensor("xg", [D, T], BF16, kind="ExternalInput").ap()
    w1c = nc.dram_tensor("w1c", [D, HSL], I32, kind="ExternalInput").ap()
    a1S = nc.dram_tensor("a1S", [64, HSL], F32, kind="ExternalInput").ap()
    w3c = nc.dram_tensor("w3c", [D, HSL], I32, kind="ExternalInput").ap()
    a3S = nc.dram_tensor("a3S", [64, HSL], F32, kind="ExternalInput").ap()
    w2c = nc.dram_tensor("w2c", [HSL, D], I32, kind="ExternalInput").ap()
    a2S = nc.dram_tensor("a2S", [22, D], F32, kind="ExternalInput").ap()
    x3T = nc.dram_tensor("x3T", [D, To], F32, kind="ExternalInput").ap()

    x4 = nc.dram_tensor("x4", [To, D], F32, kind="ExternalOutput").ap()

    w1d = nc.dram_tensor("w1d", [128, NHT * NK * 128], BF16, kind="Internal").ap()
    w3d = nc.dram_tensor("w3d", [128, NHT * NK * 128], BF16, kind="Internal").ap()
    w2d = nc.dram_tensor("w2d", [HSL, D], BF16, kind="Internal").ap()
    pb = nc.dram_tensor("pb", [T, D], F32, kind="Internal").ap()
    rs_out = nc.dram_tensor("rs_out", [To, D], F32, kind="Internal").ap()

    with contextlib.ExitStack() as stk:
        S = Sched(nc, stk)
        IO = dict(xg=xg, w1c=w1c, a1S=a1S, w3c=w3c, a3S=a3S, w2c=w2c, a2S=a2S,
                  x3T=x3T, x4=x4, w1d=w1d, w3d=w3d, w2d=w2d, pb=pb,
                  rs_out=rs_out)
        with contextlib.ExitStack() as pstk:
            phase_ffn1(nc, S, pstk, IO)
        S.emit(block=stk.enter_context(nc.Block()))
    return nc


def build_fused():
    nc = bass.Bass("TRN2", target_bir_lowering=False)
    To = OWN_T
    TS = TS_M

    inp = lambda n, shp, dt: nc.dram_tensor(n, shp, dt, kind="ExternalInput").ap()
    itn = lambda n, shp, dt: nc.dram_tensor(n, shp, dt, kind="Internal").ap()

    x0d = inp("x0d", [T, DSL], F32)
    x0dT = inp("x0dT", [DSL, T], F32)
    alog = inp("alog", [128, NDT * NST], F32)
    dvec = inp("dvec", [128, NDT], F32)
    xpwS = inp("xpwS", [128, NDT * 160], F32)
    dtwT = inp("dtwT", [DTR, DSL], F32)
    dtb = inp("dtb", [128, NDT], F32)
    wqT = inp("wqT", [D, D], I32)
    scaleT = inp("scaleT", [32, D], F32)
    zeroT = inp("zeroT", [32, D], F32)
    bias0 = inp("bias0", [1, D], F32)
    wkvT = inp("wkvT", [D, 3 * PART], F32)
    bkv_c = inp("bkv_c", [128, 48], F32)
    x1idx = inp("x1idx", [128, D // 128], I32)
    woT = inp("woT", [PART, D], F32)
    bo_c = inp("bo_c", [128, 32], F32)
    qpos = inp("qpos", [128, To // 128], F32)
    w1c = inp("w1c", [D, HSL], I32)
    a1X = inp("a1X", [D, HSL], F32)
    w3c = inp("w3c", [D, HSL], I32)
    a3X = inp("a3X", [D, HSL], F32)
    w2c = inp("w2c", [HSL, D], I32)
    a2X = inp("a2X", [HSL, D], F32)

    x4 = nc.dram_tensor("x4", [To, D], F32, kind="ExternalOutput").ap()

    st_in = itn("st_in", [3, T], F32)
    st_out = itn("st_out", [3, T], F32)
    st_scr = itn("st_scr", [2, T], F32)
    xp_ins = [itn(f"xp_in{i}", [160, TS], F32) for i in range(T // TS)]
    xp_outs = [itn(f"xp_out{i}", [160, TS], F32) for i in range(T // TS)]
    x1d = itn("x1d", [DSL, T], F32)
    x1g = itn("x1g", [D, T], F32)
    wdeq = itn("wdeq", [128, 32 * 32 * 128], BF16)
    wkvb = itn("wkvb", [128, 48 * 32 * 128], BF16)
    b0scr = itn("b0scr", [1, D], BF16)
    x2Ti = itn("x2Ti", [D, To], F32)
    qdi = itn("qdi", [PART, To], BF16)
    kdi = itn("kdi", [PART, To], BF16)
    vti = itn("vti", [To, PART], BF16)
    kga = itn("kga", [N_CORES, PART, To], BF16)
    vga = itn("vga", [N_CORES, To, PART], BF16)
    x3scr = itn("x3scr", [D, To], F32)
    xn3i = itn("xn3i", [D, To], BF16)
    xga = itn("xga", [N_CORES, D, To], BF16)
    w1d = itn("w1d", [128, (HSL // 128) * 32 * 128], BF16)
    w3d = itn("w3d", [128, (HSL // 128) * 32 * 128], BF16)
    w2d = itn("w2d", [HSL, D], BF16)
    pb = itn("pb", [T, D], F32)
    rs_out = itn("rs_out", [To, D], F32)

    with contextlib.ExitStack() as stk:
        S = Sched(nc, stk)

        IO1 = dict(pfx="pA_", x0d=x0d, x0dT=x0dT, alog=alog, dvec=dvec, xpwS=xpwS,
                   dtwT=dtwT, dtb=dtb, x1d=x1d, st_in=st_in, st_out=st_out,
                   st_scr=st_scr, xp_ins=xp_ins, xp_outs=xp_outs)
        with contextlib.ExitStack() as p1:
            r1 = phase_mamba(nc, S, p1, IO1)
        t_ag1 = S.cc("AllGather", ALU.bypass, [x1d.opt()], [x1g.opt()],
                     deps=r1["x1d_stores"])
        S.barrier()

        IO2 = dict(pfx="pB_", x1T=None, wqT=wqT, scaleT=scaleT, zeroT=zeroT, bias0=bias0,
                   wkvT=wkvT, bkv_c=bkv_c, x2T=x2Ti, qd=qdi, kd=kdi, vt=vti,
                   wdeq=wdeq, wkvb=wkvb, b0scr=b0scr,
                   x1idx=x1idx, x1v=x1g.rearrange("d (b t) -> (d b) t",
                                                  b=N_CORES),
                   tok_pre=[t_ag1])
        with contextlib.ExitStack() as p2:
            r2 = phase_ffn0(nc, S, p2, IO2)
        t_ag2 = S.cc("AllGather", ALU.bypass, [kdi.opt()], [kga.opt()],
                     deps=r2["kv"] + [r2["x2"]])
        t_ag3 = S.cc("AllGather", ALU.bypass, [vti.opt()], [vga.opt()],
                     deps=r2["kv"])
        S.barrier()

        IO3 = dict(pfx="pC_", qd=qdi, kg=None, vg=None, woT=woT, bo_c=bo_c, x2T=x2Ti,
                   qpos=qpos, x3T=None, xn3T=xn3i, x3scr=x3scr,
                   kga=kga, vga=vga, tok_pre=[t_ag2, t_ag3])
        IO3["dq_io"] = dict(w1c=w1c, a1X=a1X, w3c=w3c, a3X=a3X, w2c=w2c,
                            a2X=a2X, w1d=w1d, w3d=w3d, w2d=w2d)
        with contextlib.ExitStack() as p3:
            r3 = phase_attn(nc, S, p3, IO3)
        t_ag4 = S.cc("AllGather", ALU.bypass, [xn3i.opt()], [xga.opt()],
                     deps=[r3["xn3"]])
        S.barrier()

        IO4 = dict(pfx="pD_", xg=None, w1c=w1c, a1S=None, w3c=w3c, a3S=None,
                   w2c=w2c, a2S=None, x3T=x3scr, x4=x4, w1d=w1d, w3d=w3d,
                   w2d=w2d, pb=pb, rs_out=rs_out, xga=xga, tok_pre=[t_ag4],
                   dq_external=True)
        with contextlib.ExitStack() as p4:
            r4 = phase_ffn1(nc, S, p4, IO4)

        S.emit(block=stk.enter_context(nc.Block()))
    return nc


_NCS = {}


def _nc(name):
    if name not in _NCS:
        _NCS[name] = {"mamba": build_mamba, "ffn0": build_ffn0_qkv,
                      "attn": build_attn, "ffn1": build_ffn1,
                      "fused": build_fused}[name]()
    return _NCS[name]


def _run(name, in_maps):
    res = run_bass_kernel_spmd(_nc(name), in_maps, core_ids=list(range(N_CORES)))
    return res.results


def run_mamba(L0_attn_x, L0_attn_A_log, L0_attn_D, L0_attn_x_proj_w,
              L0_attn_dt_proj_w, L0_attn_dt_proj_b):
    x = np.ascontiguousarray(np.asarray(L0_attn_x).reshape(T, D))
    xT = np.ascontiguousarray(x.T)
    xpwT = np.ascontiguousarray(np.asarray(L0_attn_x_proj_w).T)
    dtwT = np.ascontiguousarray(np.asarray(L0_attn_dt_proj_w).T)
    A_log = np.asarray(L0_attn_A_log)
    Dv = np.asarray(L0_attn_D)
    dtb = np.asarray(L0_attn_dt_proj_b)
    in_maps = []
    for c in range(N_CORES):
        dsl = slice(DSL * c, DSL * (c + 1))
        in_maps.append({
            "x0d": np.ascontiguousarray(x[:, dsl]),
            "x0dT": np.ascontiguousarray(xT[dsl, :]),
            "alog": np.ascontiguousarray(
                A_log[dsl, :].reshape(NDT, 128, NST)
                .transpose(1, 0, 2).reshape(128, NDT * NST)),
            "dvec": np.ascontiguousarray(Dv[dsl].reshape(NDT, 128).T),
            "xpwS": np.ascontiguousarray(
                xpwT[dsl, :].reshape(NDT, 128, 160)
                .transpose(1, 0, 2).reshape(128, NDT * 160)),
            "dtwT": np.ascontiguousarray(dtwT[:, dsl]),
            "dtb": np.ascontiguousarray(dtb[dsl].reshape(NDT, 128).T),
        })
    res = _run("mamba", in_maps)
    return np.concatenate([r["x1d"] for r in res], axis=0)


def kernel_multi(**inputs):
    f32 = lambda a: np.ascontiguousarray(np.asarray(a, dtype=np.float32))

    x1 = run_mamba(inputs["L0_attn_x"], inputs["L0_attn_A_log"],
                   inputs["L0_attn_D"], inputs["L0_attn_x_proj_w"],
                   inputs["L0_attn_dt_proj_w"], inputs["L0_attn_dt_proj_b"])

    wqT = np.ascontiguousarray(np.asarray(inputs["L0_ffn_W_q"]).T)
    scaleT = f32(np.asarray(inputs["L0_ffn_scale"]).T)
    zeroT = f32(np.asarray(inputs["L0_ffn_zero"]).T)
    bias0 = f32(inputs["L0_ffn_bias"]).reshape(1, D)
    wkvT = np.ascontiguousarray(f32(inputs["L1_attn_w_qkv"]).T)
    bkv_c = np.ascontiguousarray(f32(inputs["L1_attn_b_qkv"]).reshape(48, 128).T)
    in_maps = []
    for c in range(N_CORES):
        tsl = slice(OWN_T * c, OWN_T * (c + 1))
        in_maps.append({
            "x1T": np.ascontiguousarray(x1[:, tsl]),
            "wqT": wqT, "scaleT": scaleT, "zeroT": zeroT, "bias0": bias0,
            "wkvT": wkvT, "bkv_c": bkv_c,
        })
    resB = _run("ffn0", in_maps)
    x2s = [r["x2T"] for r in resB]
    kg = np.concatenate([r["kd"] for r in resB], axis=1)
    vg = np.concatenate([r["vt"] for r in resB], axis=0)

    woT = np.ascontiguousarray(f32(inputs["L1_attn_w_o"]).T)
    bo_c = np.ascontiguousarray(f32(inputs["L1_attn_b_o"]).reshape(32, 128).T)
    in_maps = []
    for c in range(N_CORES):
        qpos = (c * OWN_T + np.arange(OWN_T).reshape(OWN_T // 128, 128).T
                ).astype(np.float32)
        in_maps.append({
            "qd": resB[c]["qd"], "kg": kg, "vg": vg, "woT": woT, "bo_c": bo_c,
            "x2T": x2s[c], "qpos": np.ascontiguousarray(qpos),
        })
    resC = _run("attn", in_maps)
    xg = np.concatenate([r["xn3T"] for r in resC], axis=1)

    HP = N_CORES * HSL
    w1T = np.asarray(inputs["L1_ffn_w1_codes"]).T
    w3T = np.asarray(inputs["L1_ffn_w3_codes"]).T
    w2T = np.asarray(inputs["L1_ffn_w2_codes"]).T
    w1Tp = np.full((D, HP), 7, np.int32); w1Tp[:, :HID] = w1T
    w3Tp = np.full((D, HP), 7, np.int32); w3Tp[:, :HID] = w3T
    w2Tp = np.full((HP, D), 7, np.int32); w2Tp[:HID, :] = w2T
    a1Tp = np.zeros((64, HP), np.float32)
    a1Tp[:, :HID] = np.asarray(inputs["L1_ffn_w1_absmax"]).T
    a3Tp = np.zeros((64, HP), np.float32)
    a3Tp[:, :HID] = np.asarray(inputs["L1_ffn_w3_absmax"]).T
    a2p = np.zeros((HP // 64, D), np.float32)
    a2p[:HID // 64, :] = np.asarray(inputs["L1_ffn_w2_absmax"]).T
    in_maps = []
    for c in range(N_CORES):
        hs = slice(c * HSL, (c + 1) * HSL)
        in_maps.append({
            "xg": xg,
            "w1c": np.ascontiguousarray(w1Tp[:, hs]),
            "a1S": np.ascontiguousarray(a1Tp[:, hs]),
            "w3c": np.ascontiguousarray(w3Tp[:, hs]),
            "a3S": np.ascontiguousarray(a3Tp[:, hs]),
            "w2c": np.ascontiguousarray(w2Tp[hs, :]),
            "a2S": np.ascontiguousarray(a2p[c * 22:(c + 1) * 22, :]),
            "x3T": resC[c]["x3T"],
        })
    resD = _run("ffn1", in_maps)
    x4 = np.concatenate([r["x4"] for r in resD], axis=0)
    return x4.reshape(1, T, D).astype(np.float32)


def _fused_in_maps(inputs):
    f32 = lambda a: np.ascontiguousarray(np.asarray(a, dtype=np.float32))
    x = np.ascontiguousarray(np.asarray(inputs["L0_attn_x"]).reshape(T, D))
    xT = np.ascontiguousarray(x.T)
    xpwT = np.ascontiguousarray(np.asarray(inputs["L0_attn_x_proj_w"]).T)
    dtwT = np.ascontiguousarray(np.asarray(inputs["L0_attn_dt_proj_w"]).T)
    A_log = np.asarray(inputs["L0_attn_A_log"])
    Dv = np.asarray(inputs["L0_attn_D"])
    dtb = np.asarray(inputs["L0_attn_dt_proj_b"])
    wqT = np.ascontiguousarray(np.asarray(inputs["L0_ffn_W_q"]).T)
    scaleT = f32(np.asarray(inputs["L0_ffn_scale"]).T)
    zeroT = f32(np.asarray(inputs["L0_ffn_zero"]).T)
    bias0 = f32(inputs["L0_ffn_bias"]).reshape(1, D)
    wkvT = np.ascontiguousarray(f32(inputs["L1_attn_w_qkv"]).T)
    bkv_c = np.ascontiguousarray(f32(inputs["L1_attn_b_qkv"]).reshape(48, 128).T)
    woT = np.ascontiguousarray(f32(inputs["L1_attn_w_o"]).T)
    bo_c = np.ascontiguousarray(f32(inputs["L1_attn_b_o"]).reshape(32, 128).T)
    HP = N_CORES * HSL
    w1T = np.asarray(inputs["L1_ffn_w1_codes"]).T
    w3T = np.asarray(inputs["L1_ffn_w3_codes"]).T
    w2T = np.asarray(inputs["L1_ffn_w2_codes"]).T
    w1Tp = np.full((D, HP), 7, np.int32); w1Tp[:, :HID] = w1T
    w3Tp = np.full((D, HP), 7, np.int32); w3Tp[:, :HID] = w3T
    w2Tp = np.full((HP, D), 7, np.int32); w2Tp[:HID, :] = w2T
    a1Tp = np.zeros((64, HP), np.float32)
    a1Tp[:, :HID] = np.asarray(inputs["L1_ffn_w1_absmax"]).T
    a3Tp = np.zeros((64, HP), np.float32)
    a3Tp[:, :HID] = np.asarray(inputs["L1_ffn_w3_absmax"]).T
    a2p = np.zeros((HP // 64, D), np.float32)
    a2p[:HID // 64, :] = np.asarray(inputs["L1_ffn_w2_absmax"]).T

    in_maps = []
    for c in range(N_CORES):
        dsl = slice(DSL * c, DSL * (c + 1))
        hs = slice(c * HSL, (c + 1) * HSL)
        p = np.arange(128)
        g = np.arange(D // 128)
        x1idx = ((g[None, :] * 128 + p[:, None]) * N_CORES + c).astype(np.int32)
        qpos = (c * OWN_T + np.arange(OWN_T).reshape(OWN_T // 128, 128).T
                ).astype(np.float32)
        in_maps.append({
            "x0d": np.ascontiguousarray(x[:, dsl]),
            "x0dT": np.ascontiguousarray(xT[dsl, :]),
            "alog": np.ascontiguousarray(
                A_log[dsl, :].reshape(NDT, 128, NST)
                .transpose(1, 0, 2).reshape(128, NDT * NST)),
            "dvec": np.ascontiguousarray(Dv[dsl].reshape(NDT, 128).T),
            "xpwS": np.ascontiguousarray(
                xpwT[dsl, :].reshape(NDT, 128, 160)
                .transpose(1, 0, 2).reshape(128, NDT * 160)),
            "dtwT": np.ascontiguousarray(dtwT[:, dsl]),
            "dtb": np.ascontiguousarray(dtb[dsl].reshape(NDT, 128).T),
            "wqT": wqT, "scaleT": scaleT, "zeroT": zeroT, "bias0": bias0,
            "wkvT": wkvT, "bkv_c": bkv_c,
            "x1idx": np.ascontiguousarray(x1idx),
            "woT": woT, "bo_c": bo_c, "qpos": np.ascontiguousarray(qpos),
            "w1c": np.ascontiguousarray(w1Tp[:, hs]),
            "a1X": np.repeat(np.ascontiguousarray(a1Tp[:, hs]), 64, axis=0),
            "w3c": np.ascontiguousarray(w3Tp[:, hs]),
            "a3X": np.repeat(np.ascontiguousarray(a3Tp[:, hs]), 64, axis=0),
            "w2c": np.ascontiguousarray(w2Tp[hs, :]),
            "a2X": np.repeat(
                np.ascontiguousarray(a2p[c * 22:(c + 1) * 22, :]), 64, axis=0),
        })
    return in_maps


def kernel(**inputs):
    in_maps = _fused_in_maps(inputs)
    res = _run("fused", in_maps)
    x4 = np.concatenate([r["x4"] for r in res.results], axis=0) \
        if hasattr(res, "results") else np.concatenate(
            [r["x4"] for r in res], axis=0)
    return x4.reshape(1, T, D).astype(np.float32)


# version 08
# speedup vs baseline: 1.4403x; 1.4403x over previous
import contextlib
import numpy as np
import ml_dtypes
import concourse.bass as bass
import concourse.mybir as mybir
from concourse.bass_utils import run_bass_kernel_spmd

F32 = mybir.dt.float32
BF16 = mybir.dt.bfloat16
I32 = mybir.dt.int32
AF = mybir.ActivationFunctionType
ALU = mybir.AluOpType
AX = mybir.AxisListType
NP_BF16 = ml_dtypes.bfloat16

N_CORES = 8
RG = [list(range(N_CORES))]
D = 4096
T = 8192
NST = 16
DTR = 128
DSL = D // N_CORES
HEADS = 8
HDIM = 256
PART = 2048
HID = 11008
OWN_T = T // N_CORES
LN_EPS = 1e-5

NF4 = [-1.0, -0.6961928009986877, -0.5250730514526367, -0.39491748809814453,
       -0.28444138169288635, -0.18477343022823334, -0.09105003625154495, 0.0,
       0.07958029955625534, 0.16093020141124725, 0.24611230194568634,
       0.33791524171829224, 0.44070982933044434, 0.5626170039176941,
       0.6989939212799072, 1.0]


class Sched:
    ENGS = ("sync", "gpsimd", "vector", "scalar", "tensor")

    def __init__(self, nc, stack):
        self.nc = nc
        self.stack = stack
        self.ops = {e: [] for e in self.ENGS}
        self.sem = {e: stack.enter_context(nc.semaphore(f"sem_{e}"))
                    for e in self.ENGS}
        self.cnt = {e: 0 for e in self.ENGS}
        self.dsem = {}
        self.dcnt = {}
        self.ccsem = stack.enter_context(nc.semaphore("sem_cc"))
        self.cccnt = 0

    extra = ()

    def op(self, eng, fn, deps=(), inc=None, tag=""):
        if inc is None:
            inc = 16 if eng in ("sync", "gpsimd") else 1
        self.cnt[eng] += inc
        tok = (self.sem[eng], self.cnt[eng])
        self.ops[eng].append((fn, tuple(deps) + tuple(self.extra), tok, inc, tag))
        return tok

    def dma(self, eng, out, in_, deps=(), tag="", ch=0):
        key = (eng, ch)
        if key not in self.dsem:
            self.dsem[key] = self.stack.enter_context(
                self.nc.semaphore(f"dma_{eng}{ch}"))
            self.dcnt[key] = 0
        self.dcnt[key] += 16
        tok = (self.dsem[key], self.dcnt[key])
        self.ops[eng].append(
            (lambda e: e.dma_start(out=out, in_=in_),
             tuple(deps) + tuple(self.extra), tok, 16, tag))
        return tok

    def idma(self, out, in_view, idx_ap, deps=(), tag="", ch=0, element_offset=0):
        import bass_rust as _br
        key = ("gpsimd", ch)
        if key not in self.dsem:
            self.dsem[key] = self.stack.enter_context(
                self.nc.semaphore(f"dma_gpsimd{ch}"))
            self.dcnt[key] = 0
        self.dcnt[key] += 16
        tok = (self.dsem[key], self.dcnt[key])

        def fn(e, out=out, in_view=in_view, idx_ap=idx_ap, eo=element_offset):
            return e.indirect_dma_start(
                out=out, out_offset=None, in_=in_view,
                in_offset=_br.IndirectOffsetOnAxis(idx_ap, 0),
                element_offset=eo)
        self.ops["gpsimd"].append((fn, tuple(deps) + tuple(self.extra), tok,
                                   16, tag))
        return tok

    def barrier(self):
        toks = []
        for e in self.ENGS:
            if self.cnt[e]:
                toks.append((self.sem[e], self.cnt[e]))
        for k, sem in self.dsem.items():
            toks.append((sem, self.dcnt[k]))
        if self.cccnt:
            toks.append((self.ccsem, self.cccnt))
        nops = []
        for e in ("gpsimd", "vector", "scalar", "tensor"):
            nops.append(self.op(e, lambda eng: eng.nop(), deps=toks, inc=1))
        self.extra = tuple(nops)
        return nops

    def cc(self, kind, alu, ins, outs, deps=(), tag=""):
        self.cccnt += 1
        tok = (self.ccsem, self.cccnt)

        def fn(e):
            return e.collective_compute(kind, alu, replica_groups=RG,
                                        ins=ins, outs=outs)
        self.ops["gpsimd"].append((fn, tuple(deps) + tuple(self.extra), tok,
                                   None, tag))
        return tok

    def emit(self, block):
        ccsem = self.ccsem
        for eng_name in self.ENGS:
            ops = self.ops[eng_name]
            if not ops:
                continue

            self_serial = eng_name in ("vector", "scalar", "gpsimd")
            esem = self.sem[eng_name]

            def body(eng, ops=ops, self_serial=self_serial, esem=esem):
                waited = {}
                last_compute = None
                for fn, deps, tok, inc, tag in ops:
                    is_dma = tok[0] in self.dsem.values()
                    deps = list(deps)
                    if self_serial and not is_dma and last_compute is not None:
                        deps.append(last_compute)
                    best = {}
                    for (sem, val) in deps:
                        k = id(sem)
                        if best.get(k, (None, -1))[1] < val:
                            best[k] = (sem, val)
                    for (sem, val) in best.values():
                        key = id(sem)
                        if waited.get(key, -1) < val:
                            eng.wait_ge(sem, val)
                            waited[key] = val
                    inst = fn(eng)
                    if isinstance(inst, (list, tuple)):
                        inst = inst[-1]
                    if tok[0] is ccsem:
                        inst.then_inc(tok[0])
                    else:
                        inst.then_inc(tok[0], inc)
                    if not is_dma and tok[0] is esem:
                        last_compute = tok
            getattr(block, eng_name)(body)


def inner_bcast(ap, width):
    a = ap.ap
    assert a[-1][1] == 1, a
    return bass.AP(tensor=ap.tensor, offset=ap.offset,
                   ap=list(a[:-1]) + [[0, width]])


def mid_bcast(ap2d, n_mid):
    return bass.AP(tensor=ap2d.tensor, offset=ap2d.offset,
                   ap=[ap2d.ap[0], [0, n_mid], ap2d.ap[1]])


def emit_identity(S, idn):
    t0 = S.op("gpsimd", lambda e: e.memset(idn[:], 0.0), inc=1)
    return S.op("gpsimd", lambda e: e.affine_select(
        out=idn[:], in_=idn[:], compare_op=ALU.not_equal, fill=1.0,
        base=0, pattern=[[-1, idn.shape[1]]], channel_multiplier=1),
        deps=[t0], inc=1)


TS_M = 1024
NSUB_M = T // TS_M
NDT = DSL // 128


def phase_mamba(nc, S, stk, IO):
    TS = TS_M
    TB = T // 128
    NSUB = T // TS
    x0d = IO['x0d']
    x0dT = IO['x0dT']
    alog = IO['alog']
    dvec = IO['dvec']
    xpwS = IO['xpwS']
    dtwT = IO['dtwT']
    dtb = IO['dtb']
    x1d = IO['x1d']
    st_in = IO['st_in']
    st_out = IO['st_out']
    st_scr = IO['st_scr']
    xp_ins = IO['xp_ins']
    xp_outs = IO['xp_outs']
    _pfx = IO.get("pfx", "")
    sb = lambda name, shape, dt: stk.enter_context(
        nc.sbuf_tensor(_pfx + name, shape, dt))
    psu = lambda name, shape: stk.enter_context(
        nc.psum_tensor(_pfx + name, shape, F32))

    xps_f = sb("xps_f", [128, 4 * 160], F32)
    xps_b = sb("xps_b", [128, 4 * 160], BF16)
    dtw_b = sb("dtw_b", [128, NDT * 128], BF16)
    a4 = sb("a4", [128, NDT * NST], F32)
    d4 = sb("d4", [128, NDT], F32)
    dtb4 = sb("dtb4", [128, NDT], F32)
    onesc = sb("onesc", [128, 1], F32)
    ones1r = sb("ones1r", [1, 128], F32)
    state = sb("state", [128, NDT * NST], F32)
    idn = sb("idn", [128, 128], F32)
    stat_a = sb("stat_a", [128, TB], F32)
    stat_b = sb("stat_b", [128, TB], F32)
    stat_t = sb("stat_t", [TB, 128], F32)
    sa_t = sb("sa_t", [TB, 128], F32)
    sb_t = sb("sb_t", [TB, 128], F32)
    m64 = sb("m64", [TB, 128], F32)
    r64 = sb("r64", [TB, 128], F32)
    t64 = sb("t64", [TB, 128], F32)
    wsrow = sb("wsrow", [1, 160], F32)
    wcolA = sb("wcolA", [128, 1], F32)
    wcolB = sb("wcolB", [16, 1], F32)
    wcolC = sb("wcolC", [16, 1], F32)
    ceps = sb("ceps", [128, 1], F32)
    cone = sb("cone", [128, 1], F32)
    cneg1 = sb("cneg1", [128, 1], F32)
    mrow_s = sb("mrow_s", [1, TS], F32)
    rrow_s = sb("rrow_s", [1, TS], F32)
    bcrow = sb("bcrow", [1, TS], F32)

    x0t = sb("x0t", [128, NDT * TS], F32)
    x0b = sb("x0b", [128, NDT * 512], BF16)
    xln = sb("xln", [128, NDT * TS], F32)
    dt_ = sb("dt_", [128, NDT * TS], F32)
    u_ = sb("u_", [128, NDT * TS], F32)
    da = sb("da", [128, NDT * (TS + 1)], F32)
    dbx = sb("dbx", [128, NDT * (TS + 1)], F32)
    h_ = sb("h_", [128, NDT * (TS + 1)], F32)
    ch = sb("ch", [128, NDT * TS], F32)
    y_ = sb("y_", [128, NDT * TS], F32)
    brep = sb("brep", [128, TS], F32)
    crep = sb("crep", [128, TS], F32)
    mrep = sb("mrep", [128, TS], F32)
    rrep = sb("rrep", [128, TS], F32)
    xpr = sb("xpr", [128, TS], F32)
    bcr_b = sb("bcr_b", [16, TS], F32)
    bcr_c = sb("bcr_c", [16, TS], F32)
    dtin_b = sb("dtin_b", [128, TS], BF16)
    se1 = sb("se1", [128, 512], F32)
    se2 = sb("se2", [128, 512], F32)

    ps_m = psu("ps_m", [128, 512])
    ps_b = psu("ps_b", [16, 512])
    ps_c = psu("ps_c", [16, 512])
    ps_r = psu("ps_r", [1, 160])
    ps_t = psu("ps_t", [TB, 128])
    ps_e = psu("ps_e", [128, 512])


    def r3(buf, inner):
        return buf[:].rearrange("p (a t) -> p a t", a=NDT)

    x0t3 = lambda: r3(x0t, TS)
    xln3 = lambda: r3(xln, TS)
    dt3 = lambda: r3(dt_, TS)
    u3 = lambda: r3(u_, TS)
    ch3 = lambda: r3(ch, TS)
    da3 = lambda: da[:].rearrange("p (a t) -> p a t", a=NDT)
    dbx3 = lambda: dbx[:].rearrange("p (a t) -> p a t", a=NDT)
    h3 = lambda: h_[:].rearrange("p (a t) -> p a t", a=NDT)
    st3 = lambda: state[:].rearrange("p (a n) -> p a n", a=NDT)
    a43 = lambda: a4[:].rearrange("p (a n) -> p a n", a=NDT)
    d43 = lambda: d4[:].rearrange("p (a o) -> p a o", a=NDT)

    t_alog = S.dma("sync", a4[:], alog, ch=0)
    t_dvec = S.dma("sync", d4[:], dvec, ch=0)
    t_dtb = S.dma("sync", dtb4[:], dtb, ch=0)
    t_dtw = S.dma("sync", x0t[:, 0:NDT * 128], dtwT, ch=0)
    t_xps = S.dma("sync", xps_f[:], xpwS, ch=0)
    setup_g = [t_alog, t_dvec, t_dtb, t_dtw, t_xps]
    t_alog = t_dvec = t_dtb = t_dtw = t_xps = tuple(setup_g[-1])
    setup_g = [tuple(setup_g[-1])]
    t_alog, = setup_g; t_dvec, = setup_g; t_dtb, = setup_g
    t_dtw, = setup_g; t_xps, = setup_g

    t_a1 = S.op("scalar", lambda e: e.activation(out=a4[:], in_=a4[:], func=AF.Exp),
                deps=[t_alog])
    t_a2 = S.op("vector", lambda e: e.tensor_scalar(
        out=a4[:], in0=a4[:], scalar1=-1.0, scalar2=None, op0=ALU.mult),
        deps=[t_a1])
    t_dtwb = S.op("vector", lambda e: e.tensor_copy(
        out=dtw_b[:], in_=x0t[:, 0:NDT * 128]), deps=[t_dtw])
    t_xpsb = S.op("vector", lambda e: e.tensor_copy(out=xps_b[:], in_=xps_f[:]),
                  deps=[t_xps])
    t_ones = S.op("vector", lambda e: e.memset(onesc[:], 1.0))
    t_ceps = S.op("vector", lambda e: e.memset(ceps[:], LN_EPS))
    t_cone = S.op("vector", lambda e: e.memset(cone[:], 1.0))
    t_cneg1 = S.op("vector", lambda e: e.memset(cneg1[:], -1.0))
    t_ones1r = S.op("vector", lambda e: e.memset(ones1r[:], 1.0))
    t_idn = emit_identity(S, idn)
    t_st0 = S.op("vector", lambda e: e.memset(state[:], 0.0))
    t_da0 = S.op("vector", lambda e: e.memset(da[:], 0.0))
    t_dbx0 = S.op("vector", lambda e: e.memset(dbx[:], 0.0))
    t_y0 = S.op("vector", lambda e: e.memset(y_[:], 0.0))

    last_use = {0: None, 1: None}
    all_stats = []
    x0sA = brep[:, 0:512]
    x0sB = crep[:, 0:512]
    for j in range(TB):
        buf = x0sA if j % 2 == 0 else x0sB
        sebuf = se1 if j % 2 == 0 else se2
        pu = last_use[j % 2]
        t_l = S.dma("sync", buf, x0d[j * 128:(j + 1) * 128, :],
                    deps=[pu] if pu else [], ch=1 + (j % 2))
        t_r = S.op("vector", lambda e, b=buf, j=j: e.reduce_sum(
            out=stat_a[:, j:j + 1], in_=b, axis=AX.X), deps=[t_l])
        t_s = S.op("scalar", lambda e, b=buf, o=sebuf: e.activation(
            out=o[:], in_=b, func=AF.Square), deps=[t_l])
        t_q = S.op("vector", lambda e, o=sebuf, j=j: e.reduce_sum(
            out=stat_b[:, j:j + 1], in_=o[:], axis=AX.X), deps=[t_s])
        last_use[j % 2] = t_q
        all_stats += [t_r, t_q]
    TBP = TB

    t_z2 = S.op("vector", lambda e: e.memset(stat_t[:], 0.0))
    t_d2z = S.dma("gpsimd", st_in.rearrange("r (p f) -> (r p) f", p=TB)[2 * TB:3 * TB, :],
                  stat_t[:], deps=[t_z2])
    t_ws = None
    for a in range(4):
        t_ws = S.op("tensor", lambda e, a=a: e.matmul(
            ps_r[:], onesc[:], xps_f[:, a * 160:(a + 1) * 160],
            start=(a == 0), stop=(a == 3)), deps=[t_ones, t_xps])
    t_wse = S.op("vector", lambda e: e.tensor_copy(out=wsrow[:], in_=ps_r[:]),
                 deps=[t_ws])
    t_dws = S.dma("gpsimd", st_in[2:3, 0:160], wsrow[:], deps=[t_wse, t_d2z])

    t_tp1 = S.op("tensor", lambda e: e.transpose(ps_t[:], stat_a[:], idn[:]),
                 deps=all_stats + [t_idn, t_ws])
    t_ev1 = S.op("vector", lambda e: e.tensor_copy(out=stat_t[:], in_=ps_t[:]),
                 deps=[t_tp1, t_dws])
    t_d1 = S.dma("gpsimd", st_in.rearrange("r (p f) -> (r p) f", p=TB)[0:TB, :],
                 stat_t[:], deps=[t_ev1])
    t_tp2 = S.op("tensor", lambda e: e.transpose(ps_t[:], stat_b[:], idn[:]),
                 deps=[t_ev1])
    t_ev2 = S.op("vector", lambda e: e.tensor_copy(out=sa_t[:], in_=ps_t[:]),
                 deps=[t_tp2])
    t_d2 = S.dma("gpsimd", st_in.rearrange("r (p f) -> (r p) f", p=TB)[TB:2 * TB, :],
                 sa_t[:], deps=[t_ev2])

    t_ccst = S.cc("AllReduce", ALU.add, [st_in.opt()], [st_out.opt()],
                  deps=[t_d1, t_d2, t_dws])

    t_la = S.dma("sync", sa_t[:], st_out.rearrange("r (p f) -> (r p) f", p=TB)[0:TB, :],
                 deps=[t_ccst, t_d2])
    t_lb = S.dma("sync", sb_t[:], st_out.rearrange("r (p f) -> (r p) f", p=TB)[TB:2 * TB, :],
                 deps=[t_ccst])
    t_wa = S.dma("sync", wcolA[:], st_out[2:3, 0:128].rearrange("o r -> r o"),
                 deps=[t_ccst])
    t_wb = S.dma("sync", wcolB[:], st_out[2:3, 128:144].rearrange("o r -> r o"),
                 deps=[t_ccst])
    t_wc = S.dma("sync", wcolC[:], st_out[2:3, 144:160].rearrange("o r -> r o"),
                 deps=[t_ccst])
    t_la = t_lb = t_wa = t_wb = t_wc = t_wc
    t_m1 = S.op("vector", lambda e: e.tensor_scalar(
        out=m64[:], in0=sa_t[:], scalar1=1.0 / D, scalar2=None, op0=ALU.mult),
        deps=[t_la])
    t_m2 = S.op("vector", lambda e: e.tensor_scalar(
        out=r64[:], in0=sb_t[:], scalar1=1.0 / D, scalar2=None, op0=ALU.mult),
        deps=[t_lb])
    t_m3 = S.op("vector", lambda e: e.tensor_tensor(
        out=t64[:], in0=m64[:], in1=m64[:], op=ALU.mult), deps=[t_m1])
    t_m4 = S.op("vector", lambda e: e.tensor_tensor(
        out=r64[:], in0=r64[:], in1=t64[:], op=ALU.subtract), deps=[t_m2, t_m3])
    t_m5 = S.op("scalar", lambda e: e.activation(
        out=t64[:], in_=r64[:], func=AF.Sqrt, bias=ceps[0:TB, 0:1]),
        deps=[t_m4, t_ceps])
    t_m6 = S.op("vector", lambda e: e.reciprocal(out=r64[:], in_=t64[:]),
                deps=[t_m5])
    t_dm = S.dma("gpsimd", st_scr.rearrange("r (p f) -> (r p) f", p=TB)[0:TB, :],
                 m64[:], deps=[t_m3, t_m6])
    t_dr = S.dma("gpsimd", st_scr.rearrange("r (p f) -> (r p) f", p=TB)[TB:2 * TB, :],
                 r64[:], deps=[t_m6])

    prev = {}
    tok_scan = None
    tok_se_prev = None

    for s in range(NSUB):
        t0c = s * TS
        t_x0 = S.dma("sync", x0t3(),
                     x0dT.rearrange("(a p) t -> p a t", p=128)[:, :, t0c:t0c + TS],
                     deps=[d for d in [prev.get("x0t")] if d], ch=3)
        t_ms = S.dma("sync", mrow_s[:], st_scr[0:1, t0c:t0c + TS],
                     deps=[t_dr] + ([prev["mrow"]] if "mrow" in prev else []),
                     ch=4)
        t_rs = S.dma("sync", rrow_s[:], st_scr[1:2, t0c:t0c + TS],
                     deps=[t_dr] + ([prev["rrow"]] if "rrow" in prev else []),
                     ch=4)
        t_ms = t_rs

        ev_mr = []
        ev_prev0 = prev.get("ps_e")
        for half in range(2):
            sl = slice(half * 512, (half + 1) * 512)
            t_bm = S.op("tensor", lambda e, sl=sl: e.matmul(
                ps_e[:], ones1r[:], mrow_s[0:1, sl], start=True, stop=True),
                deps=[t_ms, t_ones1r]
                + ([prev["mrep"]] if "mrep" in prev else [])
                + ([ev_prev0] if ev_prev0 else []))
            t_em = S.op("vector", lambda e, sl=sl: e.tensor_copy(
                out=mrep[:, sl], in_=ps_e[:]), deps=[t_bm])
            t_br = S.op("tensor", lambda e, sl=sl: e.matmul(
                ps_e[:], ones1r[:], rrow_s[0:1, sl], start=True, stop=True),
                deps=[t_rs, t_em])
            t_er = S.op("vector", lambda e, sl=sl: e.tensor_copy(
                out=rrep[:, sl], in_=ps_e[:]), deps=[t_br])
            ev_mr += [t_em, t_er]
            ev_prev0 = t_er
            prev["mrep"] = t_er
        prev["ps_e"] = t_er

        dma_xp = []
        for half in range(2):
            sl = slice(half * 512, (half + 1) * 512)
            t_cb = S.op("vector", lambda e, sl=sl: e.tensor_copy(
                out=x0b[:], in_=r3(x0t, TS)[:, :, sl]),
                deps=[t_x0] + ([prev["x0b"]] if "x0b" in prev else []))
            tm = td = tb2 = tc2 = None
            for a in range(4):
                tm = S.op("tensor", lambda e, a=a: e.matmul(
                    ps_m[:], xps_b[:, a * 160:a * 160 + 128],
                    x0b[:, a * 512:(a + 1) * 512],
                    start=(a == 0), stop=(a == 3)),
                    deps=[t_cb, t_xpsb] + ([prev["ps_m"]] if "ps_m" in prev else []))
            for a in range(4):
                tb2 = S.op("tensor", lambda e, a=a: e.matmul(
                    ps_b[:], xps_b[:, a * 160 + 128:a * 160 + 144],
                    x0b[:, a * 512:(a + 1) * 512],
                    start=(a == 0), stop=(a == 3)),
                    deps=[prev["ps_b"]] if "ps_b" in prev else [])
            for a in range(4):
                tc2 = S.op("tensor", lambda e, a=a: e.matmul(
                    ps_c[:], xps_b[:, a * 160 + 144:a * 160 + 160],
                    x0b[:, a * 512:(a + 1) * 512],
                    start=(a == 0), stop=(a == 3)),
                    deps=[prev["ps_c"]] if "ps_c" in prev else [])
            t_e1 = S.op("vector", lambda e, sl=sl: e.tensor_copy(
                out=xpr[:, sl], in_=ps_m[:]), deps=[tm])
            bcr_war = [prev["bcr"]] if "bcr" in prev else []
            t_e2 = S.op("vector", lambda e, sl=sl: e.tensor_copy(
                out=bcr_b[:, sl], in_=ps_b[:]), deps=[tb2] + bcr_war)
            t_e3 = S.op("vector", lambda e, sl=sl: e.tensor_copy(
                out=bcr_c[:, sl], in_=ps_c[:]), deps=[tc2] + bcr_war)
            prev["ps_m"], prev["ps_b"], prev["ps_c"] = t_e1, t_e2, t_e3
            prev["x0b"] = tm if tc2 is None else tc2
            dma_xp += [t_e1, t_e2, t_e3]
        t_dx1 = S.dma("gpsimd", xp_ins[s][0:128, :], xpr[:], deps=dma_xp, ch=3)
        t_dx2 = S.dma("gpsimd", xp_ins[s][128:144, :], bcr_b[:], deps=dma_xp, ch=3)
        t_dx3 = S.dma("gpsimd", xp_ins[s][144:160, :], bcr_c[:], deps=dma_xp, ch=3)
        t_ccxp = S.cc("AllReduce", ALU.add, [xp_ins[s].opt()], [xp_outs[s].opt()],
                      deps=[t_dx1, t_dx2, t_dx3])
        t_lx1 = S.dma("sync", xpr[:], xp_outs[s][0:128, :], deps=[t_ccxp], ch=5)
        bw = [prev["bcr"]] if "bcr" in prev else []
        t_lx2 = S.dma("sync", bcr_b[:], xp_outs[s][128:144, :],
                      deps=[t_ccxp] + bw, ch=5)
        t_lx3 = S.dma("sync", bcr_c[:], xp_outs[s][144:160, :],
                      deps=[t_ccxp] + bw, ch=5)
        t_lx1 = t_lx2 = t_lx3

        t_c1 = S.op("vector", lambda e: e.tensor_scalar(
            out=se1[:], in0=mrep[:, 0:512], scalar1=wcolA[:], scalar2=None,
            op0=ALU.mult), deps=[t_lx1, t_wa, prev["mrep"]])
        t_c1b = S.op("vector", lambda e: e.tensor_scalar(
            out=se2[:], in0=mrep[:, 512:1024], scalar1=wcolA[:], scalar2=None,
            op0=ALU.mult), deps=[t_c1])
        t_c2 = S.op("vector", lambda e: e.tensor_tensor(
            out=xpr[:, 0:512], in0=xpr[:, 0:512], in1=se1[:], op=ALU.subtract))
        t_c2b = S.op("vector", lambda e: e.tensor_tensor(
            out=xpr[:, 512:1024], in0=xpr[:, 512:1024], in1=se2[:], op=ALU.subtract))
        t_c3 = S.op("vector", lambda e: e.tensor_tensor(
            out=xpr[:], in0=xpr[:], in1=rrep[:], op=ALU.mult))
        t_c4 = S.op("vector", lambda e: e.tensor_copy(out=dtin_b[:], in_=xpr[:]),
                    deps=[t_c3])
        t_c5 = S.op("vector", lambda e: e.tensor_scalar(
            out=se1[0:16, :], in0=mrep[0:16, 0:512], scalar1=wcolB[:],
            scalar2=None, op0=ALU.mult), deps=[t_lx2, t_wb])
        t_c5b = S.op("vector", lambda e: e.tensor_scalar(
            out=se2[0:16, :], in0=mrep[0:16, 512:1024], scalar1=wcolB[:],
            scalar2=None, op0=ALU.mult))
        t_c6 = S.op("vector", lambda e: e.tensor_tensor(
            out=bcr_b[0:16, 0:512], in0=bcr_b[0:16, 0:512], in1=se1[0:16, :],
            op=ALU.subtract))
        t_c6b = S.op("vector", lambda e: e.tensor_tensor(
            out=bcr_b[0:16, 512:1024], in0=bcr_b[0:16, 512:1024], in1=se2[0:16, :],
            op=ALU.subtract))
        t_c7 = S.op("vector", lambda e: e.tensor_tensor(
            out=bcr_b[:], in0=bcr_b[:], in1=rrep[0:16, :], op=ALU.mult))
        t_c8 = S.op("vector", lambda e: e.tensor_scalar(
            out=se1[0:16, :], in0=mrep[0:16, 0:512], scalar1=wcolC[:],
            scalar2=None, op0=ALU.mult), deps=[t_lx3, t_wc])
        t_c8b = S.op("vector", lambda e: e.tensor_scalar(
            out=se2[0:16, :], in0=mrep[0:16, 512:1024], scalar1=wcolC[:],
            scalar2=None, op0=ALU.mult))
        t_c9 = S.op("vector", lambda e: e.tensor_tensor(
            out=bcr_c[0:16, 0:512], in0=bcr_c[0:16, 0:512], in1=se1[0:16, :],
            op=ALU.subtract))
        t_c9b = S.op("vector", lambda e: e.tensor_tensor(
            out=bcr_c[0:16, 512:1024], in0=bcr_c[0:16, 512:1024], in1=se2[0:16, :],
            op=ALU.subtract))
        t_c10 = S.op("vector", lambda e: e.tensor_tensor(
            out=bcr_c[:], in0=bcr_c[:], in1=rrep[0:16, :], op=ALU.mult))
        tok_bc = t_c10

        t_sp = []
        last_ps_reader = None
        prev_dt = tok_bc
        for a in range(4):
            for half in range(2):
                sl = slice(half * 512, (half + 1) * 512)
                deps_mm = [t_c4, t_dtwb]
                if last_ps_reader:
                    deps_mm.append(last_ps_reader)
                if "dtin" in prev:
                    deps_mm.append(prev.pop("dtin"))
                t_mm = S.op("tensor", lambda e, a=a, sl=sl: e.matmul(
                    ps_m[:], dtw_b[:, a * 128:(a + 1) * 128], dtin_b[:, sl],
                    start=True, stop=True), deps=deps_mm)
                t_s1 = S.op("scalar", lambda e, a=a: e.activation(
                    out=se1[:], in_=ps_m[:], func=AF.Abs, bias=dtb4[:, a:a + 1]),
                    deps=[t_mm, t_dtb, prev_dt])
                t_s2 = S.op("scalar", lambda e: e.activation(
                    out=se2[:], in_=se1[:], func=AF.Exp,
                    scale=cneg1[:]), deps=[t_cneg1])
                t_s3 = S.op("scalar", lambda e: e.activation(
                    out=se1[:], in_=se2[:], func=AF.Ln,
                    bias=cone[:]), deps=[t_cone])
                t_s4 = S.op("scalar", lambda e, a=a: e.activation(
                    out=se2[:], in_=ps_m[:], func=AF.Relu, bias=dtb4[:, a:a + 1]))
                last_ps_reader = t_s4
                t_dt = S.op("vector", lambda e, a=a, sl=sl: e.tensor_tensor(
                    out=dt_[:, a * TS + sl.start: a * TS + sl.stop],
                    in0=se1[:], in1=se2[:], op=ALU.add), deps=[t_s3, t_s4])
                t_sp.append(t_dt)
                prev_dt = t_dt
        last_dtmm = t_mm
        prev["ps_m"] = t_s4
        tok_dt = t_sp[-1]

        t_xl1 = S.op("vector", lambda e: e.tensor_tensor(
            out=ch3(), in0=x0t3(), in1=mid_bcast(mrep[:], NDT), op=ALU.subtract),
            deps=[t_x0, prev["mrep"]] + t_sp[-1:]
            + ([prev["chout"]] if "chout" in prev else []))
        t_xl2 = S.op("vector", lambda e: e.tensor_tensor(
            out=xln3(), in0=ch3(), in1=mid_bcast(rrep[:], NDT), op=ALU.mult))
        t_u = S.op("vector", lambda e: e.tensor_tensor(
            out=u3(), in0=dt3(), in1=xln3(), op=ALU.mult), deps=[tok_dt])
        tok_u = t_u

        for n in range(NST):
            dep_b = [tok_bc]
            if tok_scan:
                dep_b.append(tok_scan)
            if "bcrow" in prev:
                dep_b.append(prev["bcrow"])
            t_rb = S.dma("gpsimd", bcrow[:], bcr_b[n:n + 1, :], deps=dep_b,
                         ch=1)
            tb = tc = None
            ev_prev = prev.get("ps_e")
            for half in range(2):
                sl = slice(half * 512, (half + 1) * 512)
                tb_m = S.op("tensor", lambda e, sl=sl: e.matmul(
                    ps_e[:], ones1r[:], bcrow[0:1, sl], start=True,
                    stop=True), deps=[t_rb]
                    + ([tok_scan] if tok_scan else [])
                    + ([ev_prev] if ev_prev else []))
                tb = S.op("vector", lambda e, sl=sl: e.tensor_copy(
                    out=brep[:, sl], in_=ps_e[:]), deps=[tb_m])
                ev_prev = tb
            t_rc = S.dma("gpsimd", bcrow[:], bcr_c[n:n + 1, :],
                         deps=[tb_m, tb], ch=1)
            for half in range(2):
                sl = slice(half * 512, (half + 1) * 512)
                tc_m = S.op("tensor", lambda e, sl=sl: e.matmul(
                    ps_e[:], ones1r[:], bcrow[0:1, sl], start=True,
                    stop=True), deps=[t_rc, ev_prev])
                tc_m_last = tc_m
                prev["bcrow"] = tc_m
                tc = S.op("vector", lambda e, sl=sl: e.tensor_copy(
                    out=crep[:, sl], in_=ps_e[:]), deps=[tc_m])
                ev_prev = tc
            prev["ps_e"] = tc
            a_col = inner_bcast(a43()[:, :, n:n + 1], TS)
            t_dta = S.op("vector", lambda e, ac=a_col: e.tensor_tensor(
                out=da3()[:, :, 1:TS + 1], in0=dt3(), in1=ac, op=ALU.mult),
                deps=[tok_u, t_a2] + ([tok_se_prev] if tok_se_prev else [t_da0]))
            t_exp = S.op("scalar", lambda e: e.activation(
                out=da3()[:, :, 1:TS + 1], in_=da3()[:, :, 1:TS + 1],
                func=AF.Exp), deps=[t_dta] + ([tok_scan] if tok_scan else []))
            tok_se_prev = t_exp
            t_dbx = S.op("vector", lambda e, n=n: e.tensor_tensor(
                out=dbx3()[:, :, 1:TS + 1], in0=u3(), in1=mid_bcast(brep[:], NDT),
                op=ALU.mult), deps=[tc, t_dbx0])
            t_inj = S.op("vector", lambda e, n=n: e.tensor_copy(
                out=dbx3()[:, :, 0:1], in_=st3()[:, :, n:n + 1]),
                deps=[t_st0])
            t_scan = S.op("vector", lambda e: e.tensor_tensor_scan(
                out=h_[:], data0=da[:], data1=dbx[:], initial=0.0,
                op0=ALU.mult, op1=ALU.add), deps=[t_exp, t_dbx, t_inj])
            t_sv = S.op("vector", lambda e, n=n: e.tensor_copy(
                out=st3()[:, :, n:n + 1], in_=h3()[:, :, TS:TS + 1]),
                deps=[t_scan])
            t_ch = S.op("vector", lambda e: e.tensor_tensor(
                out=ch3(), in0=h3()[:, :, 1:TS + 1], in1=mid_bcast(crep[:], NDT),
                op=ALU.mult), deps=[t_scan, t_xl2])
            t_y = S.op("vector", lambda e: e.tensor_tensor(
                out=y_[:], in0=y_[:], in1=ch[:], op=ALU.add), deps=[t_y0])
            tok_scan = t_y

        t_f1 = S.op("vector", lambda e: e.tensor_tensor(
            out=ch3(), in0=xln3(), in1=inner_bcast(d43()[:, :, 0:1], TS),
            op=ALU.mult), deps=[tok_scan, t_dvec])
        t_f2 = S.op("vector", lambda e: e.tensor_tensor(
            out=y_[:], in0=y_[:], in1=ch[:], op=ALU.add))
        t_f3 = S.op("vector", lambda e: e.tensor_tensor(
            out=ch[:], in0=y_[:], in1=x0t[:], op=ALU.add))
        t_out = S.dma("gpsimd",
                      x1d.rearrange("(a p) t -> p a t", p=128)[:, :, t0c:t0c + TS],
                      ch3(), deps=[t_f3], ch=2)
        t_y0 = S.op("vector", lambda e: e.memset(y_[:], 0.0))
        prev["x0t"] = t_out
        prev["chout"] = t_out
        prev["mrow"] = t_br
        prev["rrow"] = t_br
        prev["bcr"] = tc_m_last
        prev["dtin"] = last_dtmm
        prev["mrep"] = t_f1

    return {'x1d_stores': [prev['chout']]}


def build_mamba():
    nc = bass.Bass("TRN2", target_bir_lowering=False)
    TS = TS_M
    TB = T // 128
    NSUB = T // TS

    x0d = nc.dram_tensor("x0d", [T, DSL], F32, kind="ExternalInput").ap()
    x0dT = nc.dram_tensor("x0dT", [DSL, T], F32, kind="ExternalInput").ap()
    alog = nc.dram_tensor("alog", [128, NDT * NST], F32, kind="ExternalInput").ap()
    dvec = nc.dram_tensor("dvec", [128, NDT], F32, kind="ExternalInput").ap()
    xpwS = nc.dram_tensor("xpwS", [128, NDT * 160], F32, kind="ExternalInput").ap()
    dtwT = nc.dram_tensor("dtwT", [DTR, DSL], F32, kind="ExternalInput").ap()
    dtb = nc.dram_tensor("dtb", [128, NDT], F32, kind="ExternalInput").ap()

    x1d = nc.dram_tensor("x1d", [DSL, T], F32, kind="ExternalOutput").ap()

    st_in = nc.dram_tensor("st_in", [3, T], F32, kind="Internal").ap()
    st_out = nc.dram_tensor("st_out", [3, T], F32, kind="Internal").ap()
    st_scr = nc.dram_tensor("st_scr", [2, T], F32, kind="Internal").ap()
    xp_ins = [nc.dram_tensor(f"xp_in{s}", [160, TS], F32, kind="Internal").ap()
              for s in range(T // TS_M)]
    xp_outs = [nc.dram_tensor(f"xp_out{s}", [160, TS], F32, kind="Internal").ap()
               for s in range(T // TS_M)]

    with contextlib.ExitStack() as stk:
        S = Sched(nc, stk)
        IO = dict(x0d=x0d, x0dT=x0dT, alog=alog, dvec=dvec, xpwS=xpwS,
                  dtwT=dtwT, dtb=dtb, x1d=x1d, st_in=st_in, st_out=st_out,
                  st_scr=st_scr, xp_ins=xp_ins, xp_outs=xp_outs)
        with contextlib.ExitStack() as pstk:
            phase_mamba(nc, S, pstk, IO)
        S.emit(block=stk.enter_context(nc.Block()))
    return nc


def phase_ffn0(nc, S, stk, IO):
    To = OWN_T
    NK = D // 128
    NH = To // 512
    x1T = IO.get('x1T')
    wqT = IO.get('wqT')
    scaleT = IO.get('scaleT')
    zeroT = IO.get('zeroT')
    bias0 = IO.get('bias0')
    wkvT = IO.get('wkvT')
    bkv_c = IO.get('bkv_c')
    x2T = IO.get('x2T')
    qd = IO.get('qd')
    kd = IO.get('kd')
    vt = IO.get('vt')
    wdeq = IO.get('wdeq')
    wkvb = IO.get('wkvb')
    b0scr = IO.get('b0scr')
    x1idx = IO.get("x1idx")
    x1v = IO.get("x1v")
    tok_pre = IO.get("tok_pre", [])

    def x1_load(buf, rowtile, deps, ch, colsl=None):
        if x1idx is None:
            src = x1T[rowtile * 128:(rowtile + 1) * 128,
                      colsl if colsl is not None else slice(0, To)]
            return S.dma("sync", buf, src, deps=deps, ch=ch)
        eo = colsl.start if colsl is not None else 0
        return S.idma(buf, x1v, x1idx_sb[:, rowtile:rowtile + 1],
                      deps=list(deps) + list(tok_pre), ch=6 + ch % 2,
                      element_offset=eo)
    _pfx = IO.get("pfx", "")
    sb = lambda name, shape, dt: stk.enter_context(
        nc.sbuf_tensor(_pfx + name, shape, dt))
    psu = lambda name, shape, dt=F32: stk.enter_context(
        nc.psum_tensor(_pfx + name, shape, dt))

    xn = sb("xn", [128, NK * To], BF16)
    x1idx_sb = sb("x1idx_sb", [128, NK], I32) if x1idx is not None else None
    if x1idx is not None:
        IO["tok_idx"] = S.dma("sync", x1idx_sb[:], x1idx, ch=0)
        tok_pre = list(tok_pre) + [IO["tok_idx"]]
    e_all = sb("e_all", [128, NK * 32], BF16)
    zsc = sb("zsc", [32, D], F32)
    zsn_b = sb("zsn_b", [32, D], BF16)
    wq_i = sb("wq_i", [128, 512], I32)
    wq_f = sb("wq_f", [128, 512], F32)
    s_rep = sb("s_rep", [128, 512], F32)
    wq_b = sb("wq_b", [128, D], BF16)
    xt_a = sb("xt_a", [128, To], F32)
    xt_b = sb("xt_b", [128, To], F32)
    sqt = sb("sqt", [128, To], F32)
    mrep = sb("mrep", [128, To], F32)
    rrep = sb("rrep", [128, To], F32)
    srow = sb("srow", [1, To], F32)
    qrow = sb("qrow", [1, To], F32)
    trow = sb("trow", [1, To], F32)
    s_b = sb("s_b", [32, To], BF16)
    wst_a = sb("wst_a", [128, 32 * 128], BF16)
    wst_b = sb("wst_b", [128, 32 * 128], BF16)
    wkv_f = sb("wkv_f", [128, 3072], F32)
    wkv_c = sb("wkv_c", [128, 3072], BF16)
    bq = sb("bq", [128, 48], F32)
    b0_b = sb("b0_b", [1, D], BF16)
    ones1r = sb("ones1r", [1, 128], F32)
    ones1b = sb("ones1b", [1, 512], BF16)
    scrow = sb("scrow", [1, 512], F32)
    onesc = sb("onesc", [128, 1], F32)
    ceps = sb("ceps", [128, 1], F32)
    out_b = sb("out_b", [128, To], BF16)
    idn_b = sb("idn_b", [128, 128], BF16)
    vt_s = sb("vt_s", [128, PART], BF16)

    ps_o = psu("ps_o", [128, 512])
    ps_e = psu("ps_e", [128, 512])
    ps_s = psu("ps_s", [32, 512])
    ps_tb = psu("ps_tb", [128, 128], BF16)
    ps_r = [psu(f"ps_r{i}", [1, 512]) for i in range(2 * NH)]


    t_sc = S.dma("sync", zsc[:], scaleT, ch=0)

    t_b0 = S.dma("sync", xt_a[0:8, 0:512],
                 bias0.rearrange("o (a f) -> (o a) f", a=8), ch=0)
    t_bq = S.dma("sync", bq[:], bkv_c, ch=0)
    t_set = t_sc = t_b0 = t_bq
    t_ones = S.op("vector", lambda e: e.memset(ones1r[:], 1.0))
    t_onesb = S.op("vector", lambda e: e.memset(ones1b[:], 1.0))
    t_onesc = S.op("vector", lambda e: e.memset(onesc[:], 1.0))
    t_ceps = S.op("vector", lambda e: e.memset(ceps[:], LN_EPS))
    t_zs3 = None
    pzs = None
    for q8 in range(D // 512):
        sl = slice(q8 * 512, (q8 + 1) * 512)
        t_z1 = S.dma("sync", wq_f[0:32, :], zeroT[:, sl],
                     deps=[pzs] if pzs else [], ch=3)
        t_z2 = S.op("vector", lambda e, sl=sl: e.tensor_tensor(
            out=wq_f[0:32, :], in0=wq_f[0:32, :], in1=zsc[0:32, sl],
            op=ALU.mult), deps=[t_z1, t_set])
        t_z3 = S.op("vector", lambda e: e.tensor_scalar(
            out=wq_f[0:32, :], in0=wq_f[0:32, :], scalar1=-1.0,
            scalar2=None, op0=ALU.mult))
        t_zs3 = S.op("vector", lambda e, sl=sl: e.tensor_copy(
            out=zsn_b[:, sl], in_=wq_f[0:32, :]))
        pzs = t_zs3
    t_b0c = S.op("vector", lambda e: e.tensor_copy(
        out=out_b[0:8, 0:512], in_=xt_a[0:8, 0:512]), deps=[t_set])
    t_b0d = S.dma("gpsimd", b0scr.rearrange("o (a f) -> (o a) f", a=8),
                  out_b[0:8, 0:512], deps=[t_b0c], ch=3)
    t_b0b = S.dma("sync", b0_b[:], b0scr, deps=[t_b0d], ch=0)
    t_id0 = S.op("gpsimd", lambda e: e.memset(idn_b[:], 0.0), inc=1)
    t_idn = S.op("gpsimd", lambda e: e.affine_select(
        out=idn_b[:], in_=idn_b[:], compare_op=ALU.not_equal, fill=1.0,
        base=0, pattern=[[-1, 128]], channel_multiplier=1),
        deps=[t_id0], inc=1)
    t_e0 = S.op("gpsimd", lambda e: e.memset(e_all[:], 0.0), inc=1)
    t_eall = t_e0
    for g in range(NK):
        t_eall = S.op("gpsimd", lambda e, g=g: e.affine_select(
            out=e_all[:, g * 32:(g + 1) * 32],
            in_=e_all[:, g * 32:(g + 1) * 32],
            compare_op=ALU.not_equal, fill=1.0,
            base=-g, pattern=[[1, 32]], channel_multiplier=0),
            deps=[t_e0], inc=1)

    ln_state = {"war": [t_b0b]}

    def emit_ln(src_dram, norm_into_xn=True, use_x1_loader=False):
        war = ln_state["war"]
        tma = tmb = None
        prevload = {0: None, 1: None}
        for g in range(NK):
            buf = xt_a if g % 2 == 0 else xt_b
            pu = prevload[g % 2]
            if use_x1_loader:
                t_l = x1_load(buf[:], g, deps=([pu] if pu else []) + war,
                              ch=g % 2)
            else:
                t_l = S.dma("sync", buf[:], src_dram[g * 128:(g + 1) * 128, :],
                            deps=([pu] if pu else []) + war, ch=1 + g % 2)
            t_sq = S.op("scalar", lambda e, b=buf: e.activation(
                out=sqt[:], in_=b[:], func=AF.Square),
                deps=[t_l] + ([tmb] if tmb else []))
            for h in range(NH):
                sl = slice(h * 512, (h + 1) * 512)
                tma = S.op("tensor", lambda e, b=buf, sl=sl, g=g, h=h:
                           e.matmul(ps_r[h][:], onesc[:], b[:, sl],
                                    start=(g == 0), stop=(g == NK - 1)),
                           deps=[t_l, t_onesc] + war)
                tmb = S.op("tensor", lambda e, sl=sl, g=g, h=h: e.matmul(
                    ps_r[NH + h][:], onesc[:], sqt[:, sl],
                    start=(g == 0), stop=(g == NK - 1)),
                    deps=[t_sq])
            prevload[g % 2] = tmb
        ev = []
        for h in range(NH):
            sl = slice(h * 512, (h + 1) * 512)
            ev.append(S.op("vector", lambda e, sl=sl, h=h: e.tensor_scalar(
                out=srow[:, sl], in0=ps_r[h][:], scalar1=1.0 / D,
                scalar2=None, op0=ALU.mult), deps=[tma, tmb]))
            ev.append(S.op("vector", lambda e, sl=sl, h=h: e.tensor_scalar(
                out=qrow[:, sl], in0=ps_r[NH + h][:], scalar1=1.0 / D,
                scalar2=None, op0=ALU.mult), deps=[tmb]))
        t_v1 = S.op("vector", lambda e: e.tensor_tensor(
            out=trow[:], in0=srow[:], in1=srow[:], op=ALU.mult), deps=ev)
        t_v2 = S.op("vector", lambda e: e.tensor_tensor(
            out=qrow[:], in0=qrow[:], in1=trow[:], op=ALU.subtract))
        t_v3 = S.op("scalar", lambda e: e.activation(
            out=trow[:], in_=qrow[:], func=AF.Sqrt, bias=ceps[0:1, 0:1]),
            deps=[t_v2, t_ceps])
        t_v4 = S.op("vector", lambda e: e.reciprocal(
            out=qrow[:], in_=trow[:]), deps=[t_v3])
        evb = None
        for h in range(NH):
            sl = slice(h * 512, (h + 1) * 512)
            t_m1 = S.op("tensor", lambda e, sl=sl: e.matmul(
                ps_e[:], ones1r[:], srow[0:1, sl], start=True, stop=True),
                deps=[t_v4, t_ones] + ([evb] if evb else []) + war
                + ([ln_state["ps_e"]] if ln_state.get("ps_e") else []))
            t_m2 = S.op("vector", lambda e, sl=sl: e.tensor_copy(
                out=mrep[:, sl], in_=ps_e[:]), deps=[t_m1])
            t_m3 = S.op("tensor", lambda e, sl=sl: e.matmul(
                ps_e[:], ones1r[:], qrow[0:1, sl], start=True, stop=True),
                deps=[t_m2])
            evb = S.op("vector", lambda e, sl=sl: e.tensor_copy(
                out=rrep[:, sl], in_=ps_e[:]), deps=[t_m3])
        tok_norm = evb
        ln_state["ps_e"] = evb
        tok_xn = None
        if norm_into_xn:
            prevload = {0: None, 1: None}
            for g in range(NK):
                buf = xt_a if g % 2 == 0 else xt_b
                pu = prevload[g % 2]
                if use_x1_loader:
                    t_l = x1_load(buf[:], g, deps=([pu] if pu else [tma, tmb]),
                                  ch=g % 2)
                else:
                    t_l = S.dma("sync", buf[:], src_dram[g * 128:(g + 1) * 128, :],
                                deps=([pu] if pu else [tma, tmb]), ch=1 + g % 2)
                t_n1 = S.op("vector", lambda e, b=buf: e.tensor_tensor(
                    out=sqt[:], in0=b[:], in1=mrep[:], op=ALU.subtract),
                    deps=[t_l, tok_norm, tmb])
                tok_xn = S.op("vector", lambda e, g=g: e.tensor_tensor(
                    out=xn[:, g * To:(g + 1) * To], in0=sqt[:], in1=rrep[:],
                    op=ALU.mult))
                prevload[g % 2] = tok_xn
        return tok_xn, tok_norm

    tok_xn1, _ = emit_ln(x1T, use_x1_loader=(x1idx is not None))

    tok_deq = []
    pv = {"wq": None, "srep": None, "store": None}
    for g in range(NK):
        for q8 in range(D // 512):
            sl = slice(q8 * 512, (q8 + 1) * 512)
            t_lw = S.dma("sync", wq_i[:], wqT[g * 128:(g + 1) * 128, sl],
                         deps=[pv["wq"]] if pv["wq"] else [], ch=3)
            t_sr = S.dma("gpsimd", scrow[:], zsc[g:g + 1, sl],
                         deps=[pv["bm"]] if pv.get("bm") else [], ch=2)
            t_bm = S.op("tensor", lambda e: e.matmul(
                ps_e[:], ones1r[:], scrow[:], start=True, stop=True),
                deps=[t_set, t_ones, t_sr]
                + ([pv["srep"]] if pv["srep"] else [])
                + ([ln_state["ps_e"]] if ln_state.get("ps_e") else []))
            pv["bm"] = t_bm
            ln_state["ps_e"] = None
            t_eb = S.op("vector", lambda e: e.tensor_copy(
                out=s_rep[:], in_=ps_e[:]), deps=[t_bm])
            pv["srep"] = t_eb
            ln_state["ps_e"] = t_eb
            t_cf = S.op("vector", lambda e: e.tensor_copy(
                out=wq_f[:], in_=wq_i[:]), deps=[t_lw])
            pv["wq"] = t_cf
            t_mul = S.op("vector", lambda e, sl=sl: e.tensor_tensor(
                out=wq_b[:, sl], in0=wq_f[:], in1=s_rep[:], op=ALU.mult),
                deps=[t_cf, t_eb]
                + ([pv["store"]] if pv["store"] else []))
        t_st = None
        for nt in range(NK):
            t_st = S.dma("gpsimd",
                         wdeq.rearrange("p (nt g nn) -> p nt g nn",
                                        nt=NK, g=NK)[:, nt, g, :],
                         wq_b[:, nt * 128:(nt + 1) * 128],
                         deps=[t_mul], ch=0)
        pv["store"] = t_st
        tok_deq.append(t_st)

    tok_kv = None
    pkv = {}
    for g in range(NK):
        for hlf in range(2):
            t_lk = S.dma("sync", wkv_f[:],
                         wkvT[g * 128:(g + 1) * 128,
                              hlf * 3072:(hlf + 1) * 3072],
                         deps=[pkv["c"]] if pkv.get("c") else [],
                         ch=6 + hlf)
            t_ck = S.op("vector", lambda e: e.tensor_copy(
                out=wkv_c[:], in_=wkv_f[:]),
                deps=[t_lk] + ([pkv["st"]] if pkv.get("st") else []))
            pkv["c"] = t_ck
            for rc in range(24):
                tok_kv = S.dma("gpsimd",
                               wkvb.rearrange("p (rc g nn) -> p rc g nn",
                                              rc=48, g=NK)[
                                   :, hlf * 24 + rc, g, :],
                               wkv_c[:, rc * 128:(rc + 1) * 128],
                               deps=[t_ck], ch=3)
            pkv["st"] = tok_kv

    tok_x2 = []
    tok_s2mm = []
    pv2 = {"wtile": None, "x1res": None, "psol": None}
    for h in range(NH):
        sl = slice(h * 512, (h + 1) * 512)
        tsm = None
        for g in range(NK):
            tsm = S.op("tensor", lambda e, g=g, sl=sl: e.matmul(
                ps_s[:], e_all[:, g * 32:(g + 1) * 32],
                xn[:, g * To + sl.start:g * To + sl.stop],
                start=(g == 0), stop=(g == NK - 1)),
                deps=[tok_xn1, t_eall]
                + ([pv2["sevac"]] if "sevac" in pv2 else []))
        t_sev = S.op("vector", lambda e, sl=sl: e.tensor_copy(
            out=s_b[:, sl], in_=ps_s[:]), deps=[tsm])
        pv2["sevac"] = t_sev
        for nt in range(NK):
            deps_first = [tok_deq[-1], t_sev]
            if pv2.get("psol"):
                deps_first.append(pv2["psol"])
            wbuf = wst_a if nt % 2 == 0 else wst_b
            t_lw = S.dma("sync", wbuf[:],
                         wdeq[:, nt * NK * 128:(nt + 1) * NK * 128],
                         deps=[pv2[f"wt{nt % 2}"]]
                         if pv2.get(f"wt{nt % 2}") else [tok_deq[-1]],
                         ch=4 + nt % 2, tag="wdq")
            tmm = None
            for g in range(NK):
                tmm = S.op("tensor", lambda e, g=g, nt=nt, sl=sl, w=wbuf:
                           e.matmul(ps_o[:], w[:, g * 128:(g + 1) * 128],
                                    xn[:, g * To + sl.start:g * To + sl.stop],
                                    start=(g == 0), stop=False),
                           deps=[t_lw] + deps_first)
            pv2[f"wt{nt % 2}"] = tmm
            t_zc2 = S.op("tensor", lambda e, nt=nt, sl=sl: e.matmul(
                ps_o[:], zsn_b[0:32, nt * 128:(nt + 1) * 128], s_b[0:32, sl],
                start=False, stop=False), deps=[t_zs3, t_sev])
            t_bc2 = S.op("tensor", lambda e, nt=nt: e.matmul(
                ps_o[:], b0_b[0:1, nt * 128:(nt + 1) * 128], ones1b[:],
                start=False, stop=True), deps=[t_b0b, t_onesb])
            if x1idx is None:
                t_rx = S.dma("sync", xt_a[:, sl] if h == 0 else xt_b[:, sl],
                             x1T[nt * 128:(nt + 1) * 128, sl],
                             deps=[pv2["x1res"]] if pv2["x1res"] else [], ch=1)
            else:
                t_rx = x1_load(xt_a[:, sl] if h == 0 else xt_b[:, sl], nt,
                               deps=[pv2["x1res"]] if pv2["x1res"] else [],
                               ch=h, colsl=sl)
            rbuf = xt_a if h == 0 else xt_b
            t_x2 = S.op("vector", lambda e, sl=sl, b=rbuf: e.scalar_tensor_tensor(
                out=sqt[:, sl], in0=ps_o[:], scalar=1.0, in1=b[:, sl],
                op0=ALU.mult, op1=ALU.add), deps=[t_bc2, t_rx])
            pv2["x1res"] = t_x2
            pv2["psol"] = t_x2
            t_so = S.dma("gpsimd", x2T[nt * 128:(nt + 1) * 128, sl],
                         sqt[:, sl], deps=[t_x2], ch=1)
            tok_x2.append(t_so)
            pv2["x1res"] = t_so

    ln_state["war"] = [tok_x2[-1], pv2["psol"]]
    tok_xn2, _ = emit_ln(x2T)

    t_qkv_done = []
    pvq = {"w": None, "w2": None, "ob": None, "vs": None}
    for part in range(3):
        for rt in range(PART // 128):
            rcol = part * 16 + rt
            wb = wst_a if rcol % 2 == 0 else wst_b
            t_lw = S.dma("sync", wb[:],
                         wkvb[:, rcol * NK * 128:(rcol + 1) * NK * 128],
                         deps=[pvq[f"wt{rcol % 2}"]]
                         if pvq.get(f"wt{rcol % 2}") else [tok_kv],
                         ch=4 + rcol % 2)
            tmm = None
            for g in range(NK):
                for h in range(NH):
                    sl = slice(h * 512, (h + 1) * 512)
                    tmm = S.op("tensor", lambda e, wb=wb, g=g, sl=sl, h=h:
                               e.matmul(ps_o[:] if h == 0 else ps_e[:],
                                        wb[:, g * 128:(g + 1) * 128],
                                        xn[:, g * To + sl.start:
                                           g * To + sl.stop],
                                        start=(g == 0), stop=(g == NK - 1)),
                               deps=[t_lw, tok_xn2]
                               + ([pvq["ev"]] if pvq.get("ev") else [])
                               + ([ln_state["ps_e"]]
                                  if ln_state.get("ps_e") else []))
                pvq[f"wt{rcol % 2}"] = tmm
                ln_state["ps_e"] = None
            evs = []
            for h in range(NH):
                pso = ps_o if h == 0 else ps_e
                sl = slice(h * 512, (h + 1) * 512)
                evs.append(S.op("scalar", lambda e, p=pso, sl=sl, rc=rcol:
                                e.activation(out=out_b[:, sl], in_=p[:],
                                             func=AF.Identity,
                                             bias=bq[:, rc:rc + 1]),
                                deps=[tmm, t_set] +
                                ([pvq["ob"]] if pvq["ob"] else [])))
            pvq["ob"] = evs[-1]
            pvq["ev"] = evs[-1]
            if part < 2:
                dst = qd if part == 0 else kd
                t_so = S.dma("gpsimd", dst[rt * 128:(rt + 1) * 128, :],
                             out_b[:], deps=evs, ch=2)
                pvq["ob"] = t_so
                t_qkv_done.append(t_so)
            else:
                tts = []
                for tt in range(To // 128):
                    t_tp = S.op("tensor", lambda e, tt=tt: e.transpose(
                        ps_tb[:], out_b[:, tt * 128:(tt + 1) * 128],
                        idn_b[:]), deps=evs + [t_idn]
                        + ([pvq["vs"]] if pvq["vs"] else []))
                    t_ev = S.op("vector", lambda e, tt=tt, rt=rt: e.tensor_copy(
                        out=vt_s[tt * 128 if False else 0:128,
                                 rt * 128:(rt + 1) * 128]
                        if False else
                        vt_s[0:128, rt * 128:(rt + 1) * 128],
                        in_=ps_tb[:]), deps=[t_tp])
                    t_sv = S.dma("gpsimd",
                                 vt[tt * 128:(tt + 1) * 128,
                                    rt * 128:(rt + 1) * 128],
                                 vt_s[0:128, rt * 128:(rt + 1) * 128],
                                 deps=[t_ev], ch=2)
                    pvq["vs"] = t_sv
                    tts.append(t_sv)
                t_qkv_done.append(tts[-1])
                pvq["ob"] = tts[-1]

    S.op("gpsimd", lambda e: e.engine_nop(),
         deps=[tok_x2[-1], t_qkv_done[-1]], inc=1)

    return {'x2': tok_x2[-1], 'kv': t_qkv_done, 'kd': pvq.get('obk'),
            'last': t_qkv_done[-1]}


def build_ffn0_qkv():
    nc = bass.Bass("TRN2", target_bir_lowering=False)
    To = OWN_T
    NK = D // 128
    NH = To // 512

    x1T = nc.dram_tensor("x1T", [D, To], F32, kind="ExternalInput").ap()
    wqT = nc.dram_tensor("wqT", [D, D], I32, kind="ExternalInput").ap()
    scaleT = nc.dram_tensor("scaleT", [32, D], F32, kind="ExternalInput").ap()
    zeroT = nc.dram_tensor("zeroT", [32, D], F32, kind="ExternalInput").ap()
    bias0 = nc.dram_tensor("bias0", [1, D], F32, kind="ExternalInput").ap()
    wkvT = nc.dram_tensor("wkvT", [D, 3 * PART], F32, kind="ExternalInput").ap()
    bkv_c = nc.dram_tensor("bkv_c", [128, 48], F32, kind="ExternalInput").ap()

    x2T = nc.dram_tensor("x2T", [D, To], F32, kind="ExternalOutput").ap()
    qd = nc.dram_tensor("qd", [PART, To], BF16, kind="ExternalOutput").ap()
    kd = nc.dram_tensor("kd", [PART, To], BF16, kind="ExternalOutput").ap()
    vt = nc.dram_tensor("vt", [To, PART], BF16, kind="ExternalOutput").ap()

    wdeq = nc.dram_tensor("wdeq", [128, 32 * 32 * 128], BF16, kind="Internal").ap()
    wkvb = nc.dram_tensor("wkvb", [128, 48 * 32 * 128], BF16, kind="Internal").ap()
    b0scr = nc.dram_tensor("b0scr", [1, D], BF16, kind="Internal").ap()

    with contextlib.ExitStack() as stk:
        S = Sched(nc, stk)
        IO = dict(x1T=x1T, wqT=wqT, scaleT=scaleT, zeroT=zeroT, bias0=bias0,
                  wkvT=wkvT, bkv_c=bkv_c, x2T=x2T, qd=qd, kd=kd, vt=vt,
                  wdeq=wdeq, wkvb=wkvb, b0scr=b0scr)
        with contextlib.ExitStack() as pstk:
            phase_ffn0(nc, S, pstk, IO)
        S.emit(block=stk.enter_context(nc.Block()))
    return nc


def phase_attn(nc, S, stk, IO):
    To = OWN_T
    NK = D // 128
    NQT = To // 128
    NKB = T // 512
    NCT = PART // 128
    qd = IO.get('qd')
    kg = IO.get('kg')
    vg = IO.get('vg')
    woT = IO.get('woT')
    bo_c = IO.get('bo_c')
    x2T = IO.get('x2T')
    qpos = IO.get('qpos')
    x3T = IO.get('x3T')
    xn3T = IO.get('xn3T')
    x3scr = IO.get('x3scr')
    kga = IO.get('kga')
    vga = IO.get('vga')
    tok_pre = IO.get('tok_pre', [])
    _pfx = IO.get("pfx", "")
    sb = lambda name, shape, dt: stk.enter_context(
        nc.sbuf_tensor(_pfx + name, shape, dt))
    psu = lambda name, shape, dt=F32: stk.enter_context(
        nc.psum_tensor(_pfx + name, shape, dt))

    kh = sb("kh", [128, 2 * T], BF16)
    vh = sb("vh", [128, (T // 128) * 256], BF16)
    qh = sb("qh", [128, 2 * To], BF16)
    vals = sb("vals", [128, NQT * PART], BF16)
    valsT = sb("valsT", [128, NCT * To], BF16)
    NKB2 = T // 1024
    iota_r = sb("iota_r", [128, 1024], F32)
    qkc = sb("qkc", [128, NQT * NKB2], F32)
    selm = sb("selm", [128, 1024], F32)
    sc_f = sb("sc_f", [128, 1024], F32)
    p_b = sb("p_b", [128, 1024], BF16)
    pt_b = sb("pt_b", [128, 1024], BF16)
    vacc = sb("vacc", [128, 256], F32)
    m_run = sb("m_run", [128, 1], F32)
    mn_c = sb("mn_c", [128, 1], F32)
    am_c = sb("am_c", [128, 1], F32)
    ac_c = sb("ac_c", [128, 1], F32)
    nm_c = sb("nm_c", [128, 1], F32)
    lb_c = sb("lb_c", [128, 1], F32)
    l_run = sb("l_run", [128, 1], F32)
    rl_c = sb("rl_c", [128, 1], F32)
    idn_b = sb("idn_b", [128, 128], BF16)
    bo_sb = sb("bo_sb", [128, 32], F32)
    qp_c = sb("qp_c", [128, NQT], F32)
    onesc = sb("onesc", [128, 1], F32)
    ones1r = sb("ones1r", [1, 128], F32)
    ceps = sb("ceps", [128, 1], F32)

    ps_s = psu("ps_s", [128, 1024])
    ps_v = psu("ps_v", [128, 256])
    ps_tb = psu("ps_tb", [128, 1024], BF16)
    ps_o = psu("ps_o", [128, 512])
    ps_e = psu("ps_e", [128, 512])
    ps_r = [psu(f"ps_r{i}", [1, 512]) for i in range(2)]

    NH2 = To // 512
    ps_q = [ps_o, ps_e]

    t_bo = S.dma("sync", bo_sb[:], bo_c, ch=0)
    t_qp = S.dma("sync", qp_c[:], qpos, ch=0)
    t_set = t_bo = t_qp
    t_onesc = S.op("vector", lambda e: e.memset(onesc[:], 1.0))
    t_ones1r = S.op("vector", lambda e: e.memset(ones1r[:], 1.0))
    t_ceps = S.op("vector", lambda e: e.memset(ceps[:], LN_EPS))
    t_id0 = S.op("gpsimd", lambda e: e.memset(idn_b[:], 0.0), inc=1)
    t_idn = S.op("gpsimd", lambda e: e.affine_select(
        out=idn_b[:], in_=idn_b[:], compare_op=ALU.not_equal, fill=1.0,
        base=0, pattern=[[-1, 128]], channel_multiplier=1),
        deps=[t_id0], inc=1)
    dqstk = contextlib.ExitStack()
    if IO.get("dq_io") is not None:
        _dq_emit, _dq_n, _dq_done = make_dequant_emitter(
            nc, S, dqstk, IO["dq_io"], pfx="dq_")
        IO["dq_emit"] = _dq_emit
        IO["dq_done"] = _dq_done
        IO["dq_per_head"] = (_dq_n + HEADS - 1) // HEADS
    t_iot = S.op("gpsimd", lambda e: e.iota(
        out=iota_r[:], pattern=[[1, 1024]], base=0, channel_multiplier=0,
        allow_small_or_imprecise_dtypes=True), inc=1)
    t_qkc = None
    for qt in range(NQT):
        for kb in range(NKB2):
            t_qkc = S.op("vector", lambda e, qt=qt, kb=kb: e.tensor_scalar(
                out=qkc[:, qt * NKB2 + kb:qt * NKB2 + kb + 1],
                in0=qp_c[:, qt:qt + 1], scalar1=float(-kb * 1024),
                scalar2=None, op0=ALU.add), deps=[t_set])

    pv = {}
    for h in range(HEADS):
        if kga is None:
            t_k = S.dma("sync", kh[:].rearrange("p (c t) -> p c t", c=2),
                        kg.rearrange("(c p) t -> p c t", p=128)[
                            :, 2 * h:2 * h + 2, :],
                        deps=[pv["kh"]] if "kh" in pv else [], ch=1)
            t_v = S.dma("sync", vh[:].rearrange("p (a c) -> p a c", a=T // 128),
                        vg.rearrange("(a p) c -> p a c", p=128)[
                            :, :, 256 * h:256 * (h + 1)],
                        deps=[pv["vh"]] if "vh" in pv else [], ch=2)
        else:
            kd_ = [pv["kh"]] if "kh" in pv else []
            kd_ = kd_ + list(tok_pre)
            for cs in range(2):
                for r in range(8):
                    t_k = S.dma("sync",
                                kh[:].rearrange("p (c t) -> p c t", c=2)
                                [:, cs, r * To:(r + 1) * To],
                                kga[r, (2 * h + cs) * 128:
                                    (2 * h + cs + 1) * 128, :],
                                deps=kd_, ch=1)
            vd_ = [pv["vh"]] if "vh" in pv else []
            vd_ = vd_ + list(tok_pre)
            tpr = To // 128
            for aa in range(T // 128):
                t_v = S.dma("sync",
                            vh[:].rearrange("p (a c) -> p a c", a=T // 128)
                            [:, aa, :],
                            vga[aa // tpr, (aa % tpr) * 128:
                                (aa % tpr + 1) * 128,
                                256 * h:256 * (h + 1)],
                            deps=vd_, ch=2)
        t_q = S.dma("sync", qh[:].rearrange("p (c t) -> p c t", c=2),
                    qd.rearrange("(c p) t -> p c t", p=128)[
                        :, 2 * h:2 * h + 2, :],
                    deps=([pv["qh"]] if "qh" in pv else []) + list(tok_pre),
                    ch=3)
        if IO.get("dq_emit"):
            IO["dq_emit"](IO.get("dq_per_head", 10))
        for qt in range(NQT):
            t_ms = S.op("vector", lambda e: e.memset(m_run[:], 1e30),
                        deps=[pv["fin"]] if "fin" in pv else [])
            t_ls = S.op("vector", lambda e: e.memset(l_run[:], 0.0))
            t_vs = S.op("vector", lambda e: e.memset(vacc[:], 0.0))
            for kb in range(NKB2):
                j = qt * NKB2 + kb
                tmm = None
                for half in range(2):
                    psl = slice(half * 512, (half + 1) * 512)
                    for cs in range(2):
                        tmm = S.op("tensor", lambda e, cs=cs, qt=qt, kb=kb,
                                   psl=psl, half=half: e.matmul(
                            ps_s[:, psl],
                            qh[:, cs * To + qt * 128:cs * To + (qt + 1) * 128],
                            kh[:, cs * T + kb * 1024 + half * 512:
                               cs * T + kb * 1024 + half * 512 + 512],
                            start=(cs == 0), stop=(cs == 1)),
                            deps=[t_k, t_q] + ([pv["sc"]] if "sc" in pv else []))
                t_sel = S.op("vector", lambda e, j=j: e.tensor_scalar(
                    out=selm[:], in0=iota_r[:], scalar1=qkc[:, j:j + 1],
                    scalar2=-10000.0, op0=ALU.is_gt, op1=ALU.mult),
                    deps=[t_qkc, t_iot])
                t_sc = S.op("vector", lambda e: e.tensor_tensor(
                    out=sc_f[:], in0=ps_s[:], in1=selm[:], op=ALU.add),
                    deps=[tmm])
                pv["sc"] = t_sc
                nm_old = m_run if kb % 2 == 0 else mn_c
                nm_new = mn_c if kb % 2 == 0 else m_run
                t_rm = S.op("vector", lambda e: e.reduce_max(
                    out=am_c[:], in_=sc_f[:], axis=AX.X, negate=True))
                t_mn = S.op("vector", lambda e, a=nm_old, b=nm_new:
                            e.tensor_tensor(out=b[:], in0=am_c[:], in1=a[:],
                                            op=ALU.min), deps=[t_ms])
                t_am = S.op("vector", lambda e, a=nm_old, b=nm_new:
                            e.tensor_tensor(out=am_c[:], in0=b[:], in1=a[:],
                                            op=ALU.subtract))
                t_al = S.op("scalar", lambda e: e.activation(
                    out=ac_c[:], in_=am_c[:], func=AF.Exp),
                    deps=[t_am] + ([pv["pexp"]] if "pexp" in pv else []))
                t_pe = S.op("scalar", lambda e, b=nm_new: e.activation(
                    out=p_b[:], in_=sc_f[:], func=AF.Exp, bias=b[:],
                    accum_out=lb_c[:]), deps=[t_mn, t_al])
                pv["pexp"] = t_pe
                t_lr = S.op("vector", lambda e: e.scalar_tensor_tensor(
                    out=l_run[:], in0=l_run[:], scalar=ac_c[:], in1=lb_c[:],
                    op0=ALU.mult, op1=ALU.add), deps=[t_pe, t_ls])
                tps = []
                for st4 in range(8):
                    t_tp = S.op("tensor", lambda e, st4=st4: e.transpose(
                        ps_tb[:, st4 * 128:(st4 + 1) * 128],
                        p_b[:, st4 * 128:(st4 + 1) * 128], idn_b[:]),
                        deps=[t_pe, t_idn]
                        + ([pv["ptev"]] if "ptev" in pv else []))
                    tps.append(t_tp)
                t_ev = S.op("vector", lambda e: e.tensor_copy(
                    out=pt_b[:], in_=ps_tb[:]), deps=[tps[-1]])
                pv["ptev"] = t_ev
                tav = None
                for st4 in range(8):
                    tav = S.op("tensor", lambda e, st4=st4, kb=kb: e.matmul(
                        ps_v[:], pt_b[:, st4 * 128:(st4 + 1) * 128],
                        vh[:, (kb * 8 + st4) * 256:(kb * 8 + st4 + 1) * 256],
                        start=(st4 == 0), stop=(st4 == 7)),
                        deps=[t_ev, t_v]
                        + ([pv["vaccr"]] if "vaccr" in pv else []))
                t_va = S.op("vector", lambda e: e.scalar_tensor_tensor(
                    out=vacc[:], in0=vacc[:], scalar=ac_c[:], in1=ps_v[:],
                    op0=ALU.mult, op1=ALU.add), deps=[tav, t_al, t_vs])
                pv["vaccr"] = t_va
            t_rl = S.op("vector", lambda e: e.reciprocal(
                out=rl_c[:], in_=l_run[:]), deps=[t_lr])
            t_ob = S.op("vector", lambda e, h=h, qt=qt: e.tensor_scalar(
                out=vals[:, qt * PART + h * 256:qt * PART + (h + 1) * 256],
                in0=vacc[:], scalar1=rl_c[:], scalar2=None, op0=ALU.mult),
                deps=[t_va, t_rl])
            pv["fin"] = t_ob
        pv["kh"] = tmm
        pv["vh"] = tav
        pv["qh"] = tmm
    if IO.get("dq_emit"):
        IO["dq_emit"](1000)
        pv3_seed = list(IO["dq_done"]())
    else:
        pv3_seed = []
    dqstk.close()
    wo_f = sb("wo_f", [128, 128], F32)
    wo_b = sb("wo_b", [128, 128], BF16)
    wo_f2 = sb("wo_f2", [128, 128], F32)
    wo_b2 = sb("wo_b2", [128, 128], BF16)
    xt_a = sb("xt_a", [128, To], F32)
    xt_b = sb("xt_b", [128, To], F32)
    sqt = sb("sqt", [128, To], F32)
    mrep = sb("mrep", [128, To], F32)
    rrep = sb("rrep", [128, To], F32)
    srow = sb("srow", [1, To], F32)
    qrow = sb("qrow", [1, To], F32)
    trow = sb("trow", [1, To], F32)
    xnb = sb("xnb", [128, To], BF16)

    pv2 = {}
    t_vT = None
    for qt in range(NQT):
        for ct in range(NCT):
            t_tp = S.op("tensor", lambda e, qt=qt, ct=ct: e.transpose(
                ps_tb[:, 0:128],
                vals[:, qt * PART + ct * 128:qt * PART + (ct + 1) * 128],
                idn_b[:]), deps=[pv["fin"], t_idn]
                + ([t_vT] if t_vT else [pv.get("ptev")]))
            t_vT = S.op("vector", lambda e, qt=qt, ct=ct: e.tensor_copy(
                out=valsT[:, ct * To + qt * 128:ct * To + qt * 128 + 128],
                in_=ps_tb[:, 0:128]), deps=[t_tp])

    tok_x3 = []
    pv3 = {}
    for nt in range(NK):
        tmm = None
        for ct in range(NCT):
            wf = wo_f if ct % 2 == 0 else wo_f2
            wb = wo_b if ct % 2 == 0 else wo_b2
            t_lw = S.dma("sync", wf[:],
                         woT[ct * 128:(ct + 1) * 128, nt * 128:(nt + 1) * 128],
                         deps=[pv3["w"]] if "w" in pv3 else pv3_seed,
                         ch=4 + ct % 2)
            t_cw = S.op("vector", lambda e, wf=wf, wb=wb: e.tensor_copy(
                out=wb[:], in_=wf[:]),
                deps=[t_lw] + ([pv3["w2"]] if "w2" in pv3 else []))
            pv3["w"] = t_cw
            for hh in range(NH2):
                sl = slice(hh * 512, (hh + 1) * 512)
                tmm = S.op("tensor", lambda e, wb=wb, ct=ct, sl=sl, hh=hh:
                           e.matmul(ps_o[:] if hh == 0 else ps_e[:], wb[:],
                                    valsT[:, ct * To + sl.start:ct * To + sl.stop],
                                    start=(ct == 0), stop=(ct == NCT - 1)),
                           deps=[t_cw, t_vT] + pv3_seed
                           + ([pv3["ev"]] if "ev" in pv3 else []))
            pv3["w2"] = tmm
        t_rx = S.dma("sync", xt_a[:], x2T[nt * 128:(nt + 1) * 128, :],
                     deps=[pv3["rx"]] if "rx" in pv3 else pv3_seed, ch=1)
        evs = []
        for hh in range(NH2):
            pso = ps_o if hh == 0 else ps_e
            sl = slice(hh * 512, (hh + 1) * 512)
            t_bi = S.op("scalar", lambda e, p=pso, sl=sl, nt=nt: e.activation(
                out=sqt[:, sl], in_=p[:], func=AF.Identity,
                bias=bo_sb[:, nt:nt + 1]),
                deps=[tmm, t_set] + ([pv3["ev"]] if "ev" in pv3 else [])
                + pv3.get("sq", []))
            evs.append(t_bi)
        pv3["ev"] = evs[-1]
        t_x3 = S.op("vector", lambda e: e.tensor_tensor(
            out=sqt[:], in0=sqt[:], in1=xt_a[:], op=ALU.add),
            deps=evs + [t_rx])
        pv3["rx"] = t_x3
        t_s2 = S.dma("gpsimd", x3scr[nt * 128:(nt + 1) * 128, :], sqt[:],
                     deps=[t_x3], ch=1)
        if x3T is not None:
            t_s1 = S.dma("gpsimd", x3T[nt * 128:(nt + 1) * 128, :], sqt[:],
                         deps=[t_x3], ch=0)
        else:
            t_s1 = t_s2
        pv3["ev"] = t_x3
        pv3["sq"] = [t_s1, t_s2]
        tok_x3 += [t_s1, t_s2]

    tma = tmb = None
    prevload = {0: None, 1: None}
    war = [tok_x3[-2], tok_x3[-1], pv3["rx"]]
    for g in range(NK):
        buf = xt_a if g % 2 == 0 else xt_b
        pu = prevload[g % 2]
        t_l = S.dma("sync", buf[:], x3scr[g * 128:(g + 1) * 128, :],
                    deps=([pu] if pu else []) + war, ch=1 + g % 2)
        t_sq = S.op("scalar", lambda e, b=buf: e.activation(
            out=sqt[:], in_=b[:], func=AF.Square),
            deps=[t_l] + ([tmb] if tmb else []) + [pv3["rx"]]
            + pv3.get("sq", []))
        for hh in range(NH2):
            sl = slice(hh * 512, (hh + 1) * 512)
            tma = S.op("tensor", lambda e, b=buf, sl=sl, g=g, hh=hh:
                       e.matmul(ps_r[hh][:], onesc[:], b[:, sl],
                                start=(g == 0), stop=(g == NK - 1)),
                       deps=[t_l, t_onesc] + war)
            tmb = S.op("tensor", lambda e, sl=sl, g=g, hh=hh: e.matmul(
                ps_q[hh][0:1, :], onesc[:], sqt[:, sl],
                start=(g == 0), stop=(g == NK - 1)),
                deps=[t_sq] + ([pv3["ev"]] if g == 0 else []))
        prevload[g % 2] = tmb
    ev = []
    for hh in range(NH2):
        sl = slice(hh * 512, (hh + 1) * 512)
        ev.append(S.op("vector", lambda e, sl=sl, hh=hh: e.tensor_scalar(
            out=srow[:, sl], in0=ps_r[hh][:], scalar1=1.0 / D,
            scalar2=None, op0=ALU.mult), deps=[tma, tmb]))
        ev.append(S.op("vector", lambda e, sl=sl, hh=hh: e.tensor_scalar(
            out=qrow[:, sl], in0=ps_q[hh][0:1, :], scalar1=1.0 / D,
            scalar2=None, op0=ALU.mult), deps=[tmb]))
    t_v1 = S.op("vector", lambda e: e.tensor_tensor(
        out=trow[:], in0=srow[:], in1=srow[:], op=ALU.mult), deps=ev)
    t_v2 = S.op("vector", lambda e: e.tensor_tensor(
        out=qrow[:], in0=qrow[:], in1=trow[:], op=ALU.subtract))
    t_v3 = S.op("scalar", lambda e: e.activation(
        out=trow[:], in_=qrow[:], func=AF.Sqrt, bias=ceps[0:1, 0:1]),
        deps=[t_v2, t_ceps])
    t_v4 = S.op("vector", lambda e: e.reciprocal(
        out=qrow[:], in_=trow[:]), deps=[t_v3])
    evb = None
    for hh in range(NH2):
        sl = slice(hh * 512, (hh + 1) * 512)
        t_m1 = S.op("tensor", lambda e, sl=sl: e.matmul(
            ps_e[:], ones1r[:], srow[0:1, sl], start=True, stop=True),
            deps=[t_v4, t_ones1r] + ([evb] if evb else [pv3["ev"]]))
        t_m2 = S.op("vector", lambda e, sl=sl: e.tensor_copy(
            out=mrep[:, sl], in_=ps_e[:]), deps=[t_m1])
        t_m3 = S.op("tensor", lambda e, sl=sl: e.matmul(
            ps_e[:], ones1r[:], qrow[0:1, sl], start=True, stop=True),
            deps=[t_m2])
        evb = S.op("vector", lambda e, sl=sl: e.tensor_copy(
            out=rrep[:, sl], in_=ps_e[:]), deps=[t_m3])
    tok_norm = evb
    prevload = {0: None, 1: None}
    t_last = None
    for g in range(NK):
        buf = xt_a if g % 2 == 0 else xt_b
        pu = prevload[g % 2]
        t_l = S.dma("sync", buf[:], x3scr[g * 128:(g + 1) * 128, :],
                    deps=([pu] if pu else [tma, tmb]), ch=1 + g % 2)
        t_n1 = S.op("vector", lambda e, b=buf: e.tensor_tensor(
            out=sqt[:], in0=b[:], in1=mrep[:], op=ALU.subtract),
            deps=[t_l, tok_norm, tmb])
        t_n2 = S.op("vector", lambda e: e.tensor_tensor(
            out=xnb[:], in0=sqt[:], in1=rrep[:], op=ALU.mult),
            deps=([t_last] if t_last else []))
        t_st = S.dma("gpsimd", xn3T[g * 128:(g + 1) * 128, :], xnb[:],
                     deps=[t_n2], ch=2)
        prevload[g % 2] = t_st
        t_last = t_st

    S.op("gpsimd", lambda e: e.engine_nop(),
         deps=[t_last, tok_x3[-2]], inc=1)

    return {'x3': tok_x3, 'xn3': t_last}


def build_attn():
    nc = bass.Bass("TRN2", target_bir_lowering=False)
    To = OWN_T
    NK = D // 128
    NQT = To // 128
    NKB = T // 512
    NCT = PART // 128

    qd = nc.dram_tensor("qd", [PART, To], BF16, kind="ExternalInput").ap()
    kg = nc.dram_tensor("kg", [PART, T], BF16, kind="ExternalInput").ap()
    vg = nc.dram_tensor("vg", [T, PART], BF16, kind="ExternalInput").ap()
    woT = nc.dram_tensor("woT", [PART, D], F32, kind="ExternalInput").ap()
    bo_c = nc.dram_tensor("bo_c", [128, 32], F32, kind="ExternalInput").ap()
    x2T = nc.dram_tensor("x2T", [D, To], F32, kind="ExternalInput").ap()
    qpos = nc.dram_tensor("qpos", [128, NQT], F32, kind="ExternalInput").ap()

    x3T = nc.dram_tensor("x3T", [D, To], F32, kind="ExternalOutput").ap()
    xn3T = nc.dram_tensor("xn3T", [D, To], BF16, kind="ExternalOutput").ap()
    x3scr = nc.dram_tensor("x3scr", [D, To], F32, kind="Internal").ap()

    with contextlib.ExitStack() as stk:
        S = Sched(nc, stk)
        IO = dict(qd=qd, kg=kg, vg=vg, woT=woT, bo_c=bo_c, x2T=x2T, qpos=qpos,
                  x3T=x3T, xn3T=xn3T, x3scr=x3scr)
        with contextlib.ExitStack() as pstk:
            phase_attn(nc, S, pstk, IO)
        S.emit(block=stk.enter_context(nc.Block()))
    return nc


HSL = 1408


def make_dequant_emitter(nc, S, stk, IO, pfx="dq_"):
    NK = D // 128
    NHT = HSL // 128
    w2c, a2X = IO["w2c"], IO["a2X"]
    w2d = IO["w2d"]
    sb = lambda name, shape, dt: stk.enter_context(
        nc.sbuf_tensor(pfx + name, shape, dt))
    WQC = 2048
    code_i = sb("code_i", [128, WQC], I32)
    code_h = sb("code_h", [128, WQC], mybir.dt.float16)
    lut_v = sb("lut_v", [128, WQC], mybir.dt.float16)
    lut_t = sb("lut_t", [128, WQC], mybir.dt.float16)
    a_repf = sb("a_repf", [128, WQC], F32)
    wd_b = sb("wd_b", [128, WQC], BF16)
    pv = {}
    last = {}

    def emit_tile(codes, aX, out_dram, a, width, tiled):
        for wh in range(0, width, WQC):
            wsl = slice(wh, min(wh + WQC, width))
            w = wsl.stop - wsl.start
            t_lc = S.dma("gpsimd", code_i[:, 0:w],
                         codes[a * 128:(a + 1) * 128, wsl],
                         deps=[pv["code"]] if pv.get("code") else [], ch=4)
            t_la = S.dma("gpsimd", a_repf[:, 0:w],
                         aX[a * 128:(a + 1) * 128, wsl],
                         deps=[pv["arep"]] if pv.get("arep") else [], ch=6)
            last["lc"] = t_lc
            last["la"] = t_la
            t_cf = S.op("gpsimd", lambda e, w=w: e.tensor_copy(
                out=code_h[:, 0:w], in_=code_i[:, 0:w]), deps=[t_lc], inc=1)
            pv["code"] = t_cf
            t_acc = S.op("gpsimd", lambda e, w=w: e.tensor_scalar(
                out=lut_v[:, 0:w], in0=code_h[:, 0:w], scalar1=0.0,
                scalar2=float(NF4[0]), op0=ALU.is_equal, op1=ALU.mult),
                deps=[t_cf], inc=1)
            for i in range(1, 16):
                S.op("gpsimd", lambda e, i=i, w=w: e.tensor_scalar(
                    out=lut_t[:, 0:w], in0=code_h[:, 0:w],
                    scalar1=float(i), scalar2=float(NF4[i]),
                    op0=ALU.is_equal, op1=ALU.mult), inc=1)
                t_acc = S.op("gpsimd", lambda e, w=w: e.tensor_tensor(
                    out=lut_v[:, 0:w], in0=lut_v[:, 0:w],
                    in1=lut_t[:, 0:w], op=ALU.add), inc=1)
            t_mul = S.op("gpsimd", lambda e, w=w: e.tensor_tensor(
                out=wd_b[:, 0:w], in0=lut_v[:, 0:w], in1=a_repf[:, 0:w],
                op=ALU.mult),
                deps=[t_la, t_acc] + ([last["st"]] if last.get("st") else []),
                inc=1)
            pv["arep"] = t_mul
            last["vec"] = t_mul
            t_st = S.dma("gpsimd",
                         out_dram[a * 128:(a + 1) * 128, wsl],
                         wd_b[:, 0:w], deps=[t_mul], ch=5)
            last["st"] = t_st
        return last["st"]

    units = [(w2c, a2X, w2d, a, D, False) for a in range(NHT)]
    state = {"i": 0}

    def emit_units(k):
        last_t = None
        while k > 0 and state["i"] < len(units):
            last_t = emit_tile(*units[state["i"]])
            state["i"] += 1
            k -= 1
        return last_t

    def done_tokens():
        return [t for t in last.values() if t is not None]

    return emit_units, len(units), done_tokens


def phase_ffn1(nc, S, stk, IO):
    To = OWN_T
    NK = D // 128
    NHT = HSL // 128
    NTC = T // 512
    xg = IO.get('xg')
    w1c = IO.get('w1c')
    a1S = IO.get('a1S')
    w3c = IO.get('w3c')
    a3S = IO.get('a3S')
    w2c = IO.get('w2c')
    a2S = IO.get('a2S')
    x3T = IO.get('x3T')
    x4 = IO.get('x4')
    w1d = IO.get('w1d')
    w3d = IO.get('w3d')
    w2d = IO.get('w2d')
    pb = IO.get('pb')
    rs_out = IO.get('rs_out')
    xga = IO.get('xga')
    tok_pre = IO.get('tok_pre', [])
    def w1d_3(t):
        return t.rearrange("p (ht a hh) -> p ht a hh", ht=NHT, a=NK)

    _pfx = IO.get("pfx", "")
    sb = lambda name, shape, dt: stk.enter_context(
        nc.sbuf_tensor(_pfx + name, shape, dt))
    psu = lambda name, shape, dt=F32: stk.enter_context(
        nc.psum_tensor(_pfx + name, shape, dt))

    WQC = 2048
    code_i = sb("code_i", [128, WQC], I32)
    code_h = sb("code_h", [128, WQC], mybir.dt.float16)
    lut_v = sb("lut_v", [128, WQC], mybir.dt.float16)
    lut_t = sb("lut_t", [128, WQC], mybir.dt.float16)
    a_rep = sb("a_rep", [128, WQC], mybir.dt.float16)
    a_rf = sb("a_rf", [128, 512], F32)
    wd_b = sb("wd_b", [128, WQC], BF16)
    arow = sb("arow", [1, WQC], F32)
    xtc_a = sb("xtc_a", [128, NK * 512], BF16)
    xtc_b = sb("xtc_b", [128, NK * 512], BF16)
    hbuf_a = sb("hbuf_a", [128, NHT * 512], BF16)
    hbuf_b = sb("hbuf_b", [128, NHT * 512], BF16)
    w1s_a = sb("w1s_a", [128, NK * 128], BF16)
    w1s_b = sb("w1s_b", [128, NK * 128], BF16)
    w3s_a = sb("w3s_a", [128, NK * 128], BF16)
    w3s_b = sb("w3s_b", [128, NK * 128], BF16)
    w2t_a = sb("w2t_a", [128, 512], BF16)
    w2t_b = sb("w2t_b", [128, 512], BF16)
    sil = sb("sil", [128, 512], F32)
    hpt_a = sb("hpt_a", [128, 512], F32)
    hpt_b = sb("hpt_b", [128, 512], F32)
    ones1r = sb("ones1r", [1, 128], F32)
    idn_f = sb("idn_f", [128, 128], F32)
    otile = sb("otile", [128, D], F32)
    rst = sb("rst", [128, D], F32)
    x3blk = sb("x3blk", [128, 128], F32)

    ps_1 = psu("ps_1", [128, 512])
    ps_3 = psu("ps_3", [128, 512])
    ps_t = [psu(f"ps_t{i}", [128, 512]) for i in range(4)]
    ps_e = psu("ps_e", [128, 512])
    ps_f = psu("ps_f", [128, 128])


    t_ones = S.op("vector", lambda e: e.memset(ones1r[:], 1.0))
    t_id0 = S.op("gpsimd", lambda e: e.memset(idn_f[:], 0.0), inc=1)
    t_idn = S.op("gpsimd", lambda e: e.affine_select(
        out=idn_f[:], in_=idn_f[:], compare_op=ALU.not_equal, fill=1.0,
        base=0, pattern=[[-1, 128]], channel_multiplier=1),
        deps=[t_id0], inc=1)

    pv = {}

    def emit_nf4_lut(width, dep_load):
        t_cf = S.op("vector", lambda e: e.tensor_copy(
            out=code_h[:, 0:width], in_=code_i[:, 0:width]), deps=[dep_load])
        t_acc = S.op("vector", lambda e: e.tensor_scalar(
            out=lut_v[:, 0:width], in0=code_h[:, 0:width], scalar1=0.0,
            scalar2=float(NF4[0]), op0=ALU.is_equal, op1=ALU.mult),
            deps=[t_cf])
        for i in range(1, 16):
            t_i = S.op("vector", lambda e, i=i: e.tensor_scalar(
                out=lut_t[:, 0:width], in0=code_h[:, 0:width],
                scalar1=float(i), scalar2=float(NF4[i]), op0=ALU.is_equal,
                op1=ALU.mult))
            t_acc = S.op("vector", lambda e: e.tensor_tensor(
                out=lut_v[:, 0:width], in0=lut_v[:, 0:width],
                in1=lut_t[:, 0:width], op=ALU.add))
        return t_acc

    def emit_arep(a_dram, row0, wofs, width, dep):
        toks = []
        for half in range(2):
            t_ar = S.dma("gpsimd", arow[:, 0:width],
                         a_dram[row0 + half:row0 + half + 1,
                                wofs:wofs + width],
                         deps=dep + (toks[-2:] if toks else []), ch=2)
            for q in range((width + 511) // 512):
                sl = slice(q * 512, min((q + 1) * 512, width))
                w = sl.stop - sl.start
                t_bm = S.op("tensor", lambda e, sl=sl, w=w: e.matmul(
                    ps_e[0:64, 0:w], ones1r[0:1, 0:64], arow[0:1, sl],
                    start=True, stop=True),
                    deps=[t_ar, t_ones] + (toks[-1:] if toks else [])
                    + ([pv["ps_e"]] if pv.get("ps_e") else []))
                t_e1 = S.op("vector", lambda e, w=w: e.tensor_copy(
                    out=a_rf[0:64, 0:w], in_=ps_e[0:64, 0:w]), deps=[t_bm])
                t_ev = S.op("vector", lambda e, sl=sl, half=half, w=w:
                            e.tensor_copy(
                                out=a_rep[half * 64:(half + 1) * 64, sl],
                                in_=a_rf[0:64, 0:w]), deps=[t_e1])
                pv["ps_e"] = t_e1
                toks.append(t_bm)
                toks.append(t_ev)
        return toks[-1]

    def emit_dequant_tile(codes, a_dram, out_dram, a, width, tiled):
        last_st = pv.get("wst")
        for wh in range(0, width, WQC):
            wsl = slice(wh, min(wh + WQC, width))
            w = wsl.stop - wsl.start
            t_lc = S.dma("sync", code_i[:, 0:w],
                         codes[a * 128:(a + 1) * 128, wsl],
                         deps=[pv["code"]] if pv.get("code") else [], ch=0)
            t_lv = emit_nf4_lut(w, t_lc)
            pv["code"] = t_lv
            t_arep = emit_arep(a_dram, 2 * a, wh, w,
                               [pv["arep"]] if pv.get("arep") else [])
            t_mul = S.op("vector", lambda e, w=w: e.tensor_tensor(
                out=lut_v[:, 0:w], in0=lut_v[:, 0:w],
                in1=a_rep[:, 0:w], op=ALU.mult), deps=[t_arep, t_lv])
            pv["arep"] = t_mul
            t_cb = S.op("vector", lambda e, w=w: e.tensor_copy(
                out=wd_b[:, 0:w], in_=lut_v[:, 0:w]),
                deps=([last_st] if last_st else []))
            if tiled:
                for hti in range(w // 128):
                    ht = (wh + hti * 128) // 128
                    t_st = S.dma("gpsimd",
                                 w1d_3(out_dram)[:, ht, a, :],
                                 wd_b[:, hti * 128:(hti + 1) * 128],
                                 deps=[t_cb], ch=0)
            else:
                t_st = S.dma("gpsimd",
                             out_dram[a * 128:(a + 1) * 128, wsl],
                             wd_b[:, 0:w], deps=[t_cb], ch=0)
            last_st = t_st
            pv["wst"] = t_st
        return last_st

    def emit_dequant(codes, a_dram, out_dram, ntile, width, tiled):
        last = None
        for a in range(ntile):
            last = emit_dequant_tile(codes, a_dram, out_dram, a, width, tiled)
        return last

    if IO.get("dq_external") == "all":
        tok_deq = list(IO.get("tok_pre", []))
    elif IO.get("dq_external") == "w2":
        t_d1 = emit_dequant(w1c, a1S, w1d, NK, HSL, True)
        t_d3 = emit_dequant(w3c, a3S, w3d, NK, HSL, True)
        tok_deq = [t_d1, t_d3] + list(IO.get("tok_pre", []))
    else:
        t_d1 = emit_dequant(w1c, a1S, w1d, NK, HSL, True)
        t_d3 = emit_dequant(w3c, a3S, w3d, NK, HSL, True)
        t_d2 = emit_dequant(w2c, a2S, w2d, NHT, D, False)
        tok_deq = [t_d1, t_d3, t_d2]

    pm = {}
    t_pb_last = None
    for tc in range(NTC):
        xtc = xtc_a if tc % 2 == 0 else xtc_b
        hbuf = hbuf_a if tc % 2 == 0 else hbuf_b
        tsl = slice(tc * 512, (tc + 1) * 512)
        xw = [pm[f"xr{tc % 2}"]] if pm.get(f"xr{tc % 2}") else []
        t_x = None
        t_xs = []
        if xga is None:
            for a2 in range(2):
                t_xs.append(S.dma("sync",
                            xtc[:].rearrange("p (a t) -> p a t", a=NK)
                            [:, a2 * 16:(a2 + 1) * 16, :],
                            xg.rearrange("(a p) t -> p a t", p=128)
                            [:, a2 * 16:(a2 + 1) * 16, tsl],
                            deps=xw, ch=1 + 7 * a2))
        else:
            cpr = To // 512
            lsl = slice((tc % cpr) * 512, (tc % cpr) * 512 + 512)
            for a2 in range(2):
                t_xs.append(S.dma("sync",
                            xtc[:].rearrange("p (a t) -> p a t", a=NK)
                            [:, a2 * 16:(a2 + 1) * 16, :],
                            xga[tc // cpr].rearrange("(a p) t -> p a t", p=128)
                            [:, a2 * 16:(a2 + 1) * 16, lsl],
                            deps=xw + list(tok_pre), ch=1 + 7 * a2))
        t_x = t_xs
        for ht in range(NHT):
            w1b = w1s_a if ht % 2 == 0 else w1s_b
            w3b = w3s_a if ht % 2 == 0 else w3s_b
            war1 = [pm[f"mm1{ht % 2}"]] if pm.get(f"mm1{ht % 2}") else tok_deq
            t_w1 = S.dma("sync", w1b[:], w1d[:, ht * NK * 128:(ht + 1) * NK * 128],
                         deps=war1, ch=2 + ht % 2)
            t_w3 = S.dma("sync", w3b[:], w3d[:, ht * NK * 128:(ht + 1) * NK * 128],
                         deps=war1, ch=6 + ht % 2)
            tm1 = tm3 = None
            for a in range(NK):
                tm1 = S.op("tensor", lambda e, a=a, w=w1b, xtc=xtc: e.matmul(
                    ps_1[:], w[:, a * 128:(a + 1) * 128],
                    xtc[:, a * 512:(a + 1) * 512],
                    start=(a == 0), stop=(a == NK - 1)),
                    deps=[t_w1] + t_x + ([pm["ev1"]] if pm.get("ev1") else []))
            for a in range(NK):
                tm3 = S.op("tensor", lambda e, a=a, w=w3b, xtc=xtc: e.matmul(
                    ps_3[:], w[:, a * 128:(a + 1) * 128],
                    xtc[:, a * 512:(a + 1) * 512],
                    start=(a == 0), stop=(a == NK - 1)),
                    deps=[t_w3] + t_x + ([pm["ev3"]] if pm.get("ev3") else []))
            pm[f"mm1{ht % 2}"] = tm3
            t_si = S.op("scalar", lambda e: e.activation(
                out=sil[:], in_=ps_1[:], func=AF.Silu),
                deps=[tm1] + ([pm["sil"]] if pm.get("sil") else []))
            pm["ev1"] = t_si
            t_h = S.op("vector", lambda e, ht=ht, hbuf=hbuf: e.tensor_tensor(
                out=hbuf[:, ht * 512:(ht + 1) * 512], in0=sil[:],
                in1=ps_3[:], op=ALU.mult), deps=[t_si, tm3]
                + ([pm[f"hb{tc % 2}"]] if pm.get(f"hb{tc % 2}") else []))
            pm["ev3"] = t_h
            pm["sil"] = t_h
            pm["hlast"] = t_h
        for dc in range(D // 512):
            dsl = slice(dc * 512, (dc + 1) * 512)
            tmm = None
            for ht in range(NHT):
                wb2 = w2t_a if ht % 2 == 0 else w2t_b
                t_w2 = S.dma("sync", wb2[:],
                             w2d[ht * 128:(ht + 1) * 128, dsl],
                             deps=[pm[f"mm2{ht % 2}"]]
                             if pm.get(f"mm2{ht % 2}") else tok_deq,
                             ch=4 + ht % 2)
                for ts4 in range(4):
                    tmm = S.op("tensor", lambda e, ht=ht, ts4=ts4, w=wb2,
                               hbuf=hbuf:
                               e.matmul(ps_t[ts4][:],
                                        hbuf[:, ht * 512 + ts4 * 128:
                                             ht * 512 + (ts4 + 1) * 128],
                                        w[:], start=(ht == 0),
                                        stop=(ht == NHT - 1)),
                               deps=[t_w2, pm["hlast"]]
                               + ([pm[f"pt{ts4}"]] if pm.get(f"pt{ts4}")
                                  else []))
                pm[f"mm2{ht % 2}"] = tmm
            for ts4 in range(4):
                hb = hpt_a if ts4 % 2 == 0 else hpt_b
                t_ev = S.op("vector", lambda e, ts4=ts4, hb=hb: e.tensor_copy(
                    out=hb[:], in_=ps_t[ts4][:]), deps=[tmm]
                    + ([pm[f"hpt{ts4 % 2}"]] if pm.get(f"hpt{ts4 % 2}")
                       else []))
                t_sp = S.dma("gpsimd",
                             pb[tc * 512 + ts4 * 128:
                                tc * 512 + (ts4 + 1) * 128, dsl],
                             hb[:], deps=[t_ev], ch=1 + 4 * (ts4 % 2))
                pm[f"pt{ts4}"] = t_ev
                pm[f"hpt{ts4 % 2}"] = t_sp
                pm[f"pbl{ts4 % 2}"] = t_sp
                t_pb_last = t_sp
        pm[f"xr{tc % 2}"] = tmm
        pm[f"hb{tc % 2}"] = tmm

    t_rs = S.cc("ReduceScatter", ALU.add, [pb.opt()], [rs_out.opt()],
                deps=[pm["pbl0"], pm["pbl1"]])

    pvf = {}
    for tt in range(To // 128):
        t_ev = None
        for a in range(NK):
            t_lx = S.dma("sync", x3blk[:],
                         x3T[a * 128:(a + 1) * 128, tt * 128:(tt + 1) * 128],
                         deps=[pvf["tp"]] if pvf.get("tp") else [], ch=0)
            t_tp = S.op("tensor", lambda e: e.transpose(
                ps_f[:], x3blk[:], idn_f[:]),
                deps=[t_lx, t_idn] + ([pvf["tev"]] if pvf.get("tev") else []))
            pvf["tp"] = t_tp
            t_ev = S.op("vector", lambda e, a=a: e.tensor_copy(
                out=otile[:, a * 128:(a + 1) * 128], in_=ps_f[:]),
                deps=[t_tp] + ([pvf["ost"]] if a == 0 and pvf.get("ost")
                               else []))
            pvf["tev"] = t_ev
        t_lr = S.dma("sync", rst[:],
                     rs_out.rearrange("(b p) d -> b p d", p=128)[tt],
                     deps=[t_rs] + ([pvf["rs"]] if pvf.get("rs") else []),
                     ch=2)
        t_ad = S.op("vector", lambda e: e.tensor_tensor(
            out=otile[:], in0=otile[:], in1=rst[:], op=ALU.add),
            deps=[t_ev, t_lr])
        pvf["rs"] = t_ad
        t_st = S.dma("gpsimd",
                     x4.rearrange("(b p) d -> b p d", p=128)[tt],
                     otile[:], deps=[t_ad], ch=3)
        pvf["ost"] = t_st
    S.op("gpsimd", lambda e: e.engine_nop(), deps=[pvf["ost"]], inc=1)

    return {'x4': pvf['ost']}


def build_ffn1():
    nc = bass.Bass("TRN2", target_bir_lowering=False)
    To = OWN_T
    NK = D // 128
    NHT = HSL // 128
    NTC = T // 512

    xg = nc.dram_tensor("xg", [D, T], BF16, kind="ExternalInput").ap()
    w1c = nc.dram_tensor("w1c", [D, HSL], I32, kind="ExternalInput").ap()
    a1S = nc.dram_tensor("a1S", [64, HSL], F32, kind="ExternalInput").ap()
    w3c = nc.dram_tensor("w3c", [D, HSL], I32, kind="ExternalInput").ap()
    a3S = nc.dram_tensor("a3S", [64, HSL], F32, kind="ExternalInput").ap()
    w2c = nc.dram_tensor("w2c", [HSL, D], I32, kind="ExternalInput").ap()
    a2S = nc.dram_tensor("a2S", [22, D], F32, kind="ExternalInput").ap()
    x3T = nc.dram_tensor("x3T", [D, To], F32, kind="ExternalInput").ap()

    x4 = nc.dram_tensor("x4", [To, D], F32, kind="ExternalOutput").ap()

    w1d = nc.dram_tensor("w1d", [128, NHT * NK * 128], BF16, kind="Internal").ap()
    w3d = nc.dram_tensor("w3d", [128, NHT * NK * 128], BF16, kind="Internal").ap()
    w2d = nc.dram_tensor("w2d", [HSL, D], BF16, kind="Internal").ap()
    pb = nc.dram_tensor("pb", [T, D], F32, kind="Internal").ap()
    rs_out = nc.dram_tensor("rs_out", [To, D], F32, kind="Internal").ap()

    with contextlib.ExitStack() as stk:
        S = Sched(nc, stk)
        IO = dict(xg=xg, w1c=w1c, a1S=a1S, w3c=w3c, a3S=a3S, w2c=w2c, a2S=a2S,
                  x3T=x3T, x4=x4, w1d=w1d, w3d=w3d, w2d=w2d, pb=pb,
                  rs_out=rs_out)
        with contextlib.ExitStack() as pstk:
            phase_ffn1(nc, S, pstk, IO)
        S.emit(block=stk.enter_context(nc.Block()))
    return nc


def build_fused():
    nc = bass.Bass("TRN2", target_bir_lowering=False)
    To = OWN_T
    TS = TS_M

    inp = lambda n, shp, dt: nc.dram_tensor(n, shp, dt, kind="ExternalInput").ap()
    itn = lambda n, shp, dt: nc.dram_tensor(n, shp, dt, kind="Internal").ap()

    x0d = inp("x0d", [T, DSL], F32)
    x0dT = inp("x0dT", [DSL, T], F32)
    alog = inp("alog", [128, NDT * NST], F32)
    dvec = inp("dvec", [128, NDT], F32)
    xpwS = inp("xpwS", [128, NDT * 160], F32)
    dtwT = inp("dtwT", [DTR, DSL], F32)
    dtb = inp("dtb", [128, NDT], F32)
    wqT = inp("wqT", [D, D], I32)
    scaleT = inp("scaleT", [32, D], F32)
    zeroT = inp("zeroT", [32, D], F32)
    bias0 = inp("bias0", [1, D], F32)
    wkvT = inp("wkvT", [D, 3 * PART], F32)
    bkv_c = inp("bkv_c", [128, 48], F32)
    x1idx = inp("x1idx", [128, D // 128], I32)
    woT = inp("woT", [PART, D], F32)
    bo_c = inp("bo_c", [128, 32], F32)
    qpos = inp("qpos", [128, To // 128], F32)
    w1c = inp("w1c", [D, HSL], I32)
    a1S = inp("a1S", [64, HSL], F32)
    w3c = inp("w3c", [D, HSL], I32)
    a3S = inp("a3S", [64, HSL], F32)
    w2c = inp("w2c", [HSL, D], I32)
    a2X = inp("a2X", [HSL, D], F32)

    x4 = nc.dram_tensor("x4", [To, D], F32, kind="ExternalOutput").ap()

    st_in = itn("st_in", [3, T], F32)
    st_out = itn("st_out", [3, T], F32)
    st_scr = itn("st_scr", [2, T], F32)
    xp_ins = [itn(f"xp_in{i}", [160, TS], F32) for i in range(T // TS)]
    xp_outs = [itn(f"xp_out{i}", [160, TS], F32) for i in range(T // TS)]
    x1d = itn("x1d", [DSL, T], F32)
    x1g = itn("x1g", [D, T], F32)
    wdeq = itn("wdeq", [128, 32 * 32 * 128], BF16)
    wkvb = itn("wkvb", [128, 48 * 32 * 128], BF16)
    b0scr = itn("b0scr", [1, D], BF16)
    x2Ti = itn("x2Ti", [D, To], F32)
    qdi = itn("qdi", [PART, To], BF16)
    kdi = itn("kdi", [PART, To], BF16)
    vti = itn("vti", [To, PART], BF16)
    kga = itn("kga", [N_CORES, PART, To], BF16)
    vga = itn("vga", [N_CORES, To, PART], BF16)
    x3scr = itn("x3scr", [D, To], F32)
    xn3i = itn("xn3i", [D, To], BF16)
    xga = itn("xga", [N_CORES, D, To], BF16)
    w1d = itn("w1d", [128, (HSL // 128) * 32 * 128], BF16)
    w3d = itn("w3d", [128, (HSL // 128) * 32 * 128], BF16)
    w2d = itn("w2d", [HSL, D], BF16)
    pb = itn("pb", [T, D], F32)
    rs_out = itn("rs_out", [To, D], F32)

    with contextlib.ExitStack() as stk:
        S = Sched(nc, stk)

        IO1 = dict(pfx="pA_", x0d=x0d, x0dT=x0dT, alog=alog, dvec=dvec, xpwS=xpwS,
                   dtwT=dtwT, dtb=dtb, x1d=x1d, st_in=st_in, st_out=st_out,
                   st_scr=st_scr, xp_ins=xp_ins, xp_outs=xp_outs)
        with contextlib.ExitStack() as p1:
            r1 = phase_mamba(nc, S, p1, IO1)
        t_ag1 = S.cc("AllGather", ALU.bypass, [x1d.opt()], [x1g.opt()],
                     deps=r1["x1d_stores"])
        S.barrier()

        IO2 = dict(pfx="pB_", x1T=None, wqT=wqT, scaleT=scaleT, zeroT=zeroT, bias0=bias0,
                   wkvT=wkvT, bkv_c=bkv_c, x2T=x2Ti, qd=qdi, kd=kdi, vt=vti,
                   wdeq=wdeq, wkvb=wkvb, b0scr=b0scr,
                   x1idx=x1idx, x1v=x1g.rearrange("d (b t) -> (d b) t",
                                                  b=N_CORES),
                   tok_pre=[t_ag1])
        with contextlib.ExitStack() as p2:
            r2 = phase_ffn0(nc, S, p2, IO2)
        t_ag2 = S.cc("AllGather", ALU.bypass, [kdi.opt()], [kga.opt()],
                     deps=r2["kv"] + [r2["x2"]])
        t_ag3 = S.cc("AllGather", ALU.bypass, [vti.opt()], [vga.opt()],
                     deps=r2["kv"])
        S.barrier()

        IO3 = dict(pfx="pC_", qd=qdi, kg=None, vg=None, woT=woT, bo_c=bo_c, x2T=x2Ti,
                   qpos=qpos, x3T=None, xn3T=xn3i, x3scr=x3scr,
                   kga=kga, vga=vga, tok_pre=[t_ag2, t_ag3])
        IO3["dq_io"] = dict(w2c=w2c, a2X=a2X, w2d=w2d)
        with contextlib.ExitStack() as p3:
            r3 = phase_attn(nc, S, p3, IO3)
        t_ag4 = S.cc("AllGather", ALU.bypass, [xn3i.opt()], [xga.opt()],
                     deps=[r3["xn3"]])
        S.barrier()

        IO4 = dict(pfx="pD_", xg=None, w1c=w1c, a1S=a1S, w3c=w3c, a3S=a3S,
                   w2c=w2c, a2S=None, x3T=x3scr, x4=x4, w1d=w1d, w3d=w3d,
                   w2d=w2d, pb=pb, rs_out=rs_out, xga=xga, tok_pre=[t_ag4],
                   dq_external="w2")
        with contextlib.ExitStack() as p4:
            r4 = phase_ffn1(nc, S, p4, IO4)

        S.emit(block=stk.enter_context(nc.Block()))
    return nc


_NCS = {}


def _nc(name):
    if name not in _NCS:
        _NCS[name] = {"mamba": build_mamba, "ffn0": build_ffn0_qkv,
                      "attn": build_attn, "ffn1": build_ffn1,
                      "fused": build_fused}[name]()
    return _NCS[name]


def _run(name, in_maps):
    res = run_bass_kernel_spmd(_nc(name), in_maps, core_ids=list(range(N_CORES)))
    return res.results


def run_mamba(L0_attn_x, L0_attn_A_log, L0_attn_D, L0_attn_x_proj_w,
              L0_attn_dt_proj_w, L0_attn_dt_proj_b):
    x = np.ascontiguousarray(np.asarray(L0_attn_x).reshape(T, D))
    xT = np.ascontiguousarray(x.T)
    xpwT = np.ascontiguousarray(np.asarray(L0_attn_x_proj_w).T)
    dtwT = np.ascontiguousarray(np.asarray(L0_attn_dt_proj_w).T)
    A_log = np.asarray(L0_attn_A_log)
    Dv = np.asarray(L0_attn_D)
    dtb = np.asarray(L0_attn_dt_proj_b)
    in_maps = []
    for c in range(N_CORES):
        dsl = slice(DSL * c, DSL * (c + 1))
        in_maps.append({
            "x0d": np.ascontiguousarray(x[:, dsl]),
            "x0dT": np.ascontiguousarray(xT[dsl, :]),
            "alog": np.ascontiguousarray(
                A_log[dsl, :].reshape(NDT, 128, NST)
                .transpose(1, 0, 2).reshape(128, NDT * NST)),
            "dvec": np.ascontiguousarray(Dv[dsl].reshape(NDT, 128).T),
            "xpwS": np.ascontiguousarray(
                xpwT[dsl, :].reshape(NDT, 128, 160)
                .transpose(1, 0, 2).reshape(128, NDT * 160)),
            "dtwT": np.ascontiguousarray(dtwT[:, dsl]),
            "dtb": np.ascontiguousarray(dtb[dsl].reshape(NDT, 128).T),
        })
    res = _run("mamba", in_maps)
    return np.concatenate([r["x1d"] for r in res], axis=0)


def kernel_multi(**inputs):
    f32 = lambda a: np.ascontiguousarray(np.asarray(a, dtype=np.float32))

    x1 = run_mamba(inputs["L0_attn_x"], inputs["L0_attn_A_log"],
                   inputs["L0_attn_D"], inputs["L0_attn_x_proj_w"],
                   inputs["L0_attn_dt_proj_w"], inputs["L0_attn_dt_proj_b"])

    wqT = np.ascontiguousarray(np.asarray(inputs["L0_ffn_W_q"]).T)
    scaleT = f32(np.asarray(inputs["L0_ffn_scale"]).T)
    zeroT = f32(np.asarray(inputs["L0_ffn_zero"]).T)
    bias0 = f32(inputs["L0_ffn_bias"]).reshape(1, D)
    wkvT = np.ascontiguousarray(f32(inputs["L1_attn_w_qkv"]).T)
    bkv_c = np.ascontiguousarray(f32(inputs["L1_attn_b_qkv"]).reshape(48, 128).T)
    in_maps = []
    for c in range(N_CORES):
        tsl = slice(OWN_T * c, OWN_T * (c + 1))
        in_maps.append({
            "x1T": np.ascontiguousarray(x1[:, tsl]),
            "wqT": wqT, "scaleT": scaleT, "zeroT": zeroT, "bias0": bias0,
            "wkvT": wkvT, "bkv_c": bkv_c,
        })
    resB = _run("ffn0", in_maps)
    x2s = [r["x2T"] for r in resB]
    kg = np.concatenate([r["kd"] for r in resB], axis=1)
    vg = np.concatenate([r["vt"] for r in resB], axis=0)

    woT = np.ascontiguousarray(f32(inputs["L1_attn_w_o"]).T)
    bo_c = np.ascontiguousarray(f32(inputs["L1_attn_b_o"]).reshape(32, 128).T)
    in_maps = []
    for c in range(N_CORES):
        qpos = (c * OWN_T + np.arange(OWN_T).reshape(OWN_T // 128, 128).T
                ).astype(np.float32)
        in_maps.append({
            "qd": resB[c]["qd"], "kg": kg, "vg": vg, "woT": woT, "bo_c": bo_c,
            "x2T": x2s[c], "qpos": np.ascontiguousarray(qpos),
        })
    resC = _run("attn", in_maps)
    xg = np.concatenate([r["xn3T"] for r in resC], axis=1)

    HP = N_CORES * HSL
    w1T = np.asarray(inputs["L1_ffn_w1_codes"]).T
    w3T = np.asarray(inputs["L1_ffn_w3_codes"]).T
    w2T = np.asarray(inputs["L1_ffn_w2_codes"]).T
    w1Tp = np.full((D, HP), 7, np.int32); w1Tp[:, :HID] = w1T
    w3Tp = np.full((D, HP), 7, np.int32); w3Tp[:, :HID] = w3T
    w2Tp = np.full((HP, D), 7, np.int32); w2Tp[:HID, :] = w2T
    a1Tp = np.zeros((64, HP), np.float32)
    a1Tp[:, :HID] = np.asarray(inputs["L1_ffn_w1_absmax"]).T
    a3Tp = np.zeros((64, HP), np.float32)
    a3Tp[:, :HID] = np.asarray(inputs["L1_ffn_w3_absmax"]).T
    a2p = np.zeros((HP // 64, D), np.float32)
    a2p[:HID // 64, :] = np.asarray(inputs["L1_ffn_w2_absmax"]).T
    in_maps = []
    for c in range(N_CORES):
        hs = slice(c * HSL, (c + 1) * HSL)
        in_maps.append({
            "xg": xg,
            "w1c": np.ascontiguousarray(w1Tp[:, hs]),
            "a1S": np.ascontiguousarray(a1Tp[:, hs]),
            "w3c": np.ascontiguousarray(w3Tp[:, hs]),
            "a3S": np.ascontiguousarray(a3Tp[:, hs]),
            "w2c": np.ascontiguousarray(w2Tp[hs, :]),
            "a2S": np.ascontiguousarray(a2p[c * 22:(c + 1) * 22, :]),
            "x3T": resC[c]["x3T"],
        })
    resD = _run("ffn1", in_maps)
    x4 = np.concatenate([r["x4"] for r in resD], axis=0)
    return x4.reshape(1, T, D).astype(np.float32)


def _fused_in_maps(inputs):
    f32 = lambda a: np.ascontiguousarray(np.asarray(a, dtype=np.float32))
    x = np.ascontiguousarray(np.asarray(inputs["L0_attn_x"]).reshape(T, D))
    xT = np.ascontiguousarray(x.T)
    xpwT = np.ascontiguousarray(np.asarray(inputs["L0_attn_x_proj_w"]).T)
    dtwT = np.ascontiguousarray(np.asarray(inputs["L0_attn_dt_proj_w"]).T)
    A_log = np.asarray(inputs["L0_attn_A_log"])
    Dv = np.asarray(inputs["L0_attn_D"])
    dtb = np.asarray(inputs["L0_attn_dt_proj_b"])
    wqT = np.ascontiguousarray(np.asarray(inputs["L0_ffn_W_q"]).T)
    scaleT = f32(np.asarray(inputs["L0_ffn_scale"]).T)
    zeroT = f32(np.asarray(inputs["L0_ffn_zero"]).T)
    bias0 = f32(inputs["L0_ffn_bias"]).reshape(1, D)
    wkvT = np.ascontiguousarray(f32(inputs["L1_attn_w_qkv"]).T)
    bkv_c = np.ascontiguousarray(f32(inputs["L1_attn_b_qkv"]).reshape(48, 128).T)
    woT = np.ascontiguousarray(f32(inputs["L1_attn_w_o"]).T)
    bo_c = np.ascontiguousarray(f32(inputs["L1_attn_b_o"]).reshape(32, 128).T)
    HP = N_CORES * HSL
    w1T = np.asarray(inputs["L1_ffn_w1_codes"]).T
    w3T = np.asarray(inputs["L1_ffn_w3_codes"]).T
    w2T = np.asarray(inputs["L1_ffn_w2_codes"]).T
    w1Tp = np.full((D, HP), 7, np.int32); w1Tp[:, :HID] = w1T
    w3Tp = np.full((D, HP), 7, np.int32); w3Tp[:, :HID] = w3T
    w2Tp = np.full((HP, D), 7, np.int32); w2Tp[:HID, :] = w2T
    a1Tp = np.zeros((64, HP), np.float32)
    a1Tp[:, :HID] = np.asarray(inputs["L1_ffn_w1_absmax"]).T
    a3Tp = np.zeros((64, HP), np.float32)
    a3Tp[:, :HID] = np.asarray(inputs["L1_ffn_w3_absmax"]).T
    a2p = np.zeros((HP // 64, D), np.float32)
    a2p[:HID // 64, :] = np.asarray(inputs["L1_ffn_w2_absmax"]).T

    in_maps = []
    for c in range(N_CORES):
        dsl = slice(DSL * c, DSL * (c + 1))
        hs = slice(c * HSL, (c + 1) * HSL)
        p = np.arange(128)
        g = np.arange(D // 128)
        x1idx = ((g[None, :] * 128 + p[:, None]) * N_CORES + c).astype(np.int32)
        qpos = (c * OWN_T + np.arange(OWN_T).reshape(OWN_T // 128, 128).T
                ).astype(np.float32)
        in_maps.append({
            "x0d": np.ascontiguousarray(x[:, dsl]),
            "x0dT": np.ascontiguousarray(xT[dsl, :]),
            "alog": np.ascontiguousarray(
                A_log[dsl, :].reshape(NDT, 128, NST)
                .transpose(1, 0, 2).reshape(128, NDT * NST)),
            "dvec": np.ascontiguousarray(Dv[dsl].reshape(NDT, 128).T),
            "xpwS": np.ascontiguousarray(
                xpwT[dsl, :].reshape(NDT, 128, 160)
                .transpose(1, 0, 2).reshape(128, NDT * 160)),
            "dtwT": np.ascontiguousarray(dtwT[:, dsl]),
            "dtb": np.ascontiguousarray(dtb[dsl].reshape(NDT, 128).T),
            "wqT": wqT, "scaleT": scaleT, "zeroT": zeroT, "bias0": bias0,
            "wkvT": wkvT, "bkv_c": bkv_c,
            "x1idx": np.ascontiguousarray(x1idx),
            "woT": woT, "bo_c": bo_c, "qpos": np.ascontiguousarray(qpos),
            "w1c": np.ascontiguousarray(w1Tp[:, hs]),
            "a1S": np.ascontiguousarray(a1Tp[:, hs]),
            "w3c": np.ascontiguousarray(w3Tp[:, hs]),
            "a3S": np.ascontiguousarray(a3Tp[:, hs]),
            "w2c": np.ascontiguousarray(w2Tp[hs, :]),
            "a2X": np.repeat(
                np.ascontiguousarray(a2p[c * 22:(c + 1) * 22, :]), 64, axis=0),
        })
    return in_maps


def kernel(**inputs):
    in_maps = _fused_in_maps(inputs)
    res = _run("fused", in_maps)
    x4 = np.concatenate([r["x4"] for r in res.results], axis=0) \
        if hasattr(res, "results") else np.concatenate(
            [r["x4"] for r in res], axis=0)
    return x4.reshape(1, T, D).astype(np.float32)


# version 09
# speedup vs baseline: 1.5716x; 1.0911x over previous
import contextlib
import numpy as np
import ml_dtypes
import concourse.bass as bass
import concourse.mybir as mybir
from concourse.bass_utils import run_bass_kernel_spmd

F32 = mybir.dt.float32
BF16 = mybir.dt.bfloat16
I32 = mybir.dt.int32
AF = mybir.ActivationFunctionType
ALU = mybir.AluOpType
AX = mybir.AxisListType
NP_BF16 = ml_dtypes.bfloat16

N_CORES = 8
RG = [list(range(N_CORES))]
D = 4096
T = 8192
NST = 16
DTR = 128
DSL = D // N_CORES
HEADS = 8
HDIM = 256
PART = 2048
HID = 11008
OWN_T = T // N_CORES
LN_EPS = 1e-5

NF4 = [-1.0, -0.6961928009986877, -0.5250730514526367, -0.39491748809814453,
       -0.28444138169288635, -0.18477343022823334, -0.09105003625154495, 0.0,
       0.07958029955625534, 0.16093020141124725, 0.24611230194568634,
       0.33791524171829224, 0.44070982933044434, 0.5626170039176941,
       0.6989939212799072, 1.0]


class Sched:
    ENGS = ("sync", "gpsimd", "vector", "scalar", "tensor")

    def __init__(self, nc, stack):
        self.nc = nc
        self.stack = stack
        self.ops = {e: [] for e in self.ENGS}
        self.sem = {e: stack.enter_context(nc.semaphore(f"sem_{e}"))
                    for e in self.ENGS}
        self.cnt = {e: 0 for e in self.ENGS}
        self.dsem = {}
        self.dcnt = {}
        self.ccsem = stack.enter_context(nc.semaphore("sem_cc"))
        self.cccnt = 0

    extra = ()

    def op(self, eng, fn, deps=(), inc=None, tag=""):
        if inc is None:
            inc = 16 if eng in ("sync", "gpsimd") else 1
        self.cnt[eng] += inc
        tok = (self.sem[eng], self.cnt[eng])
        self.ops[eng].append((fn, tuple(deps) + tuple(self.extra), tok, inc, tag))
        return tok

    def dma(self, eng, out, in_, deps=(), tag="", ch=0):
        key = (eng, ch)
        if key not in self.dsem:
            self.dsem[key] = self.stack.enter_context(
                self.nc.semaphore(f"dma_{eng}{ch}"))
            self.dcnt[key] = 0
        self.dcnt[key] += 16
        tok = (self.dsem[key], self.dcnt[key])
        self.ops[eng].append(
            (lambda e: e.dma_start(out=out, in_=in_),
             tuple(deps) + tuple(self.extra), tok, 16, tag))
        return tok

    def idma(self, out, in_view, idx_ap, deps=(), tag="", ch=0, element_offset=0):
        import bass_rust as _br
        key = ("gpsimd", ch)
        if key not in self.dsem:
            self.dsem[key] = self.stack.enter_context(
                self.nc.semaphore(f"dma_gpsimd{ch}"))
            self.dcnt[key] = 0
        self.dcnt[key] += 16
        tok = (self.dsem[key], self.dcnt[key])

        def fn(e, out=out, in_view=in_view, idx_ap=idx_ap, eo=element_offset):
            return e.indirect_dma_start(
                out=out, out_offset=None, in_=in_view,
                in_offset=_br.IndirectOffsetOnAxis(idx_ap, 0),
                element_offset=eo)
        self.ops["gpsimd"].append((fn, tuple(deps) + tuple(self.extra), tok,
                                   16, tag))
        return tok

    def barrier(self):
        toks = []
        for e in self.ENGS:
            if self.cnt[e]:
                toks.append((self.sem[e], self.cnt[e]))
        for k, sem in self.dsem.items():
            toks.append((sem, self.dcnt[k]))
        if self.cccnt:
            toks.append((self.ccsem, self.cccnt))
        nops = []
        for e in ("gpsimd", "vector", "scalar", "tensor"):
            nops.append(self.op(e, lambda eng: eng.nop(), deps=toks, inc=1))
        self.extra = tuple(nops)
        return nops

    def cc(self, kind, alu, ins, outs, deps=(), tag=""):
        self.cccnt += 1
        tok = (self.ccsem, self.cccnt)

        def fn(e):
            return e.collective_compute(kind, alu, replica_groups=RG,
                                        ins=ins, outs=outs)
        self.ops["gpsimd"].append((fn, tuple(deps) + tuple(self.extra), tok,
                                   None, tag))
        return tok

    def emit(self, block):
        ccsem = self.ccsem
        for eng_name in self.ENGS:
            ops = self.ops[eng_name]
            if not ops:
                continue

            self_serial = eng_name in ("vector", "scalar", "gpsimd")
            esem = self.sem[eng_name]

            def body(eng, ops=ops, self_serial=self_serial, esem=esem):
                waited = {}
                last_compute = None
                for fn, deps, tok, inc, tag in ops:
                    is_dma = tok[0] in self.dsem.values()
                    deps = list(deps)
                    if self_serial and not is_dma and last_compute is not None:
                        deps.append(last_compute)
                    best = {}
                    for (sem, val) in deps:
                        k = id(sem)
                        if best.get(k, (None, -1))[1] < val:
                            best[k] = (sem, val)
                    for (sem, val) in best.values():
                        key = id(sem)
                        if waited.get(key, -1) < val:
                            eng.wait_ge(sem, val)
                            waited[key] = val
                    inst = fn(eng)
                    if isinstance(inst, (list, tuple)):
                        inst = inst[-1]
                    if tok[0] is ccsem:
                        inst.then_inc(tok[0])
                    else:
                        inst.then_inc(tok[0], inc)
                    if not is_dma and tok[0] is esem:
                        last_compute = tok
            getattr(block, eng_name)(body)


def inner_bcast(ap, width):
    a = ap.ap
    assert a[-1][1] == 1, a
    return bass.AP(tensor=ap.tensor, offset=ap.offset,
                   ap=list(a[:-1]) + [[0, width]])


def mid_bcast(ap2d, n_mid):
    return bass.AP(tensor=ap2d.tensor, offset=ap2d.offset,
                   ap=[ap2d.ap[0], [0, n_mid], ap2d.ap[1]])


def emit_identity(S, idn):
    t0 = S.op("gpsimd", lambda e: e.memset(idn[:], 0.0), inc=1)
    return S.op("gpsimd", lambda e: e.affine_select(
        out=idn[:], in_=idn[:], compare_op=ALU.not_equal, fill=1.0,
        base=0, pattern=[[-1, idn.shape[1]]], channel_multiplier=1),
        deps=[t0], inc=1)


TS_M = 1024
NSUB_M = T // TS_M
NDT = DSL // 128


def phase_mamba(nc, S, stk, IO):
    TS = TS_M
    TB = T // 128
    NSUB = T // TS
    x0d = IO['x0d']
    x0dT = IO['x0dT']
    alog = IO['alog']
    dvec = IO['dvec']
    xpwS = IO['xpwS']
    dtwT = IO['dtwT']
    dtb = IO['dtb']
    x1d = IO['x1d']
    st_in = IO['st_in']
    st_out = IO['st_out']
    st_scr = IO['st_scr']
    xp_ins = IO['xp_ins']
    xp_outs = IO['xp_outs']
    _pfx = IO.get("pfx", "")
    sb = lambda name, shape, dt: stk.enter_context(
        nc.sbuf_tensor(_pfx + name, shape, dt))
    psu = lambda name, shape: stk.enter_context(
        nc.psum_tensor(_pfx + name, shape, F32))

    xps_f = sb("xps_f", [128, 4 * 160], F32)
    xps_b = sb("xps_b", [128, 4 * 160], BF16)
    dtw_b = sb("dtw_b", [128, NDT * 128], BF16)
    a4 = sb("a4", [128, NDT * NST], F32)
    d4 = sb("d4", [128, NDT], F32)
    dtb4 = sb("dtb4", [128, NDT], F32)
    onesc = sb("onesc", [128, 1], F32)
    ones1r = sb("ones1r", [1, 128], F32)
    state = sb("state", [128, NDT * NST], F32)
    idn = sb("idn", [128, 128], F32)
    stat_a = sb("stat_a", [128, TB], F32)
    stat_b = sb("stat_b", [128, TB], F32)
    stat_t = sb("stat_t", [TB, 128], F32)
    sa_t = sb("sa_t", [TB, 128], F32)
    sb_t = sb("sb_t", [TB, 128], F32)
    m64 = sb("m64", [TB, 128], F32)
    r64 = sb("r64", [TB, 128], F32)
    t64 = sb("t64", [TB, 128], F32)
    wsrow = sb("wsrow", [1, 160], F32)
    wcolA = sb("wcolA", [128, 1], F32)
    wcolB = sb("wcolB", [16, 1], F32)
    wcolC = sb("wcolC", [16, 1], F32)
    ceps = sb("ceps", [128, 1], F32)
    cone = sb("cone", [128, 1], F32)
    cneg1 = sb("cneg1", [128, 1], F32)
    mrow_s = sb("mrow_s", [1, TS], F32)
    rrow_s = sb("rrow_s", [1, TS], F32)
    bcrow = sb("bcrow", [1, TS], F32)

    x0t = sb("x0t", [128, NDT * TS], F32)
    x0b = sb("x0b", [128, NDT * 512], BF16)
    xln = sb("xln", [128, NDT * TS], F32)
    dt_ = sb("dt_", [128, NDT * TS], F32)
    u_ = sb("u_", [128, NDT * TS], F32)
    da = sb("da", [128, NDT * (TS + 1)], F32)
    dbx = sb("dbx", [128, NDT * (TS + 1)], F32)
    h_ = sb("h_", [128, NDT * (TS + 1)], F32)
    ch = sb("ch", [128, NDT * TS], F32)
    y_ = sb("y_", [128, NDT * TS], F32)
    brep = sb("brep", [128, TS], F32)
    crep = sb("crep", [128, TS], F32)
    mrep = sb("mrep", [128, TS], F32)
    rrep = sb("rrep", [128, TS], F32)
    xpr = sb("xpr", [128, TS], F32)
    bcr_b = sb("bcr_b", [16, TS], F32)
    bcr_c = sb("bcr_c", [16, TS], F32)
    dtin_b = sb("dtin_b", [128, TS], BF16)
    se1 = sb("se1", [128, 512], F32)
    se2 = sb("se2", [128, 512], F32)

    ps_m = psu("ps_m", [128, 512])
    ps_b = psu("ps_b", [16, 512])
    ps_c = psu("ps_c", [16, 512])
    ps_r = psu("ps_r", [1, 160])
    ps_t = psu("ps_t", [TB, 128])
    ps_e = psu("ps_e", [128, 512])


    def r3(buf, inner):
        return buf[:].rearrange("p (a t) -> p a t", a=NDT)

    x0t3 = lambda: r3(x0t, TS)
    xln3 = lambda: r3(xln, TS)
    dt3 = lambda: r3(dt_, TS)
    u3 = lambda: r3(u_, TS)
    ch3 = lambda: r3(ch, TS)
    da3 = lambda: da[:].rearrange("p (a t) -> p a t", a=NDT)
    dbx3 = lambda: dbx[:].rearrange("p (a t) -> p a t", a=NDT)
    h3 = lambda: h_[:].rearrange("p (a t) -> p a t", a=NDT)
    st3 = lambda: state[:].rearrange("p (a n) -> p a n", a=NDT)
    a43 = lambda: a4[:].rearrange("p (a n) -> p a n", a=NDT)
    d43 = lambda: d4[:].rearrange("p (a o) -> p a o", a=NDT)

    t_alog = S.dma("sync", a4[:], alog, ch=0)
    t_dvec = S.dma("sync", d4[:], dvec, ch=0)
    t_dtb = S.dma("sync", dtb4[:], dtb, ch=0)
    t_dtw = S.dma("sync", x0t[:, 0:NDT * 128], dtwT, ch=0)
    t_xps = S.dma("sync", xps_f[:], xpwS, ch=0)
    setup_g = [t_alog, t_dvec, t_dtb, t_dtw, t_xps]
    t_alog = t_dvec = t_dtb = t_dtw = t_xps = tuple(setup_g[-1])
    setup_g = [tuple(setup_g[-1])]
    t_alog, = setup_g; t_dvec, = setup_g; t_dtb, = setup_g
    t_dtw, = setup_g; t_xps, = setup_g

    t_a1 = S.op("scalar", lambda e: e.activation(out=a4[:], in_=a4[:], func=AF.Exp),
                deps=[t_alog])
    t_a2 = S.op("vector", lambda e: e.tensor_scalar(
        out=a4[:], in0=a4[:], scalar1=-1.0, scalar2=None, op0=ALU.mult),
        deps=[t_a1])
    t_dtwb = S.op("vector", lambda e: e.tensor_copy(
        out=dtw_b[:], in_=x0t[:, 0:NDT * 128]), deps=[t_dtw])
    t_xpsb = S.op("vector", lambda e: e.tensor_copy(out=xps_b[:], in_=xps_f[:]),
                  deps=[t_xps])
    t_ones = S.op("vector", lambda e: e.memset(onesc[:], 1.0))
    t_ceps = S.op("vector", lambda e: e.memset(ceps[:], LN_EPS))
    t_cone = S.op("vector", lambda e: e.memset(cone[:], 1.0))
    t_cneg1 = S.op("vector", lambda e: e.memset(cneg1[:], -1.0))
    t_ones1r = S.op("vector", lambda e: e.memset(ones1r[:], 1.0))
    t_idn = emit_identity(S, idn)
    t_st0 = S.op("vector", lambda e: e.memset(state[:], 0.0))
    t_da0 = S.op("vector", lambda e: e.memset(da[:], 0.0))
    t_dbx0 = S.op("vector", lambda e: e.memset(dbx[:], 0.0))
    t_y0 = S.op("vector", lambda e: e.memset(y_[:], 0.0))

    last_use = {0: None, 1: None}
    all_stats = []
    x0sA = brep[:, 0:512]
    x0sB = crep[:, 0:512]
    for j in range(TB):
        buf = x0sA if j % 2 == 0 else x0sB
        sebuf = se1 if j % 2 == 0 else se2
        pu = last_use[j % 2]
        t_l = S.dma("sync", buf, x0d[j * 128:(j + 1) * 128, :],
                    deps=[pu] if pu else [], ch=1 + (j % 2))
        t_r = S.op("vector", lambda e, b=buf, j=j: e.reduce_sum(
            out=stat_a[:, j:j + 1], in_=b, axis=AX.X), deps=[t_l])
        t_s = S.op("scalar", lambda e, b=buf, o=sebuf: e.activation(
            out=o[:], in_=b, func=AF.Square), deps=[t_l])
        t_q = S.op("vector", lambda e, o=sebuf, j=j: e.reduce_sum(
            out=stat_b[:, j:j + 1], in_=o[:], axis=AX.X), deps=[t_s])
        last_use[j % 2] = t_q
        all_stats += [t_r, t_q]
    TBP = TB

    t_z2 = S.op("vector", lambda e: e.memset(stat_t[:], 0.0))
    t_d2z = S.dma("gpsimd", st_in.rearrange("r (p f) -> (r p) f", p=TB)[2 * TB:3 * TB, :],
                  stat_t[:], deps=[t_z2])
    t_ws = None
    for a in range(4):
        t_ws = S.op("tensor", lambda e, a=a: e.matmul(
            ps_r[:], onesc[:], xps_f[:, a * 160:(a + 1) * 160],
            start=(a == 0), stop=(a == 3)), deps=[t_ones, t_xps])
    t_wse = S.op("vector", lambda e: e.tensor_copy(out=wsrow[:], in_=ps_r[:]),
                 deps=[t_ws])
    t_dws = S.dma("gpsimd", st_in[2:3, 0:160], wsrow[:], deps=[t_wse, t_d2z])

    t_tp1 = S.op("tensor", lambda e: e.transpose(ps_t[:], stat_a[:], idn[:]),
                 deps=all_stats + [t_idn, t_ws])
    t_ev1 = S.op("vector", lambda e: e.tensor_copy(out=stat_t[:], in_=ps_t[:]),
                 deps=[t_tp1, t_dws])
    t_d1 = S.dma("gpsimd", st_in.rearrange("r (p f) -> (r p) f", p=TB)[0:TB, :],
                 stat_t[:], deps=[t_ev1])
    t_tp2 = S.op("tensor", lambda e: e.transpose(ps_t[:], stat_b[:], idn[:]),
                 deps=[t_ev1])
    t_ev2 = S.op("vector", lambda e: e.tensor_copy(out=sa_t[:], in_=ps_t[:]),
                 deps=[t_tp2])
    t_d2 = S.dma("gpsimd", st_in.rearrange("r (p f) -> (r p) f", p=TB)[TB:2 * TB, :],
                 sa_t[:], deps=[t_ev2])

    t_ccst = S.cc("AllReduce", ALU.add, [st_in.opt()], [st_out.opt()],
                  deps=[t_d1, t_d2, t_dws])

    t_la = S.dma("sync", sa_t[:], st_out.rearrange("r (p f) -> (r p) f", p=TB)[0:TB, :],
                 deps=[t_ccst, t_d2])
    t_lb = S.dma("sync", sb_t[:], st_out.rearrange("r (p f) -> (r p) f", p=TB)[TB:2 * TB, :],
                 deps=[t_ccst])
    t_wa = S.dma("sync", wcolA[:], st_out[2:3, 0:128].rearrange("o r -> r o"),
                 deps=[t_ccst])
    t_wb = S.dma("sync", wcolB[:], st_out[2:3, 128:144].rearrange("o r -> r o"),
                 deps=[t_ccst])
    t_wc = S.dma("sync", wcolC[:], st_out[2:3, 144:160].rearrange("o r -> r o"),
                 deps=[t_ccst])
    t_la = t_lb = t_wa = t_wb = t_wc = t_wc
    t_m1 = S.op("vector", lambda e: e.tensor_scalar(
        out=m64[:], in0=sa_t[:], scalar1=1.0 / D, scalar2=None, op0=ALU.mult),
        deps=[t_la])
    t_m2 = S.op("vector", lambda e: e.tensor_scalar(
        out=r64[:], in0=sb_t[:], scalar1=1.0 / D, scalar2=None, op0=ALU.mult),
        deps=[t_lb])
    t_m3 = S.op("vector", lambda e: e.tensor_tensor(
        out=t64[:], in0=m64[:], in1=m64[:], op=ALU.mult), deps=[t_m1])
    t_m4 = S.op("vector", lambda e: e.tensor_tensor(
        out=r64[:], in0=r64[:], in1=t64[:], op=ALU.subtract), deps=[t_m2, t_m3])
    t_m5 = S.op("scalar", lambda e: e.activation(
        out=t64[:], in_=r64[:], func=AF.Sqrt, bias=ceps[0:TB, 0:1]),
        deps=[t_m4, t_ceps])
    t_m6 = S.op("vector", lambda e: e.reciprocal(out=r64[:], in_=t64[:]),
                deps=[t_m5])
    t_dm = S.dma("gpsimd", st_scr.rearrange("r (p f) -> (r p) f", p=TB)[0:TB, :],
                 m64[:], deps=[t_m3, t_m6])
    t_dr = S.dma("gpsimd", st_scr.rearrange("r (p f) -> (r p) f", p=TB)[TB:2 * TB, :],
                 r64[:], deps=[t_m6])

    prev = {}
    tok_scan = None
    tok_se_prev = None

    for s in range(NSUB):
        t0c = s * TS
        t_x0 = S.dma("sync", x0t3(),
                     x0dT.rearrange("(a p) t -> p a t", p=128)[:, :, t0c:t0c + TS],
                     deps=[d for d in [prev.get("x0t")] if d], ch=3)
        t_ms = S.dma("sync", mrow_s[:], st_scr[0:1, t0c:t0c + TS],
                     deps=[t_dr] + ([prev["mrow"]] if "mrow" in prev else []),
                     ch=4)
        t_rs = S.dma("sync", rrow_s[:], st_scr[1:2, t0c:t0c + TS],
                     deps=[t_dr] + ([prev["rrow"]] if "rrow" in prev else []),
                     ch=4)
        t_ms = t_rs

        ev_mr = []
        ev_prev0 = prev.get("ps_e")
        for half in range(2):
            sl = slice(half * 512, (half + 1) * 512)
            t_bm = S.op("tensor", lambda e, sl=sl: e.matmul(
                ps_e[:], ones1r[:], mrow_s[0:1, sl], start=True, stop=True),
                deps=[t_ms, t_ones1r]
                + ([prev["mrep"]] if "mrep" in prev else [])
                + ([ev_prev0] if ev_prev0 else []))
            t_em = S.op("vector", lambda e, sl=sl: e.tensor_copy(
                out=mrep[:, sl], in_=ps_e[:]), deps=[t_bm])
            t_br = S.op("tensor", lambda e, sl=sl: e.matmul(
                ps_e[:], ones1r[:], rrow_s[0:1, sl], start=True, stop=True),
                deps=[t_rs, t_em])
            t_er = S.op("vector", lambda e, sl=sl: e.tensor_copy(
                out=rrep[:, sl], in_=ps_e[:]), deps=[t_br])
            ev_mr += [t_em, t_er]
            ev_prev0 = t_er
            prev["mrep"] = t_er
        prev["ps_e"] = t_er

        dma_xp = []
        for half in range(2):
            sl = slice(half * 512, (half + 1) * 512)
            t_cb = S.op("vector", lambda e, sl=sl: e.tensor_copy(
                out=x0b[:], in_=r3(x0t, TS)[:, :, sl]),
                deps=[t_x0] + ([prev["x0b"]] if "x0b" in prev else []))
            tm = td = tb2 = tc2 = None
            for a in range(4):
                tm = S.op("tensor", lambda e, a=a: e.matmul(
                    ps_m[:], xps_b[:, a * 160:a * 160 + 128],
                    x0b[:, a * 512:(a + 1) * 512],
                    start=(a == 0), stop=(a == 3)),
                    deps=[t_cb, t_xpsb] + ([prev["ps_m"]] if "ps_m" in prev else []))
            for a in range(4):
                tb2 = S.op("tensor", lambda e, a=a: e.matmul(
                    ps_b[:], xps_b[:, a * 160 + 128:a * 160 + 144],
                    x0b[:, a * 512:(a + 1) * 512],
                    start=(a == 0), stop=(a == 3)),
                    deps=[prev["ps_b"]] if "ps_b" in prev else [])
            for a in range(4):
                tc2 = S.op("tensor", lambda e, a=a: e.matmul(
                    ps_c[:], xps_b[:, a * 160 + 144:a * 160 + 160],
                    x0b[:, a * 512:(a + 1) * 512],
                    start=(a == 0), stop=(a == 3)),
                    deps=[prev["ps_c"]] if "ps_c" in prev else [])
            t_e1 = S.op("vector", lambda e, sl=sl: e.tensor_copy(
                out=xpr[:, sl], in_=ps_m[:]), deps=[tm])
            bcr_war = [prev["bcr"]] if "bcr" in prev else []
            t_e2 = S.op("vector", lambda e, sl=sl: e.tensor_copy(
                out=bcr_b[:, sl], in_=ps_b[:]), deps=[tb2] + bcr_war)
            t_e3 = S.op("vector", lambda e, sl=sl: e.tensor_copy(
                out=bcr_c[:, sl], in_=ps_c[:]), deps=[tc2] + bcr_war)
            prev["ps_m"], prev["ps_b"], prev["ps_c"] = t_e1, t_e2, t_e3
            prev["x0b"] = tm if tc2 is None else tc2
            dma_xp += [t_e1, t_e2, t_e3]
        t_dx1 = S.dma("gpsimd", xp_ins[s][0:128, :], xpr[:], deps=dma_xp, ch=3)
        t_dx2 = S.dma("gpsimd", xp_ins[s][128:144, :], bcr_b[:], deps=dma_xp, ch=3)
        t_dx3 = S.dma("gpsimd", xp_ins[s][144:160, :], bcr_c[:], deps=dma_xp, ch=3)
        t_ccxp = S.cc("AllReduce", ALU.add, [xp_ins[s].opt()], [xp_outs[s].opt()],
                      deps=[t_dx1, t_dx2, t_dx3])
        t_lx1 = S.dma("sync", xpr[:], xp_outs[s][0:128, :], deps=[t_ccxp], ch=5)
        bw = [prev["bcr"]] if "bcr" in prev else []
        t_lx2 = S.dma("sync", bcr_b[:], xp_outs[s][128:144, :],
                      deps=[t_ccxp] + bw, ch=5)
        t_lx3 = S.dma("sync", bcr_c[:], xp_outs[s][144:160, :],
                      deps=[t_ccxp] + bw, ch=5)
        t_lx1 = t_lx2 = t_lx3

        t_c1 = S.op("vector", lambda e: e.tensor_scalar(
            out=se1[:], in0=mrep[:, 0:512], scalar1=wcolA[:], scalar2=None,
            op0=ALU.mult), deps=[t_lx1, t_wa, prev["mrep"]])
        t_c1b = S.op("vector", lambda e: e.tensor_scalar(
            out=se2[:], in0=mrep[:, 512:1024], scalar1=wcolA[:], scalar2=None,
            op0=ALU.mult), deps=[t_c1])
        t_c2 = S.op("vector", lambda e: e.tensor_tensor(
            out=xpr[:, 0:512], in0=xpr[:, 0:512], in1=se1[:], op=ALU.subtract))
        t_c2b = S.op("vector", lambda e: e.tensor_tensor(
            out=xpr[:, 512:1024], in0=xpr[:, 512:1024], in1=se2[:], op=ALU.subtract))
        t_c3 = S.op("vector", lambda e: e.tensor_tensor(
            out=xpr[:], in0=xpr[:], in1=rrep[:], op=ALU.mult))
        t_c4 = S.op("vector", lambda e: e.tensor_copy(out=dtin_b[:], in_=xpr[:]),
                    deps=[t_c3])
        t_c5 = S.op("vector", lambda e: e.tensor_scalar(
            out=se1[0:16, :], in0=mrep[0:16, 0:512], scalar1=wcolB[:],
            scalar2=None, op0=ALU.mult), deps=[t_lx2, t_wb])
        t_c5b = S.op("vector", lambda e: e.tensor_scalar(
            out=se2[0:16, :], in0=mrep[0:16, 512:1024], scalar1=wcolB[:],
            scalar2=None, op0=ALU.mult))
        t_c6 = S.op("vector", lambda e: e.tensor_tensor(
            out=bcr_b[0:16, 0:512], in0=bcr_b[0:16, 0:512], in1=se1[0:16, :],
            op=ALU.subtract))
        t_c6b = S.op("vector", lambda e: e.tensor_tensor(
            out=bcr_b[0:16, 512:1024], in0=bcr_b[0:16, 512:1024], in1=se2[0:16, :],
            op=ALU.subtract))
        t_c7 = S.op("vector", lambda e: e.tensor_tensor(
            out=bcr_b[:], in0=bcr_b[:], in1=rrep[0:16, :], op=ALU.mult))
        t_c8 = S.op("vector", lambda e: e.tensor_scalar(
            out=se1[0:16, :], in0=mrep[0:16, 0:512], scalar1=wcolC[:],
            scalar2=None, op0=ALU.mult), deps=[t_lx3, t_wc])
        t_c8b = S.op("vector", lambda e: e.tensor_scalar(
            out=se2[0:16, :], in0=mrep[0:16, 512:1024], scalar1=wcolC[:],
            scalar2=None, op0=ALU.mult))
        t_c9 = S.op("vector", lambda e: e.tensor_tensor(
            out=bcr_c[0:16, 0:512], in0=bcr_c[0:16, 0:512], in1=se1[0:16, :],
            op=ALU.subtract))
        t_c9b = S.op("vector", lambda e: e.tensor_tensor(
            out=bcr_c[0:16, 512:1024], in0=bcr_c[0:16, 512:1024], in1=se2[0:16, :],
            op=ALU.subtract))
        t_c10 = S.op("vector", lambda e: e.tensor_tensor(
            out=bcr_c[:], in0=bcr_c[:], in1=rrep[0:16, :], op=ALU.mult))
        tok_bc = t_c10

        t_sp = []
        last_ps_reader = None
        prev_dt = tok_bc
        for a in range(4):
            for half in range(2):
                sl = slice(half * 512, (half + 1) * 512)
                deps_mm = [t_c4, t_dtwb]
                if last_ps_reader:
                    deps_mm.append(last_ps_reader)
                if "dtin" in prev:
                    deps_mm.append(prev.pop("dtin"))
                t_mm = S.op("tensor", lambda e, a=a, sl=sl: e.matmul(
                    ps_m[:], dtw_b[:, a * 128:(a + 1) * 128], dtin_b[:, sl],
                    start=True, stop=True), deps=deps_mm)
                t_s1 = S.op("scalar", lambda e, a=a: e.activation(
                    out=se1[:], in_=ps_m[:], func=AF.Abs, bias=dtb4[:, a:a + 1]),
                    deps=[t_mm, t_dtb, prev_dt])
                t_s2 = S.op("scalar", lambda e: e.activation(
                    out=se2[:], in_=se1[:], func=AF.Exp,
                    scale=cneg1[:]), deps=[t_cneg1])
                t_s3 = S.op("scalar", lambda e: e.activation(
                    out=se1[:], in_=se2[:], func=AF.Ln,
                    bias=cone[:]), deps=[t_cone])
                t_s4 = S.op("scalar", lambda e, a=a: e.activation(
                    out=se2[:], in_=ps_m[:], func=AF.Relu, bias=dtb4[:, a:a + 1]))
                last_ps_reader = t_s4
                t_dt = S.op("vector", lambda e, a=a, sl=sl: e.tensor_tensor(
                    out=dt_[:, a * TS + sl.start: a * TS + sl.stop],
                    in0=se1[:], in1=se2[:], op=ALU.add), deps=[t_s3, t_s4])
                t_sp.append(t_dt)
                prev_dt = t_dt
        last_dtmm = t_mm
        prev["ps_m"] = t_s4
        tok_dt = t_sp[-1]

        t_xl1 = S.op("vector", lambda e: e.tensor_tensor(
            out=ch3(), in0=x0t3(), in1=mid_bcast(mrep[:], NDT), op=ALU.subtract),
            deps=[t_x0, prev["mrep"]] + t_sp[-1:]
            + ([prev["chout"]] if "chout" in prev else []))
        t_xl2 = S.op("vector", lambda e: e.tensor_tensor(
            out=xln3(), in0=ch3(), in1=mid_bcast(rrep[:], NDT), op=ALU.mult))
        t_u = S.op("vector", lambda e: e.tensor_tensor(
            out=u3(), in0=dt3(), in1=xln3(), op=ALU.mult), deps=[tok_dt])
        tok_u = t_u

        for n in range(NST):
            dep_b = [tok_bc]
            if tok_scan:
                dep_b.append(tok_scan)
            if "bcrow" in prev:
                dep_b.append(prev["bcrow"])
            t_rb = S.dma("gpsimd", bcrow[:], bcr_b[n:n + 1, :], deps=dep_b,
                         ch=1)
            tb = tc = None
            ev_prev = prev.get("ps_e")
            for half in range(2):
                sl = slice(half * 512, (half + 1) * 512)
                tb_m = S.op("tensor", lambda e, sl=sl: e.matmul(
                    ps_e[:], ones1r[:], bcrow[0:1, sl], start=True,
                    stop=True), deps=[t_rb]
                    + ([tok_scan] if tok_scan else [])
                    + ([ev_prev] if ev_prev else []))
                tb = S.op("vector", lambda e, sl=sl: e.tensor_copy(
                    out=brep[:, sl], in_=ps_e[:]), deps=[tb_m])
                ev_prev = tb
            t_rc = S.dma("gpsimd", bcrow[:], bcr_c[n:n + 1, :],
                         deps=[tb_m, tb], ch=1)
            for half in range(2):
                sl = slice(half * 512, (half + 1) * 512)
                tc_m = S.op("tensor", lambda e, sl=sl: e.matmul(
                    ps_e[:], ones1r[:], bcrow[0:1, sl], start=True,
                    stop=True), deps=[t_rc, ev_prev])
                tc_m_last = tc_m
                prev["bcrow"] = tc_m
                tc = S.op("vector", lambda e, sl=sl: e.tensor_copy(
                    out=crep[:, sl], in_=ps_e[:]), deps=[tc_m])
                ev_prev = tc
            prev["ps_e"] = tc
            a_col = inner_bcast(a43()[:, :, n:n + 1], TS)
            t_dta = S.op("vector", lambda e, ac=a_col: e.tensor_tensor(
                out=da3()[:, :, 1:TS + 1], in0=dt3(), in1=ac, op=ALU.mult),
                deps=[tok_u, t_a2] + ([tok_se_prev] if tok_se_prev else [t_da0]))
            t_exp = S.op("scalar", lambda e: e.activation(
                out=da3()[:, :, 1:TS + 1], in_=da3()[:, :, 1:TS + 1],
                func=AF.Exp), deps=[t_dta] + ([tok_scan] if tok_scan else []))
            tok_se_prev = t_exp
            t_dbx = S.op("vector", lambda e, n=n: e.tensor_tensor(
                out=dbx3()[:, :, 1:TS + 1], in0=u3(), in1=mid_bcast(brep[:], NDT),
                op=ALU.mult), deps=[tc, t_dbx0])
            t_inj = S.op("vector", lambda e, n=n: e.tensor_copy(
                out=dbx3()[:, :, 0:1], in_=st3()[:, :, n:n + 1]),
                deps=[t_st0])
            t_scan = S.op("vector", lambda e: e.tensor_tensor_scan(
                out=h_[:], data0=da[:], data1=dbx[:], initial=0.0,
                op0=ALU.mult, op1=ALU.add), deps=[t_exp, t_dbx, t_inj])
            t_sv = S.op("vector", lambda e, n=n: e.tensor_copy(
                out=st3()[:, :, n:n + 1], in_=h3()[:, :, TS:TS + 1]),
                deps=[t_scan])
            t_ch = S.op("vector", lambda e: e.tensor_tensor(
                out=ch3(), in0=h3()[:, :, 1:TS + 1], in1=mid_bcast(crep[:], NDT),
                op=ALU.mult), deps=[t_scan, t_xl2])
            t_y = S.op("vector", lambda e: e.tensor_tensor(
                out=y_[:], in0=y_[:], in1=ch[:], op=ALU.add), deps=[t_y0])
            tok_scan = t_y

        t_f1 = S.op("vector", lambda e: e.tensor_tensor(
            out=ch3(), in0=xln3(), in1=inner_bcast(d43()[:, :, 0:1], TS),
            op=ALU.mult), deps=[tok_scan, t_dvec])
        t_f2 = S.op("vector", lambda e: e.tensor_tensor(
            out=y_[:], in0=y_[:], in1=ch[:], op=ALU.add))
        t_f3 = S.op("vector", lambda e: e.tensor_tensor(
            out=ch[:], in0=y_[:], in1=x0t[:], op=ALU.add))
        t_out = S.dma("gpsimd",
                      x1d.rearrange("(a p) t -> p a t", p=128)[:, :, t0c:t0c + TS],
                      ch3(), deps=[t_f3], ch=2)
        t_y0 = S.op("vector", lambda e: e.memset(y_[:], 0.0))
        prev["x0t"] = t_out
        prev["chout"] = t_out
        prev["mrow"] = t_br
        prev["rrow"] = t_br
        prev["bcr"] = tc_m_last
        prev["dtin"] = last_dtmm
        prev["mrep"] = t_f1

    return {'x1d_stores': [prev['chout']]}


def build_mamba():
    nc = bass.Bass("TRN2", target_bir_lowering=False)
    TS = TS_M
    TB = T // 128
    NSUB = T // TS

    x0d = nc.dram_tensor("x0d", [T, DSL], F32, kind="ExternalInput").ap()
    x0dT = nc.dram_tensor("x0dT", [DSL, T], F32, kind="ExternalInput").ap()
    alog = nc.dram_tensor("alog", [128, NDT * NST], F32, kind="ExternalInput").ap()
    dvec = nc.dram_tensor("dvec", [128, NDT], F32, kind="ExternalInput").ap()
    xpwS = nc.dram_tensor("xpwS", [128, NDT * 160], F32, kind="ExternalInput").ap()
    dtwT = nc.dram_tensor("dtwT", [DTR, DSL], F32, kind="ExternalInput").ap()
    dtb = nc.dram_tensor("dtb", [128, NDT], F32, kind="ExternalInput").ap()

    x1d = nc.dram_tensor("x1d", [DSL, T], F32, kind="ExternalOutput").ap()

    st_in = nc.dram_tensor("st_in", [3, T], F32, kind="Internal").ap()
    st_out = nc.dram_tensor("st_out", [3, T], F32, kind="Internal").ap()
    st_scr = nc.dram_tensor("st_scr", [2, T], F32, kind="Internal").ap()
    xp_ins = [nc.dram_tensor(f"xp_in{s}", [160, TS], F32, kind="Internal").ap()
              for s in range(T // TS_M)]
    xp_outs = [nc.dram_tensor(f"xp_out{s}", [160, TS], F32, kind="Internal").ap()
               for s in range(T // TS_M)]

    with contextlib.ExitStack() as stk:
        S = Sched(nc, stk)
        IO = dict(x0d=x0d, x0dT=x0dT, alog=alog, dvec=dvec, xpwS=xpwS,
                  dtwT=dtwT, dtb=dtb, x1d=x1d, st_in=st_in, st_out=st_out,
                  st_scr=st_scr, xp_ins=xp_ins, xp_outs=xp_outs)
        with contextlib.ExitStack() as pstk:
            phase_mamba(nc, S, pstk, IO)
        S.emit(block=stk.enter_context(nc.Block()))
    return nc


def phase_ffn0(nc, S, stk, IO):
    To = OWN_T
    NK = D // 128
    NH = To // 512
    x1T = IO.get('x1T')
    wqT = IO.get('wqT')
    scaleT = IO.get('scaleT')
    zeroT = IO.get('zeroT')
    bias0 = IO.get('bias0')
    wkvT = IO.get('wkvT')
    bkv_c = IO.get('bkv_c')
    x2T = IO.get('x2T')
    qd = IO.get('qd')
    kd = IO.get('kd')
    vt = IO.get('vt')
    wdeq = IO.get('wdeq')
    wkvb = IO.get('wkvb')
    b0scr = IO.get('b0scr')
    x1idx = IO.get("x1idx")
    x1v = IO.get("x1v")
    tok_pre = IO.get("tok_pre", [])

    def x1_load(buf, rowtile, deps, ch, colsl=None):
        if x1idx is None:
            src = x1T[rowtile * 128:(rowtile + 1) * 128,
                      colsl if colsl is not None else slice(0, To)]
            return S.dma("sync", buf, src, deps=deps, ch=ch)
        eo = colsl.start if colsl is not None else 0
        return S.idma(buf, x1v, x1idx_sb[:, rowtile:rowtile + 1],
                      deps=list(deps) + list(tok_pre), ch=6 + ch % 2,
                      element_offset=eo)
    _pfx = IO.get("pfx", "")
    sb = lambda name, shape, dt: stk.enter_context(
        nc.sbuf_tensor(_pfx + name, shape, dt))
    psu = lambda name, shape, dt=F32: stk.enter_context(
        nc.psum_tensor(_pfx + name, shape, dt))

    xn = sb("xn", [128, NK * To], BF16)
    x1idx_sb = sb("x1idx_sb", [128, NK], I32) if x1idx is not None else None
    if x1idx is not None:
        IO["tok_idx"] = S.dma("sync", x1idx_sb[:], x1idx, ch=0)
        tok_pre = list(tok_pre) + [IO["tok_idx"]]
    e_all = sb("e_all", [128, NK * 32], BF16)
    zsc = sb("zsc", [32, D], F32)
    zsn_b = sb("zsn_b", [32, D], BF16)
    wq_i = sb("wq_i", [128, 512], I32)
    wq_f = sb("wq_f", [128, 512], F32)
    s_rep = sb("s_rep", [128, 512], F32)
    wq_b = sb("wq_b", [128, D], BF16)
    xt_a = sb("xt_a", [128, To], F32)
    xt_b = sb("xt_b", [128, To], F32)
    sqt = sb("sqt", [128, To], F32)
    mrep = sb("mrep", [128, To], F32)
    rrep = sb("rrep", [128, To], F32)
    srow = sb("srow", [1, To], F32)
    qrow = sb("qrow", [1, To], F32)
    trow = sb("trow", [1, To], F32)
    s_b = sb("s_b", [32, To], BF16)
    wst_a = sb("wst_a", [128, 32 * 128], BF16)
    wst_b = sb("wst_b", [128, 32 * 128], BF16)
    wkv_f = sb("wkv_f", [128, 3072], F32)
    wkv_c = sb("wkv_c", [128, 3072], BF16)
    bq = sb("bq", [128, 48], F32)
    b0_b = sb("b0_b", [1, D], BF16)
    ones1r = sb("ones1r", [1, 128], F32)
    ones1b = sb("ones1b", [1, 512], BF16)
    scrow = sb("scrow", [1, 512], F32)
    onesc = sb("onesc", [128, 1], F32)
    ceps = sb("ceps", [128, 1], F32)
    out_b = sb("out_b", [128, To], BF16)
    idn_b = sb("idn_b", [128, 128], BF16)
    vt_s = sb("vt_s", [128, PART], BF16)

    ps_o = psu("ps_o", [128, 512])
    ps_e = psu("ps_e", [128, 512])
    ps_s = psu("ps_s", [32, 512])
    ps_tb = psu("ps_tb", [128, 128], BF16)
    ps_r = [psu(f"ps_r{i}", [1, 512]) for i in range(2 * NH)]


    t_sc = S.dma("sync", zsc[:], scaleT, ch=0)

    t_b0 = S.dma("sync", xt_a[0:8, 0:512],
                 bias0.rearrange("o (a f) -> (o a) f", a=8), ch=0)
    t_bq = S.dma("sync", bq[:], bkv_c, ch=0)
    t_set = t_sc = t_b0 = t_bq
    t_ones = S.op("vector", lambda e: e.memset(ones1r[:], 1.0))
    t_onesb = S.op("vector", lambda e: e.memset(ones1b[:], 1.0))
    t_onesc = S.op("vector", lambda e: e.memset(onesc[:], 1.0))
    t_ceps = S.op("vector", lambda e: e.memset(ceps[:], LN_EPS))
    t_zs3 = None
    pzs = None
    for q8 in range(D // 512):
        sl = slice(q8 * 512, (q8 + 1) * 512)
        t_z1 = S.dma("sync", wq_f[0:32, :], zeroT[:, sl],
                     deps=[pzs] if pzs else [], ch=3)
        t_z2 = S.op("vector", lambda e, sl=sl: e.tensor_tensor(
            out=wq_f[0:32, :], in0=wq_f[0:32, :], in1=zsc[0:32, sl],
            op=ALU.mult), deps=[t_z1, t_set])
        t_z3 = S.op("vector", lambda e: e.tensor_scalar(
            out=wq_f[0:32, :], in0=wq_f[0:32, :], scalar1=-1.0,
            scalar2=None, op0=ALU.mult))
        t_zs3 = S.op("vector", lambda e, sl=sl: e.tensor_copy(
            out=zsn_b[:, sl], in_=wq_f[0:32, :]))
        pzs = t_zs3
    t_b0c = S.op("vector", lambda e: e.tensor_copy(
        out=out_b[0:8, 0:512], in_=xt_a[0:8, 0:512]), deps=[t_set])
    t_b0d = S.dma("gpsimd", b0scr.rearrange("o (a f) -> (o a) f", a=8),
                  out_b[0:8, 0:512], deps=[t_b0c], ch=3)
    t_b0b = S.dma("sync", b0_b[:], b0scr, deps=[t_b0d], ch=0)
    t_id0 = S.op("gpsimd", lambda e: e.memset(idn_b[:], 0.0), inc=1)
    t_idn = S.op("gpsimd", lambda e: e.affine_select(
        out=idn_b[:], in_=idn_b[:], compare_op=ALU.not_equal, fill=1.0,
        base=0, pattern=[[-1, 128]], channel_multiplier=1),
        deps=[t_id0], inc=1)
    t_e0 = S.op("gpsimd", lambda e: e.memset(e_all[:], 0.0), inc=1)
    t_eall = t_e0
    for g in range(NK):
        t_eall = S.op("gpsimd", lambda e, g=g: e.affine_select(
            out=e_all[:, g * 32:(g + 1) * 32],
            in_=e_all[:, g * 32:(g + 1) * 32],
            compare_op=ALU.not_equal, fill=1.0,
            base=-g, pattern=[[1, 32]], channel_multiplier=0),
            deps=[t_e0], inc=1)

    ln_state = {"war": [t_b0b]}

    def emit_ln(src_dram, norm_into_xn=True, use_x1_loader=False):
        war = ln_state["war"]
        tma = tmb = None
        prevload = {0: None, 1: None}
        for g in range(NK):
            buf = xt_a if g % 2 == 0 else xt_b
            pu = prevload[g % 2]
            if use_x1_loader:
                t_l = x1_load(buf[:], g, deps=([pu] if pu else []) + war,
                              ch=g % 2)
            else:
                t_l = S.dma("sync", buf[:], src_dram[g * 128:(g + 1) * 128, :],
                            deps=([pu] if pu else []) + war, ch=1 + g % 2)
            t_sq = S.op("scalar", lambda e, b=buf: e.activation(
                out=sqt[:], in_=b[:], func=AF.Square),
                deps=[t_l] + ([tmb] if tmb else []))
            for h in range(NH):
                sl = slice(h * 512, (h + 1) * 512)
                tma = S.op("tensor", lambda e, b=buf, sl=sl, g=g, h=h:
                           e.matmul(ps_r[h][:], onesc[:], b[:, sl],
                                    start=(g == 0), stop=(g == NK - 1)),
                           deps=[t_l, t_onesc] + war)
                tmb = S.op("tensor", lambda e, sl=sl, g=g, h=h: e.matmul(
                    ps_r[NH + h][:], onesc[:], sqt[:, sl],
                    start=(g == 0), stop=(g == NK - 1)),
                    deps=[t_sq])
            prevload[g % 2] = tmb
        ev = []
        for h in range(NH):
            sl = slice(h * 512, (h + 1) * 512)
            ev.append(S.op("vector", lambda e, sl=sl, h=h: e.tensor_scalar(
                out=srow[:, sl], in0=ps_r[h][:], scalar1=1.0 / D,
                scalar2=None, op0=ALU.mult), deps=[tma, tmb]))
            ev.append(S.op("vector", lambda e, sl=sl, h=h: e.tensor_scalar(
                out=qrow[:, sl], in0=ps_r[NH + h][:], scalar1=1.0 / D,
                scalar2=None, op0=ALU.mult), deps=[tmb]))
        t_v1 = S.op("vector", lambda e: e.tensor_tensor(
            out=trow[:], in0=srow[:], in1=srow[:], op=ALU.mult), deps=ev)
        t_v2 = S.op("vector", lambda e: e.tensor_tensor(
            out=qrow[:], in0=qrow[:], in1=trow[:], op=ALU.subtract))
        t_v3 = S.op("scalar", lambda e: e.activation(
            out=trow[:], in_=qrow[:], func=AF.Sqrt, bias=ceps[0:1, 0:1]),
            deps=[t_v2, t_ceps])
        t_v4 = S.op("vector", lambda e: e.reciprocal(
            out=qrow[:], in_=trow[:]), deps=[t_v3])
        evb = None
        for h in range(NH):
            sl = slice(h * 512, (h + 1) * 512)
            t_m1 = S.op("tensor", lambda e, sl=sl: e.matmul(
                ps_e[:], ones1r[:], srow[0:1, sl], start=True, stop=True),
                deps=[t_v4, t_ones] + ([evb] if evb else []) + war
                + ([ln_state["ps_e"]] if ln_state.get("ps_e") else []))
            t_m2 = S.op("vector", lambda e, sl=sl: e.tensor_copy(
                out=mrep[:, sl], in_=ps_e[:]), deps=[t_m1])
            t_m3 = S.op("tensor", lambda e, sl=sl: e.matmul(
                ps_e[:], ones1r[:], qrow[0:1, sl], start=True, stop=True),
                deps=[t_m2])
            evb = S.op("vector", lambda e, sl=sl: e.tensor_copy(
                out=rrep[:, sl], in_=ps_e[:]), deps=[t_m3])
        tok_norm = evb
        ln_state["ps_e"] = evb
        tok_xn = None
        if norm_into_xn:
            prevload = {0: None, 1: None}
            for g in range(NK):
                buf = xt_a if g % 2 == 0 else xt_b
                pu = prevload[g % 2]
                if use_x1_loader:
                    t_l = x1_load(buf[:], g, deps=([pu] if pu else [tma, tmb]),
                                  ch=g % 2)
                else:
                    t_l = S.dma("sync", buf[:], src_dram[g * 128:(g + 1) * 128, :],
                                deps=([pu] if pu else [tma, tmb]), ch=1 + g % 2)
                t_n1 = S.op("vector", lambda e, b=buf: e.tensor_tensor(
                    out=sqt[:], in0=b[:], in1=mrep[:], op=ALU.subtract),
                    deps=[t_l, tok_norm, tmb])
                tok_xn = S.op("vector", lambda e, g=g: e.tensor_tensor(
                    out=xn[:, g * To:(g + 1) * To], in0=sqt[:], in1=rrep[:],
                    op=ALU.mult))
                prevload[g % 2] = tok_xn
        return tok_xn, tok_norm

    tok_xn1, _ = emit_ln(x1T, use_x1_loader=(x1idx is not None))

    tok_deq = []
    pv = {"wq": None, "srep": None, "store": None}
    for g in range(NK):
        for q8 in range(D // 512):
            sl = slice(q8 * 512, (q8 + 1) * 512)
            t_lw = S.dma("sync", wq_i[:], wqT[g * 128:(g + 1) * 128, sl],
                         deps=[pv["wq"]] if pv["wq"] else [], ch=3)
            t_sr = S.dma("gpsimd", scrow[:], zsc[g:g + 1, sl],
                         deps=[pv["bm"]] if pv.get("bm") else [], ch=2)
            t_bm = S.op("tensor", lambda e: e.matmul(
                ps_e[:], ones1r[:], scrow[:], start=True, stop=True),
                deps=[t_set, t_ones, t_sr]
                + ([pv["srep"]] if pv["srep"] else [])
                + ([ln_state["ps_e"]] if ln_state.get("ps_e") else []))
            pv["bm"] = t_bm
            ln_state["ps_e"] = None
            t_eb = S.op("vector", lambda e: e.tensor_copy(
                out=s_rep[:], in_=ps_e[:]), deps=[t_bm])
            pv["srep"] = t_eb
            ln_state["ps_e"] = t_eb
            t_cf = S.op("vector", lambda e: e.tensor_copy(
                out=wq_f[:], in_=wq_i[:]), deps=[t_lw])
            pv["wq"] = t_cf
            t_mul = S.op("vector", lambda e, sl=sl: e.tensor_tensor(
                out=wq_b[:, sl], in0=wq_f[:], in1=s_rep[:], op=ALU.mult),
                deps=[t_cf, t_eb]
                + ([pv["store"]] if pv["store"] else []))
        t_st = None
        for nt in range(NK):
            t_st = S.dma("gpsimd",
                         wdeq.rearrange("p (nt g nn) -> p nt g nn",
                                        nt=NK, g=NK)[:, nt, g, :],
                         wq_b[:, nt * 128:(nt + 1) * 128],
                         deps=[t_mul], ch=0)
        pv["store"] = t_st
        tok_deq.append(t_st)

    tok_kv = None
    pkv = {}
    for g in range(NK):
        for hlf in range(2):
            t_lk = S.dma("sync", wkv_f[:],
                         wkvT[g * 128:(g + 1) * 128,
                              hlf * 3072:(hlf + 1) * 3072],
                         deps=[pkv["c"]] if pkv.get("c") else [],
                         ch=6 + hlf)
            t_ck = S.op("vector", lambda e: e.tensor_copy(
                out=wkv_c[:], in_=wkv_f[:]),
                deps=[t_lk] + ([pkv["st"]] if pkv.get("st") else []))
            pkv["c"] = t_ck
            for rc in range(24):
                tok_kv = S.dma("gpsimd",
                               wkvb.rearrange("p (rc g nn) -> p rc g nn",
                                              rc=48, g=NK)[
                                   :, hlf * 24 + rc, g, :],
                               wkv_c[:, rc * 128:(rc + 1) * 128],
                               deps=[t_ck], ch=3)
            pkv["st"] = tok_kv

    tok_x2 = []
    tok_s2mm = []
    pv2 = {"wtile": None, "x1res": None, "psol": None}
    for h in range(NH):
        sl = slice(h * 512, (h + 1) * 512)
        tsm = None
        for g in range(NK):
            tsm = S.op("tensor", lambda e, g=g, sl=sl: e.matmul(
                ps_s[:], e_all[:, g * 32:(g + 1) * 32],
                xn[:, g * To + sl.start:g * To + sl.stop],
                start=(g == 0), stop=(g == NK - 1)),
                deps=[tok_xn1, t_eall]
                + ([pv2["sevac"]] if "sevac" in pv2 else []))
        t_sev = S.op("vector", lambda e, sl=sl: e.tensor_copy(
            out=s_b[:, sl], in_=ps_s[:]), deps=[tsm])
        pv2["sevac"] = t_sev
        for nt in range(NK):
            deps_first = [tok_deq[-1], t_sev]
            if pv2.get("psol"):
                deps_first.append(pv2["psol"])
            wbuf = wst_a if nt % 2 == 0 else wst_b
            t_lw = S.dma("sync", wbuf[:],
                         wdeq[:, nt * NK * 128:(nt + 1) * NK * 128],
                         deps=[pv2[f"wt{nt % 2}"]]
                         if pv2.get(f"wt{nt % 2}") else [tok_deq[-1]],
                         ch=4 + nt % 2, tag="wdq")
            tmm = None
            for g in range(NK):
                tmm = S.op("tensor", lambda e, g=g, nt=nt, sl=sl, w=wbuf:
                           e.matmul(ps_o[:], w[:, g * 128:(g + 1) * 128],
                                    xn[:, g * To + sl.start:g * To + sl.stop],
                                    start=(g == 0), stop=False),
                           deps=[t_lw] + deps_first)
            pv2[f"wt{nt % 2}"] = tmm
            t_zc2 = S.op("tensor", lambda e, nt=nt, sl=sl: e.matmul(
                ps_o[:], zsn_b[0:32, nt * 128:(nt + 1) * 128], s_b[0:32, sl],
                start=False, stop=False), deps=[t_zs3, t_sev])
            t_bc2 = S.op("tensor", lambda e, nt=nt: e.matmul(
                ps_o[:], b0_b[0:1, nt * 128:(nt + 1) * 128], ones1b[:],
                start=False, stop=True), deps=[t_b0b, t_onesb])
            if x1idx is None:
                t_rx = S.dma("sync", xt_a[:, sl] if h == 0 else xt_b[:, sl],
                             x1T[nt * 128:(nt + 1) * 128, sl],
                             deps=[pv2["x1res"]] if pv2["x1res"] else [], ch=1)
            else:
                t_rx = x1_load(xt_a[:, sl] if h == 0 else xt_b[:, sl], nt,
                               deps=[pv2["x1res"]] if pv2["x1res"] else [],
                               ch=h, colsl=sl)
            rbuf = xt_a if h == 0 else xt_b
            t_x2 = S.op("vector", lambda e, sl=sl, b=rbuf: e.scalar_tensor_tensor(
                out=sqt[:, sl], in0=ps_o[:], scalar=1.0, in1=b[:, sl],
                op0=ALU.mult, op1=ALU.add), deps=[t_bc2, t_rx])
            pv2["x1res"] = t_x2
            pv2["psol"] = t_x2
            t_so = S.dma("gpsimd", x2T[nt * 128:(nt + 1) * 128, sl],
                         sqt[:, sl], deps=[t_x2], ch=1)
            tok_x2.append(t_so)
            pv2["x1res"] = t_so

    ln_state["war"] = [tok_x2[-1], pv2["psol"]]
    tok_xn2, _ = emit_ln(x2T)

    t_qkv_done = []
    pvq = {"w": None, "w2": None, "ob": None, "vs": None}
    for part in range(3):
        for rt in range(PART // 128):
            rcol = part * 16 + rt
            wb = wst_a if rcol % 2 == 0 else wst_b
            t_lw = S.dma("sync", wb[:],
                         wkvb[:, rcol * NK * 128:(rcol + 1) * NK * 128],
                         deps=[pvq[f"wt{rcol % 2}"]]
                         if pvq.get(f"wt{rcol % 2}") else [tok_kv],
                         ch=4 + rcol % 2)
            tmm = None
            for g in range(NK):
                for h in range(NH):
                    sl = slice(h * 512, (h + 1) * 512)
                    tmm = S.op("tensor", lambda e, wb=wb, g=g, sl=sl, h=h:
                               e.matmul(ps_o[:] if h == 0 else ps_e[:],
                                        wb[:, g * 128:(g + 1) * 128],
                                        xn[:, g * To + sl.start:
                                           g * To + sl.stop],
                                        start=(g == 0), stop=(g == NK - 1)),
                               deps=[t_lw, tok_xn2]
                               + ([pvq["ev"]] if pvq.get("ev") else [])
                               + ([ln_state["ps_e"]]
                                  if ln_state.get("ps_e") else []))
                pvq[f"wt{rcol % 2}"] = tmm
                ln_state["ps_e"] = None
            evs = []
            for h in range(NH):
                pso = ps_o if h == 0 else ps_e
                sl = slice(h * 512, (h + 1) * 512)
                evs.append(S.op("scalar", lambda e, p=pso, sl=sl, rc=rcol:
                                e.activation(out=out_b[:, sl], in_=p[:],
                                             func=AF.Identity,
                                             bias=bq[:, rc:rc + 1]),
                                deps=[tmm, t_set] +
                                ([pvq["ob"]] if pvq["ob"] else [])))
            pvq["ob"] = evs[-1]
            pvq["ev"] = evs[-1]
            if part < 2:
                dst = qd if part == 0 else kd
                t_so = S.dma("gpsimd", dst[rt * 128:(rt + 1) * 128, :],
                             out_b[:], deps=evs, ch=2)
                pvq["ob"] = t_so
                t_qkv_done.append(t_so)
            else:
                tts = []
                for tt in range(To // 128):
                    t_tp = S.op("tensor", lambda e, tt=tt: e.transpose(
                        ps_tb[:], out_b[:, tt * 128:(tt + 1) * 128],
                        idn_b[:]), deps=evs + [t_idn]
                        + ([pvq["vs"]] if pvq["vs"] else []))
                    t_ev = S.op("vector", lambda e, tt=tt, rt=rt: e.tensor_copy(
                        out=vt_s[tt * 128 if False else 0:128,
                                 rt * 128:(rt + 1) * 128]
                        if False else
                        vt_s[0:128, rt * 128:(rt + 1) * 128],
                        in_=ps_tb[:]), deps=[t_tp])
                    t_sv = S.dma("gpsimd",
                                 vt[tt * 128:(tt + 1) * 128,
                                    rt * 128:(rt + 1) * 128],
                                 vt_s[0:128, rt * 128:(rt + 1) * 128],
                                 deps=[t_ev], ch=2)
                    pvq["vs"] = t_sv
                    tts.append(t_sv)
                t_qkv_done.append(tts[-1])
                pvq["ob"] = tts[-1]

    S.op("gpsimd", lambda e: e.engine_nop(),
         deps=[tok_x2[-1], t_qkv_done[-1]], inc=1)

    return {'x2': tok_x2[-1], 'kv': t_qkv_done, 'kd': pvq.get('obk'),
            'last': t_qkv_done[-1]}


def build_ffn0_qkv():
    nc = bass.Bass("TRN2", target_bir_lowering=False)
    To = OWN_T
    NK = D // 128
    NH = To // 512

    x1T = nc.dram_tensor("x1T", [D, To], F32, kind="ExternalInput").ap()
    wqT = nc.dram_tensor("wqT", [D, D], I32, kind="ExternalInput").ap()
    scaleT = nc.dram_tensor("scaleT", [32, D], F32, kind="ExternalInput").ap()
    zeroT = nc.dram_tensor("zeroT", [32, D], F32, kind="ExternalInput").ap()
    bias0 = nc.dram_tensor("bias0", [1, D], F32, kind="ExternalInput").ap()
    wkvT = nc.dram_tensor("wkvT", [D, 3 * PART], F32, kind="ExternalInput").ap()
    bkv_c = nc.dram_tensor("bkv_c", [128, 48], F32, kind="ExternalInput").ap()

    x2T = nc.dram_tensor("x2T", [D, To], F32, kind="ExternalOutput").ap()
    qd = nc.dram_tensor("qd", [PART, To], BF16, kind="ExternalOutput").ap()
    kd = nc.dram_tensor("kd", [PART, To], BF16, kind="ExternalOutput").ap()
    vt = nc.dram_tensor("vt", [To, PART], BF16, kind="ExternalOutput").ap()

    wdeq = nc.dram_tensor("wdeq", [128, 32 * 32 * 128], BF16, kind="Internal").ap()
    wkvb = nc.dram_tensor("wkvb", [128, 48 * 32 * 128], BF16, kind="Internal").ap()
    b0scr = nc.dram_tensor("b0scr", [1, D], BF16, kind="Internal").ap()

    with contextlib.ExitStack() as stk:
        S = Sched(nc, stk)
        IO = dict(x1T=x1T, wqT=wqT, scaleT=scaleT, zeroT=zeroT, bias0=bias0,
                  wkvT=wkvT, bkv_c=bkv_c, x2T=x2T, qd=qd, kd=kd, vt=vt,
                  wdeq=wdeq, wkvb=wkvb, b0scr=b0scr)
        with contextlib.ExitStack() as pstk:
            phase_ffn0(nc, S, pstk, IO)
        S.emit(block=stk.enter_context(nc.Block()))
    return nc


def phase_attn(nc, S, stk, IO):
    To = OWN_T
    NK = D // 128
    NQT = To // 128
    NKB = T // 512
    NCT = PART // 128
    qd = IO.get('qd')
    kg = IO.get('kg')
    vg = IO.get('vg')
    woT = IO.get('woT')
    bo_c = IO.get('bo_c')
    x2T = IO.get('x2T')
    qpos = IO.get('qpos')
    x3T = IO.get('x3T')
    xn3T = IO.get('xn3T')
    x3scr = IO.get('x3scr')
    kga = IO.get('kga')
    vga = IO.get('vga')
    tok_pre = IO.get('tok_pre', [])
    _pfx = IO.get("pfx", "")
    sb = lambda name, shape, dt: stk.enter_context(
        nc.sbuf_tensor(_pfx + name, shape, dt))
    psu = lambda name, shape, dt=F32: stk.enter_context(
        nc.psum_tensor(_pfx + name, shape, dt))

    kh = sb("kh", [128, 2 * T], BF16)
    vh = sb("vh", [128, (T // 128) * 256], BF16)
    qh = sb("qh", [128, 2 * To], BF16)
    vals = sb("vals", [128, NQT * PART], BF16)
    valsT = sb("valsT", [128, NCT * To], BF16)
    NKB2 = T // 1024
    iota_r = sb("iota_r", [128, 1024], F32)
    qkc = sb("qkc", [128, NQT * NKB2], F32)
    selm2 = [sb(f"selm{i}", [128, 1024], F32) for i in range(2)]
    sc_f2 = [sb(f"sc_f{i}", [128, 1024], F32) for i in range(2)]
    p_b2 = [sb(f"p_b{i}", [128, 1024], BF16) for i in range(2)]
    pt_b2 = [sb(f"pt_b{i}", [128, 1024], BF16) for i in range(2)]
    vacc = sb("vacc", [128, 256], F32)
    m_run = sb("m_run", [128, 1], F32)
    mn_c = sb("mn_c", [128, 1], F32)
    am2 = [sb(f"am_c{i}", [128, 1], F32) for i in range(2)]
    ac2 = [sb(f"ac_c{i}", [128, 1], F32) for i in range(2)]
    lb2 = [sb(f"lb_c{i}", [128, 1], F32) for i in range(2)]
    l_run = sb("l_run", [128, 1], F32)
    rl_c = sb("rl_c", [128, 1], F32)
    idn_b = sb("idn_b", [128, 128], BF16)
    bo_sb = sb("bo_sb", [128, 32], F32)
    qp_c = sb("qp_c", [128, NQT], F32)
    onesc = sb("onesc", [128, 1], F32)
    ones1r = sb("ones1r", [1, 128], F32)
    ceps = sb("ceps", [128, 1], F32)

    pstk = contextlib.ExitStack()
    psu2 = lambda name, shape, dt=F32: pstk.enter_context(
        nc.psum_tensor(_pfx + name, shape, dt))
    ps_s2 = [psu2(f"ps_s{i}", [128, 1024]) for i in range(2)]
    ps_v2 = [psu2(f"ps_v{i}", [128, 256]) for i in range(2)]
    ps_tb2 = [psu2(f"ps_tb{i}", [128, 1024], BF16) for i in range(2)]
    NH2 = To // 512

    t_bo = S.dma("sync", bo_sb[:], bo_c, ch=0)
    t_qp = S.dma("sync", qp_c[:], qpos, ch=0)
    t_set = t_bo = t_qp
    t_onesc = S.op("vector", lambda e: e.memset(onesc[:], 1.0))
    t_ones1r = S.op("vector", lambda e: e.memset(ones1r[:], 1.0))
    t_ceps = S.op("vector", lambda e: e.memset(ceps[:], LN_EPS))
    t_id0 = S.op("gpsimd", lambda e: e.memset(idn_b[:], 0.0), inc=1)
    t_idn = S.op("gpsimd", lambda e: e.affine_select(
        out=idn_b[:], in_=idn_b[:], compare_op=ALU.not_equal, fill=1.0,
        base=0, pattern=[[-1, 128]], channel_multiplier=1),
        deps=[t_id0], inc=1)
    dqstk = contextlib.ExitStack()
    if IO.get("dq_io") is not None:
        _dq_emit, _dq_n, _dq_done = make_dequant_emitter(
            nc, S, dqstk, IO["dq_io"], pfx="dq_")
        IO["dq_emit"] = _dq_emit
        IO["dq_done"] = _dq_done
        IO["dq_per_head"] = (_dq_n + HEADS - 1) // HEADS
    t_iot = S.op("gpsimd", lambda e: e.iota(
        out=iota_r[:], pattern=[[1, 1024]], base=0, channel_multiplier=0,
        allow_small_or_imprecise_dtypes=True), inc=1)
    t_qkc = None
    for qt in range(NQT):
        for kb in range(NKB2):
            t_qkc = S.op("vector", lambda e, qt=qt, kb=kb: e.tensor_scalar(
                out=qkc[:, qt * NKB2 + kb:qt * NKB2 + kb + 1],
                in0=qp_c[:, qt:qt + 1], scalar1=float(-kb * 1024),
                scalar2=None, op0=ALU.add), deps=[t_set])

    pv = {}
    for h in range(HEADS):
        if kga is None:
            t_k = S.dma("sync", kh[:].rearrange("p (c t) -> p c t", c=2),
                        kg.rearrange("(c p) t -> p c t", p=128)[
                            :, 2 * h:2 * h + 2, :],
                        deps=[pv["kh"]] if "kh" in pv else [], ch=1)
            t_v = S.dma("sync", vh[:].rearrange("p (a c) -> p a c", a=T // 128),
                        vg.rearrange("(a p) c -> p a c", p=128)[
                            :, :, 256 * h:256 * (h + 1)],
                        deps=[pv["vh"]] if "vh" in pv else [], ch=2)
        else:
            kd_ = [pv["kh"]] if "kh" in pv else []
            kd_ = kd_ + list(tok_pre)
            for cs in range(2):
                for r in range(8):
                    t_k = S.dma("sync",
                                kh[:].rearrange("p (c t) -> p c t", c=2)
                                [:, cs, r * To:(r + 1) * To],
                                kga[r, (2 * h + cs) * 128:
                                    (2 * h + cs + 1) * 128, :],
                                deps=kd_, ch=1)
            vd_ = [pv["vh"]] if "vh" in pv else []
            vd_ = vd_ + list(tok_pre)
            tpr = To // 128
            for aa in range(T // 128):
                t_v = S.dma("sync",
                            vh[:].rearrange("p (a c) -> p a c", a=T // 128)
                            [:, aa, :],
                            vga[aa // tpr, (aa % tpr) * 128:
                                (aa % tpr + 1) * 128,
                                256 * h:256 * (h + 1)],
                            deps=vd_, ch=2)
        t_q = S.dma("sync", qh[:].rearrange("p (c t) -> p c t", c=2),
                    qd.rearrange("(c p) t -> p c t", p=128)[
                        :, 2 * h:2 * h + 2, :],
                    deps=([pv["qh"]] if "qh" in pv else []) + list(tok_pre),
                    ch=3)
        if IO.get("dq_emit"):
            IO["dq_emit"](IO.get("dq_per_head", 10))
        for qt in range(NQT):
            t_ms = S.op("vector", lambda e: e.memset(m_run[:], 1e30),
                        deps=[pv["fin"]] if "fin" in pv else [])
            t_ls = S.op("vector", lambda e: e.memset(l_run[:], 0.0))
            t_vs = S.op("vector", lambda e: e.memset(vacc[:], 0.0))
            for kb in range(NKB2):
                par = kb % 2
                j = qt * NKB2 + kb
                ps_s = ps_s2[par]
                ps_v = ps_v2[par]
                ps_tb = ps_tb2[par]
                selm = selm2[par]
                sc_f = sc_f2[par]
                p_b = p_b2[par]
                pt_b = pt_b2[par]
                am_c = am2[par]
                ac_c = ac2[par]
                lb_c = lb2[par]
                tmm = None
                for half in range(2):
                    psl = slice(half * 512, (half + 1) * 512)
                    for cs in range(2):
                        tmm = S.op("tensor", lambda e, cs=cs, qt=qt, kb=kb,
                                   psl=psl, half=half, ps_s=ps_s: e.matmul(
                            ps_s[:, psl],
                            qh[:, cs * To + qt * 128:cs * To + (qt + 1) * 128],
                            kh[:, cs * T + kb * 1024 + half * 512:
                               cs * T + kb * 1024 + half * 512 + 512],
                            start=(cs == 0), stop=(cs == 1)),
                            deps=[t_k, t_q]
                            + ([pv[("sc", par)]] if ("sc", par) in pv else []))
                t_sel = S.op("vector", lambda e, j=j, selm=selm:
                             e.tensor_scalar(
                    out=selm[:], in0=iota_r[:], scalar1=qkc[:, j:j + 1],
                    scalar2=-10000.0, op0=ALU.is_gt, op1=ALU.mult),
                    deps=[t_qkc, t_iot])
                t_sc = S.op("vector", lambda e, ps_s=ps_s, selm=selm,
                            sc_f=sc_f: e.tensor_tensor(
                    out=sc_f[:], in0=ps_s[:], in1=selm[:], op=ALU.add),
                    deps=[tmm]
                    + ([pv[("pexpr", par)]] if ("pexpr", par) in pv else []))
                pv[("sc", par)] = t_sc
                nm_old = m_run if kb % 2 == 0 else mn_c
                nm_new = mn_c if kb % 2 == 0 else m_run
                t_rm = S.op("vector", lambda e, am_c=am_c, sc_f=sc_f:
                            e.reduce_max(
                    out=am_c[:], in_=sc_f[:], axis=AX.X, negate=True),
                    deps=([pv[("amr", par)]] if ("amr", par) in pv else []))
                t_mn = S.op("vector", lambda e, a=nm_old, b=nm_new,
                            am_c=am_c: e.tensor_tensor(
                                out=b[:], in0=am_c[:], in1=a[:],
                                op=ALU.min),
                            deps=[t_ms]
                            + ([pv[("nmr", par)]] if ("nmr", par) in pv
                               else []))
                t_am = S.op("vector", lambda e, a=nm_old, b=nm_new,
                            am_c=am_c: e.tensor_tensor(
                                out=am_c[:], in0=b[:], in1=a[:],
                                op=ALU.subtract))
                t_al = S.op("scalar", lambda e, am_c=am_c, ac_c=ac_c:
                            e.activation(
                    out=ac_c[:], in_=am_c[:], func=AF.Exp),
                    deps=[t_am]
                    + ([pv[("acr", par)]] if ("acr", par) in pv else []))
                pv[("amr", par)] = t_al
                t_pe = S.op("scalar", lambda e, b=nm_new, p_b=p_b,
                            sc_f=sc_f, lb_c=lb_c: e.activation(
                    out=p_b[:], in_=sc_f[:], func=AF.Exp, bias=b[:],
                    accum_out=lb_c[:]),
                    deps=[t_mn, t_al]
                    + ([pv[("ptp", par)]] if ("ptp", par) in pv else [])
                    + ([pv[("lbr", par)]] if ("lbr", par) in pv else []))
                pv[("pexpr", par)] = t_pe
                pv[("nmr", par)] = t_pe
                t_lr = S.op("vector", lambda e, ac_c=ac_c, lb_c=lb_c:
                            e.scalar_tensor_tensor(
                    out=l_run[:], in0=l_run[:], scalar=ac_c[:], in1=lb_c[:],
                    op0=ALU.mult, op1=ALU.add), deps=[t_pe, t_ls])
                pv[("lbr", par)] = t_lr
                tps = []
                for st4 in range(8):
                    t_tp = S.op("tensor", lambda e, st4=st4, ps_tb=ps_tb,
                                p_b=p_b: e.transpose(
                        ps_tb[:, st4 * 128:(st4 + 1) * 128],
                        p_b[:, st4 * 128:(st4 + 1) * 128], idn_b[:]),
                        deps=[t_pe, t_idn]
                        + ([pv[("ptev", par)]] if ("ptev", par) in pv
                           else []))
                    tps.append(t_tp)
                pv[("ptp", par)] = tps[-1]
                t_ev = S.op("vector", lambda e, pt_b=pt_b, ps_tb=ps_tb:
                            e.tensor_copy(
                    out=pt_b[:], in_=ps_tb[:]),
                    deps=[tps[-1]]
                    + ([pv[("av", par)]] if ("av", par) in pv else []))
                pv[("ptev", par)] = t_ev
                tav = None
                for st4 in range(8):
                    tav = S.op("tensor", lambda e, st4=st4, kb=kb,
                               ps_v=ps_v, pt_b=pt_b: e.matmul(
                        ps_v[:], pt_b[:, st4 * 128:(st4 + 1) * 128],
                        vh[:, (kb * 8 + st4) * 256:(kb * 8 + st4 + 1) * 256],
                        start=(st4 == 0), stop=(st4 == 7)),
                        deps=[t_ev, t_v]
                        + ([pv[("vaccr", par)]] if ("vaccr", par) in pv
                           else []))
                pv[("av", par)] = tav
                t_va = S.op("vector", lambda e, ac_c=ac_c, ps_v=ps_v:
                            e.scalar_tensor_tensor(
                    out=vacc[:], in0=vacc[:], scalar=ac_c[:], in1=ps_v[:],
                    op0=ALU.mult, op1=ALU.add), deps=[tav, t_al, t_vs])
                pv[("vaccr", par)] = t_va
                pv[("acr", par)] = t_va
            t_rl = S.op("vector", lambda e: e.reciprocal(
                out=rl_c[:], in_=l_run[:]), deps=[t_lr])
            t_ob = S.op("vector", lambda e, h=h, qt=qt: e.tensor_scalar(
                out=vals[:, qt * PART + h * 256:qt * PART + (h + 1) * 256],
                in0=vacc[:], scalar1=rl_c[:], scalar2=None, op0=ALU.mult),
                deps=[t_va, t_rl])
            pv["fin"] = t_ob
        pv["kh"] = tmm
        pv["vh"] = tav
        pv["qh"] = tmm
    if IO.get("dq_emit"):
        IO["dq_emit"](1000)
        pv3_seed = list(IO["dq_done"]())
    else:
        pv3_seed = []
    dqstk.close()
    wo_f = sb("wo_f", [128, 128], F32)
    wo_b = sb("wo_b", [128, 128], BF16)
    wo_f2 = sb("wo_f2", [128, 128], F32)
    wo_b2 = sb("wo_b2", [128, 128], BF16)
    xt_a = sb("xt_a", [128, To], F32)
    xt_b = sb("xt_b", [128, To], F32)
    sqt = sb("sqt", [128, To], F32)
    mrep = sb("mrep", [128, To], F32)
    rrep = sb("rrep", [128, To], F32)
    srow = sb("srow", [1, To], F32)
    qrow = sb("qrow", [1, To], F32)
    trow = sb("trow", [1, To], F32)
    xnb = sb("xnb", [128, To], BF16)

    pv2 = {}
    t_vT = None
    for qt in range(NQT):
        for ct in range(NCT):
            t_tp = S.op("tensor", lambda e, qt=qt, ct=ct: e.transpose(
                ps_tb2[0][:, 0:128],
                vals[:, qt * PART + ct * 128:qt * PART + (ct + 1) * 128],
                idn_b[:]), deps=[pv["fin"], t_idn]
                + ([t_vT] if t_vT else [pv.get(("ptev", 0))]))
            t_vT = S.op("vector", lambda e, qt=qt, ct=ct: e.tensor_copy(
                out=valsT[:, ct * To + qt * 128:ct * To + qt * 128 + 128],
                in_=ps_tb2[0][:, 0:128]), deps=[t_tp])
    pstk.close()
    ps_o = psu("ps_o", [128, 512])
    ps_e = psu("ps_e", [128, 512])
    ps_r = [psu(f"ps_r{i}", [1, 512]) for i in range(2)]
    ps_q = [ps_o, ps_e]

    tok_x3 = []
    pv3 = {}
    for nt in range(NK):
        tmm = None
        for ct in range(NCT):
            wf = wo_f if ct % 2 == 0 else wo_f2
            wb = wo_b if ct % 2 == 0 else wo_b2
            t_lw = S.dma("sync", wf[:],
                         woT[ct * 128:(ct + 1) * 128, nt * 128:(nt + 1) * 128],
                         deps=[pv3["w"]] if "w" in pv3 else pv3_seed,
                         ch=4 + ct % 2)
            t_cw = S.op("vector", lambda e, wf=wf, wb=wb: e.tensor_copy(
                out=wb[:], in_=wf[:]),
                deps=[t_lw] + ([pv3["w2"]] if "w2" in pv3 else []))
            pv3["w"] = t_cw
            for hh in range(NH2):
                sl = slice(hh * 512, (hh + 1) * 512)
                tmm = S.op("tensor", lambda e, wb=wb, ct=ct, sl=sl, hh=hh:
                           e.matmul(ps_o[:] if hh == 0 else ps_e[:], wb[:],
                                    valsT[:, ct * To + sl.start:ct * To + sl.stop],
                                    start=(ct == 0), stop=(ct == NCT - 1)),
                           deps=[t_cw, t_vT] + pv3_seed
                           + ([pv3["ev"]] if "ev" in pv3 else []))
            pv3["w2"] = tmm
        t_rx = S.dma("sync", xt_a[:], x2T[nt * 128:(nt + 1) * 128, :],
                     deps=[pv3["rx"]] if "rx" in pv3 else pv3_seed, ch=1)
        evs = []
        for hh in range(NH2):
            pso = ps_o if hh == 0 else ps_e
            sl = slice(hh * 512, (hh + 1) * 512)
            t_bi = S.op("scalar", lambda e, p=pso, sl=sl, nt=nt: e.activation(
                out=sqt[:, sl], in_=p[:], func=AF.Identity,
                bias=bo_sb[:, nt:nt + 1]),
                deps=[tmm, t_set] + ([pv3["ev"]] if "ev" in pv3 else [])
                + pv3.get("sq", []))
            evs.append(t_bi)
        pv3["ev"] = evs[-1]
        t_x3 = S.op("vector", lambda e: e.tensor_tensor(
            out=sqt[:], in0=sqt[:], in1=xt_a[:], op=ALU.add),
            deps=evs + [t_rx])
        pv3["rx"] = t_x3
        t_s2 = S.dma("gpsimd", x3scr[nt * 128:(nt + 1) * 128, :], sqt[:],
                     deps=[t_x3], ch=1)
        if x3T is not None:
            t_s1 = S.dma("gpsimd", x3T[nt * 128:(nt + 1) * 128, :], sqt[:],
                         deps=[t_x3], ch=0)
        else:
            t_s1 = t_s2
        pv3["ev"] = t_x3
        pv3["sq"] = [t_s1, t_s2]
        tok_x3 += [t_s1, t_s2]

    tma = tmb = None
    prevload = {0: None, 1: None}
    war = [tok_x3[-2], tok_x3[-1], pv3["rx"]]
    for g in range(NK):
        buf = xt_a if g % 2 == 0 else xt_b
        pu = prevload[g % 2]
        t_l = S.dma("sync", buf[:], x3scr[g * 128:(g + 1) * 128, :],
                    deps=([pu] if pu else []) + war, ch=1 + g % 2)
        t_sq = S.op("scalar", lambda e, b=buf: e.activation(
            out=sqt[:], in_=b[:], func=AF.Square),
            deps=[t_l] + ([tmb] if tmb else []) + [pv3["rx"]]
            + pv3.get("sq", []))
        for hh in range(NH2):
            sl = slice(hh * 512, (hh + 1) * 512)
            tma = S.op("tensor", lambda e, b=buf, sl=sl, g=g, hh=hh:
                       e.matmul(ps_r[hh][:], onesc[:], b[:, sl],
                                start=(g == 0), stop=(g == NK - 1)),
                       deps=[t_l, t_onesc] + war)
            tmb = S.op("tensor", lambda e, sl=sl, g=g, hh=hh: e.matmul(
                ps_q[hh][0:1, :], onesc[:], sqt[:, sl],
                start=(g == 0), stop=(g == NK - 1)),
                deps=[t_sq] + ([pv3["ev"]] if g == 0 else []))
        prevload[g % 2] = tmb
    ev = []
    for hh in range(NH2):
        sl = slice(hh * 512, (hh + 1) * 512)
        ev.append(S.op("vector", lambda e, sl=sl, hh=hh: e.tensor_scalar(
            out=srow[:, sl], in0=ps_r[hh][:], scalar1=1.0 / D,
            scalar2=None, op0=ALU.mult), deps=[tma, tmb]))
        ev.append(S.op("vector", lambda e, sl=sl, hh=hh: e.tensor_scalar(
            out=qrow[:, sl], in0=ps_q[hh][0:1, :], scalar1=1.0 / D,
            scalar2=None, op0=ALU.mult), deps=[tmb]))
    t_v1 = S.op("vector", lambda e: e.tensor_tensor(
        out=trow[:], in0=srow[:], in1=srow[:], op=ALU.mult), deps=ev)
    t_v2 = S.op("vector", lambda e: e.tensor_tensor(
        out=qrow[:], in0=qrow[:], in1=trow[:], op=ALU.subtract))
    t_v3 = S.op("scalar", lambda e: e.activation(
        out=trow[:], in_=qrow[:], func=AF.Sqrt, bias=ceps[0:1, 0:1]),
        deps=[t_v2, t_ceps])
    t_v4 = S.op("vector", lambda e: e.reciprocal(
        out=qrow[:], in_=trow[:]), deps=[t_v3])
    evb = None
    for hh in range(NH2):
        sl = slice(hh * 512, (hh + 1) * 512)
        t_m1 = S.op("tensor", lambda e, sl=sl: e.matmul(
            ps_e[:], ones1r[:], srow[0:1, sl], start=True, stop=True),
            deps=[t_v4, t_ones1r] + ([evb] if evb else [pv3["ev"]]))
        t_m2 = S.op("vector", lambda e, sl=sl: e.tensor_copy(
            out=mrep[:, sl], in_=ps_e[:]), deps=[t_m1])
        t_m3 = S.op("tensor", lambda e, sl=sl: e.matmul(
            ps_e[:], ones1r[:], qrow[0:1, sl], start=True, stop=True),
            deps=[t_m2])
        evb = S.op("vector", lambda e, sl=sl: e.tensor_copy(
            out=rrep[:, sl], in_=ps_e[:]), deps=[t_m3])
    tok_norm = evb
    prevload = {0: None, 1: None}
    t_last = None
    for g in range(NK):
        buf = xt_a if g % 2 == 0 else xt_b
        pu = prevload[g % 2]
        t_l = S.dma("sync", buf[:], x3scr[g * 128:(g + 1) * 128, :],
                    deps=([pu] if pu else [tma, tmb]), ch=1 + g % 2)
        t_n1 = S.op("vector", lambda e, b=buf: e.tensor_tensor(
            out=sqt[:], in0=b[:], in1=mrep[:], op=ALU.subtract),
            deps=[t_l, tok_norm, tmb])
        t_n2 = S.op("vector", lambda e: e.tensor_tensor(
            out=xnb[:], in0=sqt[:], in1=rrep[:], op=ALU.mult),
            deps=([t_last] if t_last else []))
        t_st = S.dma("gpsimd", xn3T[g * 128:(g + 1) * 128, :], xnb[:],
                     deps=[t_n2], ch=2)
        prevload[g % 2] = t_st
        t_last = t_st

    S.op("gpsimd", lambda e: e.engine_nop(),
         deps=[t_last, tok_x3[-2]], inc=1)

    return {'x3': tok_x3, 'xn3': t_last}


def build_attn():
    nc = bass.Bass("TRN2", target_bir_lowering=False)
    To = OWN_T
    NK = D // 128
    NQT = To // 128
    NKB = T // 512
    NCT = PART // 128

    qd = nc.dram_tensor("qd", [PART, To], BF16, kind="ExternalInput").ap()
    kg = nc.dram_tensor("kg", [PART, T], BF16, kind="ExternalInput").ap()
    vg = nc.dram_tensor("vg", [T, PART], BF16, kind="ExternalInput").ap()
    woT = nc.dram_tensor("woT", [PART, D], F32, kind="ExternalInput").ap()
    bo_c = nc.dram_tensor("bo_c", [128, 32], F32, kind="ExternalInput").ap()
    x2T = nc.dram_tensor("x2T", [D, To], F32, kind="ExternalInput").ap()
    qpos = nc.dram_tensor("qpos", [128, NQT], F32, kind="ExternalInput").ap()

    x3T = nc.dram_tensor("x3T", [D, To], F32, kind="ExternalOutput").ap()
    xn3T = nc.dram_tensor("xn3T", [D, To], BF16, kind="ExternalOutput").ap()
    x3scr = nc.dram_tensor("x3scr", [D, To], F32, kind="Internal").ap()

    with contextlib.ExitStack() as stk:
        S = Sched(nc, stk)
        IO = dict(qd=qd, kg=kg, vg=vg, woT=woT, bo_c=bo_c, x2T=x2T, qpos=qpos,
                  x3T=x3T, xn3T=xn3T, x3scr=x3scr)
        with contextlib.ExitStack() as pstk:
            phase_attn(nc, S, pstk, IO)
        S.emit(block=stk.enter_context(nc.Block()))
    return nc


HSL = 1408


def make_dequant_emitter(nc, S, stk, IO, pfx="dq_"):
    NK = D // 128
    NHT = HSL // 128
    w2c, a2X = IO["w2c"], IO["a2X"]
    w2d = IO["w2d"]
    sb = lambda name, shape, dt: stk.enter_context(
        nc.sbuf_tensor(pfx + name, shape, dt))
    WQC = 2048
    code_i = sb("code_i", [128, WQC], I32)
    code_h = sb("code_h", [128, WQC], mybir.dt.float16)
    lut_v = sb("lut_v", [128, WQC], mybir.dt.float16)
    lut_t = sb("lut_t", [128, WQC], mybir.dt.float16)
    a_repf = sb("a_repf", [128, WQC], F32)
    wd_b = sb("wd_b", [128, WQC], BF16)
    pv = {}
    last = {}

    def emit_tile(codes, aX, out_dram, a, width, tiled):
        for wh in range(0, width, WQC):
            wsl = slice(wh, min(wh + WQC, width))
            w = wsl.stop - wsl.start
            t_lc = S.dma("gpsimd", code_i[:, 0:w],
                         codes[a * 128:(a + 1) * 128, wsl],
                         deps=[pv["code"]] if pv.get("code") else [], ch=4)
            t_la = S.dma("gpsimd", a_repf[:, 0:w],
                         aX[a * 128:(a + 1) * 128, wsl],
                         deps=[pv["arep"]] if pv.get("arep") else [], ch=6)
            last["lc"] = t_lc
            last["la"] = t_la
            t_cf = S.op("gpsimd", lambda e, w=w: e.tensor_copy(
                out=code_h[:, 0:w], in_=code_i[:, 0:w]), deps=[t_lc], inc=1)
            pv["code"] = t_cf
            t_acc = S.op("gpsimd", lambda e, w=w: e.tensor_scalar(
                out=lut_v[:, 0:w], in0=code_h[:, 0:w], scalar1=0.0,
                scalar2=float(NF4[0]), op0=ALU.is_equal, op1=ALU.mult),
                deps=[t_cf], inc=1)
            for i in range(1, 16):
                S.op("gpsimd", lambda e, i=i, w=w: e.tensor_scalar(
                    out=lut_t[:, 0:w], in0=code_h[:, 0:w],
                    scalar1=float(i), scalar2=float(NF4[i]),
                    op0=ALU.is_equal, op1=ALU.mult), inc=1)
                t_acc = S.op("gpsimd", lambda e, w=w: e.tensor_tensor(
                    out=lut_v[:, 0:w], in0=lut_v[:, 0:w],
                    in1=lut_t[:, 0:w], op=ALU.add), inc=1)
            t_mul = S.op("gpsimd", lambda e, w=w: e.tensor_tensor(
                out=wd_b[:, 0:w], in0=lut_v[:, 0:w], in1=a_repf[:, 0:w],
                op=ALU.mult),
                deps=[t_la, t_acc] + ([last["st"]] if last.get("st") else []),
                inc=1)
            pv["arep"] = t_mul
            last["vec"] = t_mul
            t_st = S.dma("gpsimd",
                         out_dram[a * 128:(a + 1) * 128, wsl],
                         wd_b[:, 0:w], deps=[t_mul], ch=5)
            last["st"] = t_st
        return last["st"]

    units = [(w2c, a2X, w2d, a, D, False) for a in range(NHT)]
    state = {"i": 0}

    def emit_units(k):
        last_t = None
        while k > 0 and state["i"] < len(units):
            last_t = emit_tile(*units[state["i"]])
            state["i"] += 1
            k -= 1
        return last_t

    def done_tokens():
        return [t for t in last.values() if t is not None]

    return emit_units, len(units), done_tokens


def phase_ffn1(nc, S, stk, IO):
    To = OWN_T
    NK = D // 128
    NHT = HSL // 128
    NTC = T // 512
    xg = IO.get('xg')
    w1c = IO.get('w1c')
    a1S = IO.get('a1S')
    w3c = IO.get('w3c')
    a3S = IO.get('a3S')
    w2c = IO.get('w2c')
    a2S = IO.get('a2S')
    x3T = IO.get('x3T')
    x4 = IO.get('x4')
    w1d = IO.get('w1d')
    w3d = IO.get('w3d')
    w2d = IO.get('w2d')
    pb = IO.get('pb')
    rs_out = IO.get('rs_out')
    xga = IO.get('xga')
    tok_pre = IO.get('tok_pre', [])
    def w1d_3(t):
        return t.rearrange("p (ht a hh) -> p ht a hh", ht=NHT, a=NK)

    _pfx = IO.get("pfx", "")
    sb = lambda name, shape, dt: stk.enter_context(
        nc.sbuf_tensor(_pfx + name, shape, dt))
    psu = lambda name, shape, dt=F32: stk.enter_context(
        nc.psum_tensor(_pfx + name, shape, dt))

    WQC = 2048
    code_i = sb("code_i", [128, WQC], I32)
    code_h = sb("code_h", [128, WQC], mybir.dt.float16)
    lut_v = sb("lut_v", [128, WQC], mybir.dt.float16)
    lut_t = sb("lut_t", [128, WQC], mybir.dt.float16)
    a_rep = sb("a_rep", [128, WQC], mybir.dt.float16)
    a_rf = sb("a_rf", [128, 512], F32)
    wd_b = sb("wd_b", [128, WQC], BF16)
    arow = sb("arow", [1, WQC], F32)
    xtc_a = sb("xtc_a", [128, NK * 512], BF16)
    xtc_b = sb("xtc_b", [128, NK * 512], BF16)
    hbuf_a = sb("hbuf_a", [128, NHT * 512], BF16)
    hbuf_b = sb("hbuf_b", [128, NHT * 512], BF16)
    w1s_a = sb("w1s_a", [128, NK * 128], BF16)
    w1s_b = sb("w1s_b", [128, NK * 128], BF16)
    w3s_a = sb("w3s_a", [128, NK * 128], BF16)
    w3s_b = sb("w3s_b", [128, NK * 128], BF16)
    w2t_a = sb("w2t_a", [128, 512], BF16)
    w2t_b = sb("w2t_b", [128, 512], BF16)
    sil = sb("sil", [128, 512], F32)
    hpt_a = sb("hpt_a", [128, 512], F32)
    hpt_b = sb("hpt_b", [128, 512], F32)
    ones1r = sb("ones1r", [1, 128], F32)
    idn_f = sb("idn_f", [128, 128], F32)
    otile = sb("otile", [128, D], F32)
    rst = sb("rst", [128, D], F32)
    x3blk = sb("x3blk", [128, 128], F32)

    ps_1 = psu("ps_1", [128, 512])
    ps_3 = psu("ps_3", [128, 512])
    ps_t = [psu(f"ps_t{i}", [128, 512]) for i in range(4)]
    ps_e = psu("ps_e", [128, 512])
    ps_f = psu("ps_f", [128, 128])


    t_ones = S.op("vector", lambda e: e.memset(ones1r[:], 1.0))
    t_id0 = S.op("gpsimd", lambda e: e.memset(idn_f[:], 0.0), inc=1)
    t_idn = S.op("gpsimd", lambda e: e.affine_select(
        out=idn_f[:], in_=idn_f[:], compare_op=ALU.not_equal, fill=1.0,
        base=0, pattern=[[-1, 128]], channel_multiplier=1),
        deps=[t_id0], inc=1)

    pv = {}

    def emit_nf4_lut(width, dep_load):
        t_cf = S.op("vector", lambda e: e.tensor_copy(
            out=code_h[:, 0:width], in_=code_i[:, 0:width]), deps=[dep_load])
        t_acc = S.op("vector", lambda e: e.tensor_scalar(
            out=lut_v[:, 0:width], in0=code_h[:, 0:width], scalar1=0.0,
            scalar2=float(NF4[0]), op0=ALU.is_equal, op1=ALU.mult),
            deps=[t_cf])
        for i in range(1, 16):
            t_i = S.op("vector", lambda e, i=i: e.tensor_scalar(
                out=lut_t[:, 0:width], in0=code_h[:, 0:width],
                scalar1=float(i), scalar2=float(NF4[i]), op0=ALU.is_equal,
                op1=ALU.mult))
            t_acc = S.op("vector", lambda e: e.tensor_tensor(
                out=lut_v[:, 0:width], in0=lut_v[:, 0:width],
                in1=lut_t[:, 0:width], op=ALU.add))
        return t_acc

    def emit_arep(a_dram, row0, wofs, width, dep):
        toks = []
        for half in range(2):
            t_ar = S.dma("gpsimd", arow[:, 0:width],
                         a_dram[row0 + half:row0 + half + 1,
                                wofs:wofs + width],
                         deps=dep + (toks[-2:] if toks else []), ch=2)
            for q in range((width + 511) // 512):
                sl = slice(q * 512, min((q + 1) * 512, width))
                w = sl.stop - sl.start
                t_bm = S.op("tensor", lambda e, sl=sl, w=w: e.matmul(
                    ps_e[0:64, 0:w], ones1r[0:1, 0:64], arow[0:1, sl],
                    start=True, stop=True),
                    deps=[t_ar, t_ones] + (toks[-1:] if toks else [])
                    + ([pv["ps_e"]] if pv.get("ps_e") else []))
                t_e1 = S.op("vector", lambda e, w=w: e.tensor_copy(
                    out=a_rf[0:64, 0:w], in_=ps_e[0:64, 0:w]), deps=[t_bm])
                t_ev = S.op("vector", lambda e, sl=sl, half=half, w=w:
                            e.tensor_copy(
                                out=a_rep[half * 64:(half + 1) * 64, sl],
                                in_=a_rf[0:64, 0:w]), deps=[t_e1])
                pv["ps_e"] = t_e1
                toks.append(t_bm)
                toks.append(t_ev)
        return toks[-1]

    def emit_dequant_tile(codes, a_dram, out_dram, a, width, tiled):
        last_st = pv.get("wst")
        for wh in range(0, width, WQC):
            wsl = slice(wh, min(wh + WQC, width))
            w = wsl.stop - wsl.start
            t_lc = S.dma("sync", code_i[:, 0:w],
                         codes[a * 128:(a + 1) * 128, wsl],
                         deps=[pv["code"]] if pv.get("code") else [], ch=0)
            t_lv = emit_nf4_lut(w, t_lc)
            pv["code"] = t_lv
            t_arep = emit_arep(a_dram, 2 * a, wh, w,
                               [pv["arep"]] if pv.get("arep") else [])
            t_mul = S.op("vector", lambda e, w=w: e.tensor_tensor(
                out=lut_v[:, 0:w], in0=lut_v[:, 0:w],
                in1=a_rep[:, 0:w], op=ALU.mult), deps=[t_arep, t_lv])
            pv["arep"] = t_mul
            t_cb = S.op("vector", lambda e, w=w: e.tensor_copy(
                out=wd_b[:, 0:w], in_=lut_v[:, 0:w]),
                deps=([last_st] if last_st else []))
            if tiled:
                for hti in range(w // 128):
                    ht = (wh + hti * 128) // 128
                    t_st = S.dma("gpsimd",
                                 w1d_3(out_dram)[:, ht, a, :],
                                 wd_b[:, hti * 128:(hti + 1) * 128],
                                 deps=[t_cb], ch=0)
            else:
                t_st = S.dma("gpsimd",
                             out_dram[a * 128:(a + 1) * 128, wsl],
                             wd_b[:, 0:w], deps=[t_cb], ch=0)
            last_st = t_st
            pv["wst"] = t_st
        return last_st

    def emit_dequant(codes, a_dram, out_dram, ntile, width, tiled):
        last = None
        for a in range(ntile):
            last = emit_dequant_tile(codes, a_dram, out_dram, a, width, tiled)
        return last

    if IO.get("dq_external") == "all":
        tok_deq = list(IO.get("tok_pre", []))
    elif IO.get("dq_external") == "w2":
        t_d1 = emit_dequant(w1c, a1S, w1d, NK, HSL, True)
        t_d3 = emit_dequant(w3c, a3S, w3d, NK, HSL, True)
        tok_deq = [t_d1, t_d3] + list(IO.get("tok_pre", []))
    else:
        t_d1 = emit_dequant(w1c, a1S, w1d, NK, HSL, True)
        t_d3 = emit_dequant(w3c, a3S, w3d, NK, HSL, True)
        t_d2 = emit_dequant(w2c, a2S, w2d, NHT, D, False)
        tok_deq = [t_d1, t_d3, t_d2]

    pm = {}
    t_pb_last = None
    for tc in range(NTC):
        xtc = xtc_a if tc % 2 == 0 else xtc_b
        hbuf = hbuf_a if tc % 2 == 0 else hbuf_b
        tsl = slice(tc * 512, (tc + 1) * 512)
        xw = [pm[f"xr{tc % 2}"]] if pm.get(f"xr{tc % 2}") else []
        t_x = None
        t_xs = []
        if xga is None:
            for a2 in range(2):
                t_xs.append(S.dma("sync",
                            xtc[:].rearrange("p (a t) -> p a t", a=NK)
                            [:, a2 * 16:(a2 + 1) * 16, :],
                            xg.rearrange("(a p) t -> p a t", p=128)
                            [:, a2 * 16:(a2 + 1) * 16, tsl],
                            deps=xw, ch=1 + 7 * a2))
        else:
            cpr = To // 512
            lsl = slice((tc % cpr) * 512, (tc % cpr) * 512 + 512)
            for a2 in range(2):
                t_xs.append(S.dma("sync",
                            xtc[:].rearrange("p (a t) -> p a t", a=NK)
                            [:, a2 * 16:(a2 + 1) * 16, :],
                            xga[tc // cpr].rearrange("(a p) t -> p a t", p=128)
                            [:, a2 * 16:(a2 + 1) * 16, lsl],
                            deps=xw + list(tok_pre), ch=1 + 7 * a2))
        t_x = t_xs
        for ht in range(NHT):
            w1b = w1s_a if ht % 2 == 0 else w1s_b
            w3b = w3s_a if ht % 2 == 0 else w3s_b
            war1 = [pm[f"mm1{ht % 2}"]] if pm.get(f"mm1{ht % 2}") else tok_deq
            t_w1 = S.dma("sync", w1b[:], w1d[:, ht * NK * 128:(ht + 1) * NK * 128],
                         deps=war1, ch=2 + ht % 2)
            t_w3 = S.dma("sync", w3b[:], w3d[:, ht * NK * 128:(ht + 1) * NK * 128],
                         deps=war1, ch=6 + ht % 2)
            tm1 = tm3 = None
            for a in range(NK):
                tm1 = S.op("tensor", lambda e, a=a, w=w1b, xtc=xtc: e.matmul(
                    ps_1[:], w[:, a * 128:(a + 1) * 128],
                    xtc[:, a * 512:(a + 1) * 512],
                    start=(a == 0), stop=(a == NK - 1)),
                    deps=[t_w1] + t_x + ([pm["ev1"]] if pm.get("ev1") else []))
            for a in range(NK):
                tm3 = S.op("tensor", lambda e, a=a, w=w3b, xtc=xtc: e.matmul(
                    ps_3[:], w[:, a * 128:(a + 1) * 128],
                    xtc[:, a * 512:(a + 1) * 512],
                    start=(a == 0), stop=(a == NK - 1)),
                    deps=[t_w3] + t_x + ([pm["ev3"]] if pm.get("ev3") else []))
            pm[f"mm1{ht % 2}"] = tm3
            t_si = S.op("scalar", lambda e: e.activation(
                out=sil[:], in_=ps_1[:], func=AF.Silu),
                deps=[tm1] + ([pm["sil"]] if pm.get("sil") else []))
            pm["ev1"] = t_si
            t_h = S.op("vector", lambda e, ht=ht, hbuf=hbuf: e.tensor_tensor(
                out=hbuf[:, ht * 512:(ht + 1) * 512], in0=sil[:],
                in1=ps_3[:], op=ALU.mult), deps=[t_si, tm3]
                + ([pm[f"hb{tc % 2}"]] if pm.get(f"hb{tc % 2}") else []))
            pm["ev3"] = t_h
            pm["sil"] = t_h
            pm["hlast"] = t_h
        for dc in range(D // 512):
            dsl = slice(dc * 512, (dc + 1) * 512)
            tmm = None
            for ht in range(NHT):
                wb2 = w2t_a if ht % 2 == 0 else w2t_b
                t_w2 = S.dma("sync", wb2[:],
                             w2d[ht * 128:(ht + 1) * 128, dsl],
                             deps=[pm[f"mm2{ht % 2}"]]
                             if pm.get(f"mm2{ht % 2}") else tok_deq,
                             ch=4 + ht % 2)
                for ts4 in range(4):
                    tmm = S.op("tensor", lambda e, ht=ht, ts4=ts4, w=wb2,
                               hbuf=hbuf:
                               e.matmul(ps_t[ts4][:],
                                        hbuf[:, ht * 512 + ts4 * 128:
                                             ht * 512 + (ts4 + 1) * 128],
                                        w[:], start=(ht == 0),
                                        stop=(ht == NHT - 1)),
                               deps=[t_w2, pm["hlast"]]
                               + ([pm[f"pt{ts4}"]] if pm.get(f"pt{ts4}")
                                  else []))
                pm[f"mm2{ht % 2}"] = tmm
            for ts4 in range(4):
                hb = hpt_a if ts4 % 2 == 0 else hpt_b
                t_ev = S.op("vector", lambda e, ts4=ts4, hb=hb: e.tensor_copy(
                    out=hb[:], in_=ps_t[ts4][:]), deps=[tmm]
                    + ([pm[f"hpt{ts4 % 2}"]] if pm.get(f"hpt{ts4 % 2}")
                       else []))
                t_sp = S.dma("gpsimd",
                             pb[tc * 512 + ts4 * 128:
                                tc * 512 + (ts4 + 1) * 128, dsl],
                             hb[:], deps=[t_ev], ch=1 + 4 * (ts4 % 2))
                pm[f"pt{ts4}"] = t_ev
                pm[f"hpt{ts4 % 2}"] = t_sp
                pm[f"pbl{ts4 % 2}"] = t_sp
                t_pb_last = t_sp
        pm[f"xr{tc % 2}"] = tmm
        pm[f"hb{tc % 2}"] = tmm

    t_rs = S.cc("ReduceScatter", ALU.add, [pb.opt()], [rs_out.opt()],
                deps=[pm["pbl0"], pm["pbl1"]])

    pvf = {}
    for tt in range(To // 128):
        t_ev = None
        for a in range(NK):
            t_lx = S.dma("sync", x3blk[:],
                         x3T[a * 128:(a + 1) * 128, tt * 128:(tt + 1) * 128],
                         deps=[pvf["tp"]] if pvf.get("tp") else [], ch=0)
            t_tp = S.op("tensor", lambda e: e.transpose(
                ps_f[:], x3blk[:], idn_f[:]),
                deps=[t_lx, t_idn] + ([pvf["tev"]] if pvf.get("tev") else []))
            pvf["tp"] = t_tp
            t_ev = S.op("vector", lambda e, a=a: e.tensor_copy(
                out=otile[:, a * 128:(a + 1) * 128], in_=ps_f[:]),
                deps=[t_tp] + ([pvf["ost"]] if a == 0 and pvf.get("ost")
                               else []))
            pvf["tev"] = t_ev
        t_lr = S.dma("sync", rst[:],
                     rs_out.rearrange("(b p) d -> b p d", p=128)[tt],
                     deps=[t_rs] + ([pvf["rs"]] if pvf.get("rs") else []),
                     ch=2)
        t_ad = S.op("vector", lambda e: e.tensor_tensor(
            out=otile[:], in0=otile[:], in1=rst[:], op=ALU.add),
            deps=[t_ev, t_lr])
        pvf["rs"] = t_ad
        t_st = S.dma("gpsimd",
                     x4.rearrange("(b p) d -> b p d", p=128)[tt],
                     otile[:], deps=[t_ad], ch=3)
        pvf["ost"] = t_st
    S.op("gpsimd", lambda e: e.engine_nop(), deps=[pvf["ost"]], inc=1)

    return {'x4': pvf['ost']}


def build_ffn1():
    nc = bass.Bass("TRN2", target_bir_lowering=False)
    To = OWN_T
    NK = D // 128
    NHT = HSL // 128
    NTC = T // 512

    xg = nc.dram_tensor("xg", [D, T], BF16, kind="ExternalInput").ap()
    w1c = nc.dram_tensor("w1c", [D, HSL], I32, kind="ExternalInput").ap()
    a1S = nc.dram_tensor("a1S", [64, HSL], F32, kind="ExternalInput").ap()
    w3c = nc.dram_tensor("w3c", [D, HSL], I32, kind="ExternalInput").ap()
    a3S = nc.dram_tensor("a3S", [64, HSL], F32, kind="ExternalInput").ap()
    w2c = nc.dram_tensor("w2c", [HSL, D], I32, kind="ExternalInput").ap()
    a2S = nc.dram_tensor("a2S", [22, D], F32, kind="ExternalInput").ap()
    x3T = nc.dram_tensor("x3T", [D, To], F32, kind="ExternalInput").ap()

    x4 = nc.dram_tensor("x4", [To, D], F32, kind="ExternalOutput").ap()

    w1d = nc.dram_tensor("w1d", [128, NHT * NK * 128], BF16, kind="Internal").ap()
    w3d = nc.dram_tensor("w3d", [128, NHT * NK * 128], BF16, kind="Internal").ap()
    w2d = nc.dram_tensor("w2d", [HSL, D], BF16, kind="Internal").ap()
    pb = nc.dram_tensor("pb", [T, D], F32, kind="Internal").ap()
    rs_out = nc.dram_tensor("rs_out", [To, D], F32, kind="Internal").ap()

    with contextlib.ExitStack() as stk:
        S = Sched(nc, stk)
        IO = dict(xg=xg, w1c=w1c, a1S=a1S, w3c=w3c, a3S=a3S, w2c=w2c, a2S=a2S,
                  x3T=x3T, x4=x4, w1d=w1d, w3d=w3d, w2d=w2d, pb=pb,
                  rs_out=rs_out)
        with contextlib.ExitStack() as pstk:
            phase_ffn1(nc, S, pstk, IO)
        S.emit(block=stk.enter_context(nc.Block()))
    return nc


def build_fused():
    nc = bass.Bass("TRN2", target_bir_lowering=False)
    To = OWN_T
    TS = TS_M

    inp = lambda n, shp, dt: nc.dram_tensor(n, shp, dt, kind="ExternalInput").ap()
    itn = lambda n, shp, dt: nc.dram_tensor(n, shp, dt, kind="Internal").ap()

    x0d = inp("x0d", [T, DSL], F32)
    x0dT = inp("x0dT", [DSL, T], F32)
    alog = inp("alog", [128, NDT * NST], F32)
    dvec = inp("dvec", [128, NDT], F32)
    xpwS = inp("xpwS", [128, NDT * 160], F32)
    dtwT = inp("dtwT", [DTR, DSL], F32)
    dtb = inp("dtb", [128, NDT], F32)
    wqT = inp("wqT", [D, D], I32)
    scaleT = inp("scaleT", [32, D], F32)
    zeroT = inp("zeroT", [32, D], F32)
    bias0 = inp("bias0", [1, D], F32)
    wkvT = inp("wkvT", [D, 3 * PART], F32)
    bkv_c = inp("bkv_c", [128, 48], F32)
    x1idx = inp("x1idx", [128, D // 128], I32)
    woT = inp("woT", [PART, D], F32)
    bo_c = inp("bo_c", [128, 32], F32)
    qpos = inp("qpos", [128, To // 128], F32)
    w1c = inp("w1c", [D, HSL], I32)
    a1S = inp("a1S", [64, HSL], F32)
    w3c = inp("w3c", [D, HSL], I32)
    a3S = inp("a3S", [64, HSL], F32)
    w2c = inp("w2c", [HSL, D], I32)
    a2S = inp("a2S", [22, D], F32)

    x4 = nc.dram_tensor("x4", [To, D], F32, kind="ExternalOutput").ap()

    st_in = itn("st_in", [3, T], F32)
    st_out = itn("st_out", [3, T], F32)
    st_scr = itn("st_scr", [2, T], F32)
    xp_ins = [itn(f"xp_in{i}", [160, TS], F32) for i in range(T // TS)]
    xp_outs = [itn(f"xp_out{i}", [160, TS], F32) for i in range(T // TS)]
    x1d = itn("x1d", [DSL, T], F32)
    x1g = itn("x1g", [D, T], F32)
    wdeq = itn("wdeq", [128, 32 * 32 * 128], BF16)
    wkvb = itn("wkvb", [128, 48 * 32 * 128], BF16)
    b0scr = itn("b0scr", [1, D], BF16)
    x2Ti = itn("x2Ti", [D, To], F32)
    qdi = itn("qdi", [PART, To], BF16)
    kdi = itn("kdi", [PART, To], BF16)
    vti = itn("vti", [To, PART], BF16)
    kga = itn("kga", [N_CORES, PART, To], BF16)
    vga = itn("vga", [N_CORES, To, PART], BF16)
    x3scr = itn("x3scr", [D, To], F32)
    xn3i = itn("xn3i", [D, To], BF16)
    xga = itn("xga", [N_CORES, D, To], BF16)
    w1d = itn("w1d", [128, (HSL // 128) * 32 * 128], BF16)
    w3d = itn("w3d", [128, (HSL // 128) * 32 * 128], BF16)
    w2d = itn("w2d", [HSL, D], BF16)
    pb = itn("pb", [T, D], F32)
    rs_out = itn("rs_out", [To, D], F32)

    with contextlib.ExitStack() as stk:
        S = Sched(nc, stk)

        IO1 = dict(pfx="pA_", x0d=x0d, x0dT=x0dT, alog=alog, dvec=dvec, xpwS=xpwS,
                   dtwT=dtwT, dtb=dtb, x1d=x1d, st_in=st_in, st_out=st_out,
                   st_scr=st_scr, xp_ins=xp_ins, xp_outs=xp_outs)
        with contextlib.ExitStack() as p1:
            r1 = phase_mamba(nc, S, p1, IO1)
        t_ag1 = S.cc("AllGather", ALU.bypass, [x1d.opt()], [x1g.opt()],
                     deps=r1["x1d_stores"])
        S.barrier()

        IO2 = dict(pfx="pB_", x1T=None, wqT=wqT, scaleT=scaleT, zeroT=zeroT, bias0=bias0,
                   wkvT=wkvT, bkv_c=bkv_c, x2T=x2Ti, qd=qdi, kd=kdi, vt=vti,
                   wdeq=wdeq, wkvb=wkvb, b0scr=b0scr,
                   x1idx=x1idx, x1v=x1g.rearrange("d (b t) -> (d b) t",
                                                  b=N_CORES),
                   tok_pre=[t_ag1])
        with contextlib.ExitStack() as p2:
            r2 = phase_ffn0(nc, S, p2, IO2)
        t_ag2 = S.cc("AllGather", ALU.bypass, [kdi.opt()], [kga.opt()],
                     deps=r2["kv"] + [r2["x2"]])
        t_ag3 = S.cc("AllGather", ALU.bypass, [vti.opt()], [vga.opt()],
                     deps=r2["kv"])
        S.barrier()

        IO3 = dict(pfx="pC_", qd=qdi, kg=None, vg=None, woT=woT, bo_c=bo_c, x2T=x2Ti,
                   qpos=qpos, x3T=None, xn3T=xn3i, x3scr=x3scr,
                   kga=kga, vga=vga, tok_pre=[t_ag2, t_ag3])
        with contextlib.ExitStack() as p3:
            r3 = phase_attn(nc, S, p3, IO3)
        t_ag4 = S.cc("AllGather", ALU.bypass, [xn3i.opt()], [xga.opt()],
                     deps=[r3["xn3"]])
        S.barrier()

        IO4 = dict(pfx="pD_", xg=None, w1c=w1c, a1S=a1S, w3c=w3c, a3S=a3S,
                   w2c=w2c, a2S=a2S, x3T=x3scr, x4=x4, w1d=w1d, w3d=w3d,
                   w2d=w2d, pb=pb, rs_out=rs_out, xga=xga, tok_pre=[t_ag4])
        with contextlib.ExitStack() as p4:
            r4 = phase_ffn1(nc, S, p4, IO4)

        S.emit(block=stk.enter_context(nc.Block()))
    return nc


_NCS = {}


def _nc(name):
    if name not in _NCS:
        _NCS[name] = {"mamba": build_mamba, "ffn0": build_ffn0_qkv,
                      "attn": build_attn, "ffn1": build_ffn1,
                      "fused": build_fused}[name]()
    return _NCS[name]


def _run(name, in_maps):
    res = run_bass_kernel_spmd(_nc(name), in_maps, core_ids=list(range(N_CORES)))
    return res.results


def run_mamba(L0_attn_x, L0_attn_A_log, L0_attn_D, L0_attn_x_proj_w,
              L0_attn_dt_proj_w, L0_attn_dt_proj_b):
    x = np.ascontiguousarray(np.asarray(L0_attn_x).reshape(T, D))
    xT = np.ascontiguousarray(x.T)
    xpwT = np.ascontiguousarray(np.asarray(L0_attn_x_proj_w).T)
    dtwT = np.ascontiguousarray(np.asarray(L0_attn_dt_proj_w).T)
    A_log = np.asarray(L0_attn_A_log)
    Dv = np.asarray(L0_attn_D)
    dtb = np.asarray(L0_attn_dt_proj_b)
    in_maps = []
    for c in range(N_CORES):
        dsl = slice(DSL * c, DSL * (c + 1))
        in_maps.append({
            "x0d": np.ascontiguousarray(x[:, dsl]),
            "x0dT": np.ascontiguousarray(xT[dsl, :]),
            "alog": np.ascontiguousarray(
                A_log[dsl, :].reshape(NDT, 128, NST)
                .transpose(1, 0, 2).reshape(128, NDT * NST)),
            "dvec": np.ascontiguousarray(Dv[dsl].reshape(NDT, 128).T),
            "xpwS": np.ascontiguousarray(
                xpwT[dsl, :].reshape(NDT, 128, 160)
                .transpose(1, 0, 2).reshape(128, NDT * 160)),
            "dtwT": np.ascontiguousarray(dtwT[:, dsl]),
            "dtb": np.ascontiguousarray(dtb[dsl].reshape(NDT, 128).T),
        })
    res = _run("mamba", in_maps)
    return np.concatenate([r["x1d"] for r in res], axis=0)


def kernel_multi(**inputs):
    f32 = lambda a: np.ascontiguousarray(np.asarray(a, dtype=np.float32))

    x1 = run_mamba(inputs["L0_attn_x"], inputs["L0_attn_A_log"],
                   inputs["L0_attn_D"], inputs["L0_attn_x_proj_w"],
                   inputs["L0_attn_dt_proj_w"], inputs["L0_attn_dt_proj_b"])

    wqT = np.ascontiguousarray(np.asarray(inputs["L0_ffn_W_q"]).T)
    scaleT = f32(np.asarray(inputs["L0_ffn_scale"]).T)
    zeroT = f32(np.asarray(inputs["L0_ffn_zero"]).T)
    bias0 = f32(inputs["L0_ffn_bias"]).reshape(1, D)
    wkvT = np.ascontiguousarray(f32(inputs["L1_attn_w_qkv"]).T)
    bkv_c = np.ascontiguousarray(f32(inputs["L1_attn_b_qkv"]).reshape(48, 128).T)
    in_maps = []
    for c in range(N_CORES):
        tsl = slice(OWN_T * c, OWN_T * (c + 1))
        in_maps.append({
            "x1T": np.ascontiguousarray(x1[:, tsl]),
            "wqT": wqT, "scaleT": scaleT, "zeroT": zeroT, "bias0": bias0,
            "wkvT": wkvT, "bkv_c": bkv_c,
        })
    resB = _run("ffn0", in_maps)
    x2s = [r["x2T"] for r in resB]
    kg = np.concatenate([r["kd"] for r in resB], axis=1)
    vg = np.concatenate([r["vt"] for r in resB], axis=0)

    woT = np.ascontiguousarray(f32(inputs["L1_attn_w_o"]).T)
    bo_c = np.ascontiguousarray(f32(inputs["L1_attn_b_o"]).reshape(32, 128).T)
    in_maps = []
    for c in range(N_CORES):
        qpos = (c * OWN_T + np.arange(OWN_T).reshape(OWN_T // 128, 128).T
                ).astype(np.float32)
        in_maps.append({
            "qd": resB[c]["qd"], "kg": kg, "vg": vg, "woT": woT, "bo_c": bo_c,
            "x2T": x2s[c], "qpos": np.ascontiguousarray(qpos),
        })
    resC = _run("attn", in_maps)
    xg = np.concatenate([r["xn3T"] for r in resC], axis=1)

    HP = N_CORES * HSL
    w1T = np.asarray(inputs["L1_ffn_w1_codes"]).T
    w3T = np.asarray(inputs["L1_ffn_w3_codes"]).T
    w2T = np.asarray(inputs["L1_ffn_w2_codes"]).T
    w1Tp = np.full((D, HP), 7, np.int32); w1Tp[:, :HID] = w1T
    w3Tp = np.full((D, HP), 7, np.int32); w3Tp[:, :HID] = w3T
    w2Tp = np.full((HP, D), 7, np.int32); w2Tp[:HID, :] = w2T
    a1Tp = np.zeros((64, HP), np.float32)
    a1Tp[:, :HID] = np.asarray(inputs["L1_ffn_w1_absmax"]).T
    a3Tp = np.zeros((64, HP), np.float32)
    a3Tp[:, :HID] = np.asarray(inputs["L1_ffn_w3_absmax"]).T
    a2p = np.zeros((HP // 64, D), np.float32)
    a2p[:HID // 64, :] = np.asarray(inputs["L1_ffn_w2_absmax"]).T
    in_maps = []
    for c in range(N_CORES):
        hs = slice(c * HSL, (c + 1) * HSL)
        in_maps.append({
            "xg": xg,
            "w1c": np.ascontiguousarray(w1Tp[:, hs]),
            "a1S": np.ascontiguousarray(a1Tp[:, hs]),
            "w3c": np.ascontiguousarray(w3Tp[:, hs]),
            "a3S": np.ascontiguousarray(a3Tp[:, hs]),
            "w2c": np.ascontiguousarray(w2Tp[hs, :]),
            "a2S": np.ascontiguousarray(a2p[c * 22:(c + 1) * 22, :]),
            "x3T": resC[c]["x3T"],
        })
    resD = _run("ffn1", in_maps)
    x4 = np.concatenate([r["x4"] for r in resD], axis=0)
    return x4.reshape(1, T, D).astype(np.float32)


def _fused_in_maps(inputs):
    f32 = lambda a: np.ascontiguousarray(np.asarray(a, dtype=np.float32))
    x = np.ascontiguousarray(np.asarray(inputs["L0_attn_x"]).reshape(T, D))
    xT = np.ascontiguousarray(x.T)
    xpwT = np.ascontiguousarray(np.asarray(inputs["L0_attn_x_proj_w"]).T)
    dtwT = np.ascontiguousarray(np.asarray(inputs["L0_attn_dt_proj_w"]).T)
    A_log = np.asarray(inputs["L0_attn_A_log"])
    Dv = np.asarray(inputs["L0_attn_D"])
    dtb = np.asarray(inputs["L0_attn_dt_proj_b"])
    wqT = np.ascontiguousarray(np.asarray(inputs["L0_ffn_W_q"]).T)
    scaleT = f32(np.asarray(inputs["L0_ffn_scale"]).T)
    zeroT = f32(np.asarray(inputs["L0_ffn_zero"]).T)
    bias0 = f32(inputs["L0_ffn_bias"]).reshape(1, D)
    wkvT = np.ascontiguousarray(f32(inputs["L1_attn_w_qkv"]).T)
    bkv_c = np.ascontiguousarray(f32(inputs["L1_attn_b_qkv"]).reshape(48, 128).T)
    woT = np.ascontiguousarray(f32(inputs["L1_attn_w_o"]).T)
    bo_c = np.ascontiguousarray(f32(inputs["L1_attn_b_o"]).reshape(32, 128).T)
    HP = N_CORES * HSL
    w1T = np.asarray(inputs["L1_ffn_w1_codes"]).T
    w3T = np.asarray(inputs["L1_ffn_w3_codes"]).T
    w2T = np.asarray(inputs["L1_ffn_w2_codes"]).T
    w1Tp = np.full((D, HP), 7, np.int32); w1Tp[:, :HID] = w1T
    w3Tp = np.full((D, HP), 7, np.int32); w3Tp[:, :HID] = w3T
    w2Tp = np.full((HP, D), 7, np.int32); w2Tp[:HID, :] = w2T
    a1Tp = np.zeros((64, HP), np.float32)
    a1Tp[:, :HID] = np.asarray(inputs["L1_ffn_w1_absmax"]).T
    a3Tp = np.zeros((64, HP), np.float32)
    a3Tp[:, :HID] = np.asarray(inputs["L1_ffn_w3_absmax"]).T
    a2p = np.zeros((HP // 64, D), np.float32)
    a2p[:HID // 64, :] = np.asarray(inputs["L1_ffn_w2_absmax"]).T

    in_maps = []
    for c in range(N_CORES):
        dsl = slice(DSL * c, DSL * (c + 1))
        hs = slice(c * HSL, (c + 1) * HSL)
        p = np.arange(128)
        g = np.arange(D // 128)
        x1idx = ((g[None, :] * 128 + p[:, None]) * N_CORES + c).astype(np.int32)
        qpos = (c * OWN_T + np.arange(OWN_T).reshape(OWN_T // 128, 128).T
                ).astype(np.float32)
        in_maps.append({
            "x0d": np.ascontiguousarray(x[:, dsl]),
            "x0dT": np.ascontiguousarray(xT[dsl, :]),
            "alog": np.ascontiguousarray(
                A_log[dsl, :].reshape(NDT, 128, NST)
                .transpose(1, 0, 2).reshape(128, NDT * NST)),
            "dvec": np.ascontiguousarray(Dv[dsl].reshape(NDT, 128).T),
            "xpwS": np.ascontiguousarray(
                xpwT[dsl, :].reshape(NDT, 128, 160)
                .transpose(1, 0, 2).reshape(128, NDT * 160)),
            "dtwT": np.ascontiguousarray(dtwT[:, dsl]),
            "dtb": np.ascontiguousarray(dtb[dsl].reshape(NDT, 128).T),
            "wqT": wqT, "scaleT": scaleT, "zeroT": zeroT, "bias0": bias0,
            "wkvT": wkvT, "bkv_c": bkv_c,
            "x1idx": np.ascontiguousarray(x1idx),
            "woT": woT, "bo_c": bo_c, "qpos": np.ascontiguousarray(qpos),
            "w1c": np.ascontiguousarray(w1Tp[:, hs]),
            "a1S": np.ascontiguousarray(a1Tp[:, hs]),
            "w3c": np.ascontiguousarray(w3Tp[:, hs]),
            "a3S": np.ascontiguousarray(a3Tp[:, hs]),
            "w2c": np.ascontiguousarray(w2Tp[hs, :]),
            "a2S": np.ascontiguousarray(a2p[c * 22:(c + 1) * 22, :]),
        })
    return in_maps


def kernel(**inputs):
    in_maps = _fused_in_maps(inputs)
    res = _run("fused", in_maps)
    x4 = np.concatenate([r["x4"] for r in res.results], axis=0) \
        if hasattr(res, "results") else np.concatenate(
            [r["x4"] for r in res], axis=0)
    return x4.reshape(1, T, D).astype(np.float32)
